# version 20
# baseline (speedup 1.0000x reference)
import base64
import contextlib
import numpy as np
import ml_dtypes

import concourse.bass as bass
import concourse.mybir as mybir
from concourse.bass_utils import run_bass_kernel_spmd

F32 = mybir.dt.float32
BF16 = mybir.dt.bfloat16
AF = mybir.ActivationFunctionType
ALU = mybir.AluOpType
AX = mybir.AxisListType

NCORE = 8
TCORE = 4096
TPASS = 1024
NPASS = TCORE // TPASS
NTB = 2
NE = 8
NM = 16
NKC = 8
NF = 8
NGRP = 4

_FRE_B64 = "JuusPolzrT7l+60+N4SuPn4Mrz6+lK8+9BywPiKlsD5ELbE+W7WxPmo9sj5sxbI+ZU2zPlLVsz4yXbQ+COW0PtFstT6Q9LU+Qny2PuUDtz59i7c+BhO4PoWauD71Ibk+Vam5Pqgwuj7tt7o+JD+7PkzGuz5iTbw+bdS8PmVbvT5R4r0+Kmm+PvPvvj6sdr8+U/2/PuyDwD5xCsE+5JDBPkgXwj6XncI+1yPDPgKqwz4aMMQ+IbbEPhM8xT70wcU+wEfGPnbNxj4cU8c+q9jHPiZeyD6L48g+3GjJPhfuyT49c8o+TvjKPkl9yz4rAsw++YbMPq8LzT5QkM0+2RTOPkiZzj6iHc8+46HPPg0m0D4fqtA+FS7RPvWx0T65NdI+aLnSPvs80z5zwNM+00PUPhjH1D5EStU+U83VPkdQ1j4h09Y+31XXPoLY1z4JW9g+ct3YPsJf2T7z4dk+CGTaPv/l2j7ZZ9s+lunbPjVr3D637Nw+G27dPl7v3T6FcN4+i/HePnRy3z49898+5HPgPm704D7WdOE+IfXhPkp14j5P9eI+N3XjPvv04z6hdOQ+JPTkPoNz5T7C8uU+3nHmPtrw5j6xb+c+Y+7nPvVs6D5h6+g+rWnpPtLn6T7UZeo+suPqPmth6z4A3+s+cFzsPrnZ7D7eVu0+3NPtPrhQ7j5rze4+9knvPl3G7z6cQvA+tr7wPqk68T5xtvE+FTLyPo+t8j7kKPM+DaTzPg4f9D7pmfQ+mBT1PiGP9T5/CfY+soP2Pr399j6dd/c+VfH3PuBq+D5B5Pg+eV35PoPW+T5kT/o+Gsj6PqFA+z7/uPs+MDH8PjWp/D4PIf0+uJj9PjgQ/j6Jh/4+rv7+PqV1/z5q7P8+gzEAP7lsAD/ZpwA/4eIAP9AdAT+pWAE/apMBPxTOAT+lCAI/HkMCP4B9Aj/JtwI/+vECPxQsAz8UZgM//J8DP8rZAz+CEwQ/H00EP6OGBD8OwAQ/YfkEP5syBT+6awU/wKQFP63dBT+AFgY/Ok8GP9mHBj9ewAY/yvgGPxsxBz9SaQc/b6EHP3HZBz9YEQg/JUkIP9mACD9wuAg/6+8IP00nCT+TXgk/v5UJP87MCT/CAwo/mzoKP1hxCj/5pwo/ft4KP+gUCz82Sws/Z4ELP323Cz927Qs/UiMMPxNZDD+2jgw/PsQMP6j5DD/0Lg0/JGQNPziZDT8vzg0/CAMOP8I3Dj9hbA4/4aAOP0XVDj+KCQ8/sT0PP7txDz+mpQ8/ddkPPyQNED+0QBA/KHQQP3ynED+y2hA/yQ0RP8JAET+ccxE/VqYRP/PYET9wCxI/zT0SPwxwEj8qohI/K9QSPwsGEz/LNxM/bWkTP+6aEz9PzBM/kf0TP7IuFD+0XxQ/lJAUP1XBFD/28RQ/dSIVP9VSFT8UgxU/MrMVPy/jFT8LExY/x0IWP2FyFj/aoRY/M9EWP2kAFz9/Lxc/cl4XP0SNFz/2uxc/g+oXP/EYGD88Rxg/ZnUYP22jGD9R0Rg/Ff8YP7QsGT8zWhk/j4cZP8i0GT/e4Rk/0g4aP6M7Gj9RaBo/3JQaP0XBGj+J7Ro/rBkbP6pFGz+GcRs/Pp0bP9HIGz9E9Bs/kB8cP7lKHD/AdRw/oKAcP1/LHD/59Rw/biAdP8BKHT/sdB0/9p4dP9rIHT+a8h0/NhweP6xFHj//bh4/LJgePzXBHj8Y6h4/1hIfP3E7Hz/lYx8/M4wfP120Hz9h3B8/QQQgP/orID+OUyA/+3ogP0SiID9oySA/ZPAgPzsXIT/sPSE/dmQhP9yKIT8asSE/MdchPyT9IT/uIiI/lEgiPxJuIj9pkyI/mrgiP6TdIj+IAiM/RCcjP9pLIz9IcCM/j5QjP7C4Iz+p3CM/egAkPyQkJD+nRyQ/A2skPzaOJD9CsSQ/JtQkP+T2JD94GSU/5TslPypeJT9HgCU/O6IlPwnEJT+u5SU/KgcmP30oJj+pSSY/rWomP4eLJj84rCY/w8wmPyPtJj9cDSc/ay0nP1FNJz8PbSc/oownPw+sJz9Ryyc/aeonP1oJKD8gKCg/vUYoPzJlKD97gyg/nKEoP5O/KD9i3Sg/BfsoP4AYKT/RNSk/9lIpP/RvKT/HjCk/b6kpP+7FKT9C4ik/bf4pP20aKj9DNio/7lEqP29tKj/GiCo/86MqP/S+Kj/M2So/d/QqP/kOKz9QKSs/fEMrP31dKz9Udys/AJErP3+qKz/Vwys/AN0rP/31Kz/SDiw/eicsP/g/LD9JWCw/cHAsP2uILD87oCw/3rcsP1jPLD+k5iw/xf0sP7sULT+DKy0/IUItP5NYLT/abi0/84QtP+GaLT+jsC0/OMYtP6PbLT/h8C0/8QUuP9caLj+PLy4/G0QuP3xYLj+ubC4/toAuP5CULj8+qC4/wLsuPxTPLj894i4/N/UuPwYILz+oGi8/HC0vP2U/Lz9+US8/bWMvPy51Lz/Chi8/KZgvP2KpLz9wui8/T8svPwHcLz+G7C8/3PwvPwgNMD8EHTA/1CwwP3Y8MD/pSzA/MlswP0tqMD82eTA/9YcwP4WWMD/opDA/HrMwPyTBMD/+zjA/qtwwPyjqMD949zA/mQQxP44RMT9UHjE/7CoxP1Y3MT+RQzE/oE8xP39bMT8xZzE/tHIxPwl+MT8wiTE/KJQxP/OeMT+PqTE//LMxPzu+MT9LyDE/LtIxP+HbMT9m5TE/vu4xP+T3MT/eADI/5PcxP77uMT9m5TE/4dsxPy7SMT9LyDE/O74xP/yzMT+PqTE/854xPyiUMT8wiTE/CX4xP7RyMT8xZzE/f1sxP6BPMT+RQzE/VjcxP+wqMT9UHjE/jhExP5kEMT949zA/KOowP6rcMD/+zjA/JMEwPx6zMD/opDA/hZYwP/WHMD82eTA/S2owPzJbMD/pSzA/djwwP9QsMD8EHTA/CA0wP9z8Lz+G7C8/AdwvP0/LLz9wui8/YqkvPymYLz/Chi8/LnUvP21jLz9+US8/ZT8vPxwtLz+oGi8/BggvPzf1Lj894i4/FM8uP8C7Lj8+qC4/kJQuP7aALj+ubC4/fFguPxtELj+PLy4/1xouP/EFLj/h8C0/o9stPzjGLT+jsC0/4ZotP/OELT/abi0/k1gtPyFCLT+DKy0/uxQtP8X9LD+k5iw/WM8sP963LD87oCw/a4gsP3BwLD9JWCw/+D8sP3onLD/SDiw//fUrPwDdKz/Vwys/f6orPwCRKz9Udys/fV0rP3xDKz9QKSs/+Q4rP3f0Kj/M2So/9L4qP/OjKj/GiCo/b20qP+5RKj9DNio/bRoqP23+KT9C4ik/7sUpP2+pKT/HjCk/9G8pP/ZSKT/RNSk/gBgpPwX7KD9i3Sg/k78oP5yhKD97gyg/MmUoP71GKD8gKCg/WgkoP2nqJz9Ryyc/D6wnP6KMJz8PbSc/UU0nP2stJz9cDSc/I+0mP8PMJj84rCY/h4smP61qJj+pSSY/fSgmPyoHJj+u5SU/CcQlPzuiJT9HgCU/Kl4lP+U7JT94GSU/5PYkPybUJD9CsSQ/No4kPwNrJD+nRyQ/JCQkP3oAJD+p3CM/sLgjP4+UIz9IcCM/2ksjP0QnIz+IAiM/pN0iP5q4Ij9pkyI/Em4iP5RIIj/uIiI/JP0hPzHXIT8asSE/3IohP3ZkIT/sPSE/OxchP2TwID9oySA/RKIgP/t6ID+OUyA/+isgP0EEID9h3B8/XbQfPzOMHz/lYx8/cTsfP9YSHz8Y6h4/NcEePyyYHj//bh4/rEUePzYcHj+a8h0/2sgdP/aeHT/sdB0/wEodP24gHT/59Rw/X8scP6CgHD/AdRw/uUocP5AfHD9E9Bs/0cgbPz6dGz+GcRs/qkUbP6wZGz+J7Ro/RcEaP9yUGj9RaBo/ozsaP9IOGj/e4Rk/yLQZP4+HGT8zWhk/tCwZPxX/GD9R0Rg/baMYP2Z1GD88Rxg/8RgYP4PqFz/2uxc/RI0XP3JeFz9/Lxc/aQAXPzPRFj/aoRY/YXIWP8dCFj8LExY/L+MVPzKzFT8UgxU/1VIVP3UiFT/28RQ/VcEUP5SQFD+0XxQ/si4UP5H9Ez9PzBM/7poTP21pEz/LNxM/CwYTPyvUEj8qohI/DHASP809Ej9wCxI/89gRP1amET+ccxE/wkARP8kNET+y2hA/fKcQPyh0ED+0QBA/JA0QP3XZDz+mpQ8/u3EPP7E9Dz+KCQ8/RdUOP+GgDj9hbA4/wjcOPwgDDj8vzg0/OJkNPyRkDT/0Lg0/qPkMPz7EDD+2jgw/E1kMP1IjDD927Qs/fbcLP2eBCz82Sws/6BQLP37eCj/5pwo/WHEKP5s6Cj/CAwo/zswJP7+VCT+TXgk/TScJP+vvCD9wuAg/2YAIPyVJCD9YEQg/cdkHP2+hBz9SaQc/GzEHP8r4Bj9ewAY/2YcGPzpPBj+AFgY/rd0FP8CkBT+6awU/mzIFP2H5BD8OwAQ/o4YEPx9NBD+CEwQ/ytkDP/yfAz8UZgM/FCwDP/rxAj/JtwI/gH0CPx5DAj+lCAI/FM4BP2qTAT+pWAE/0B0BP+HiAD/ZpwA/uWwAP4MxAD9q7P8+pXX/Pq7+/j6Jh/4+OBD+PriY/T4PIf0+Nan8PjAx/D7/uPs+oUD7PhrI+j5kT/o+g9b5Pnld+T5B5Pg+4Gr4PlXx9z6dd/c+vf32PrKD9j5/CfY+IY/1PpgU9T7pmfQ+Dh/0Pg2k8z7kKPM+j63yPhUy8j5xtvE+qTrxPra+8D6cQvA+XcbvPvZJ7z5rze4+uFDuPtzT7T7eVu0+udnsPnBc7D4A3+s+a2HrPrLj6j7UZeo+0ufpPq1p6T5h6+g+9WzoPmPu5z6xb+c+2vDmPt5x5j7C8uU+g3PlPiT05D6hdOQ++/TjPjd14z5P9eI+SnXiPiH14T7WdOE+bvTgPuRz4D49898+dHLfPovx3j6FcN4+Xu/dPhtu3T637Nw+NWvcPpbp2z7ZZ9s+/+XaPghk2j7z4dk+wl/ZPnLd2D4JW9g+gtjXPt9V1z4h09Y+R1DWPlPN1T5EStU+GMfUPtND1D5zwNM++zzTPmi50j65NdI+9bHRPhUu0T4fqtA+DSbQPuOhzz6iHc8+SJnOPtkUzj5QkM0+rwvNPvmGzD4rAsw+SX3LPk74yj49c8o+F+7JPtxoyT6L48g+Jl7IPqvYxz4cU8c+ds3GPsBHxj70wcU+EzzFPiG2xD4aMMQ+AqrDPtcjwz6XncI+SBfCPuSQwT5xCsE+7IPAPlP9vz6sdr8+8+++Pippvj5R4r0+ZVu9Pm3UvD5iTbw+TMa7PiQ/uz7tt7o+qDC6PlWpuT71Ibk+hZq4PgYTuD59i7c+5QO3PkJ8tj6Q9LU+0Wy1PgjltD4yXbQ+UtWzPmVNsz5sxbI+aj2yPlu1sT5ELbE+IqWwPvQcsD6+lK8+fgyvPjeErj7l+60+iXOtPg=="
_FIM_B64 = "62JeP6FtXj8keF4/dYJeP5OMXj9/ll4/OaBeP8CpXj8Us14/OLxePyjFXj/nzV4/c9ZeP8zeXj/z5l4/6e5eP6z2Xj88/l4/mwVfP8cMXz/CE18/iRpfPx4hXz+DJ18/tC1fP7MzXz+COV8/HD9fP4ZEXz+9SV8/w05fP5ZTXz84WF8/qFxfP+VgXz/wZF8/y2hfP3JsXz/pb18/LHNfPz92Xz8geV8/zntfP0t+Xz+XgF8/sIJfP5iEXz9Ohl8/1IdfPyeJXz9Jil8/OYtfP/aLXz+DjF8/34xfPwiNXz8CjV8/yYxfP16MXz/Di18/9opfP/iJXz/HiF8/Z4dfP9WFXz8ShF8/HYJfP/h/Xz+gfV8/GXtfP2B4Xz92dV8/W3JfPw9vXz+Ta18/5GdfPwdkXz/3X18/tltfP0VXXz+jUl8/0U1fP81IXz+ZQ18/ND5fP6A4Xz/bMl8/4yxfP70mXz9mIF8/3hlfPycTXz8+DF8/JQVfP9z9Xj9k9l4/uu5eP9/mXj/W3l4/nNZePzLOXj+ZxV4/z7xeP9WzXj+rql4/UaFeP8iXXj8Ojl4/JYRePwx6Xj/Eb14/TGVeP6RaXj/NT14/xUReP485Xj8pLl4/lCJeP88WXj/cCl4/uf5dP2fyXT/l5V0/NNldP1XMXT9Gv10/CLJdP5ykXT//ll0/NYldPzt7XT8TbV0/u15dPzZQXT+CQV0/nzJdP40jXT9OFF0/3gRdP0L1XD935Vw/fdVcP1bFXD//tFw/fKRcP8iTXD/oglw/2XFcP5xgXD8zT1w/mj1cP9QrXD/gGVw/vgdcP2/1Wz/y4ls/R9BbP269Wz9qqls/NZdbP9aDWz9IcFs/jVxbP6RIWz+PNFs/TSBbP90LWz9A91o/duJaP4DNWj9duFo/DaNaP5CNWj/md1o/EGJaPw5MWj/eNVo/gx9aP/sIWj9H8lk/ZttZP1nEWT8frVk/upVZPyp+WT9sZlk/hE5ZP242WT8tHlk/wAVZPyjtWD9l1Fg/dbtYP1miWD8TiVg/oW9YPwRWWD87PFg/RyJYPykIWD/e7Vc/adNXP8m4Vz/+nVc/CYNXP+hnVz+cTFc/JjFXP4UVVz+7+VY/xN1WP6XBVj9apVY/5YhWP0ZsVj9+T1Y/izJWP20VVj8m+FU/tdpVPxq9VT9Vn1U/Z4FVP09jVT8PRVU/oyZVPw8IVT9Q6VQ/aspUP1mrVD8gjFQ/vmxUPzJNVD99LVQ/oA1UP5rtUz9rzVM/E61TP5OMUz/qa1M/GEtTPx8qUz/8CFM/sudSP0HGUj+mpFI/5IJSP/hgUj/mPlI/rBxSP0r6UT/A11E/DrVRPzWSUT81b1E/DUxRP74oUT9HBVE/quFQP+W9UD/6mVA/53VQP61RUD9MLVA/xQhQPxfkTz9Dv08/SJpPPyd1Tz/gT08/cipPP90ETz8i304/QrlOPzyTTj8QbU4/vUZOP0UgTj+n+U0/5dJNP/yrTT/uhE0/ul1NP2E2TT/kDk0/QOdMP3m/TD+Kl0w/em9MP0JHTD/mHkw/Z/ZLP8HNSz/3pEs/CnxLP/lSSz/CKUs/ZgBLP+jWSj9GrUo/f4NKP5RZSj+GL0o/VAVKP/7aST+FsEk/6YVJPyhbST9FMEk/PwVJPxXaSD/Jrkg/WINIP8ZXSD8RLEg/OQBIPz7URz8gqEc/4HtHP39PRz/7Ikc/VfZGP4vJRj+hnEY/lG9GP2ZCRj8WFUY/o+dFPw+6RT9ajEU/g15FP4swRT9zAkU/N9REP9ylRD9fd0Q/wkhEPwMaRD8k60M/JbxDPwWNQz/EXUM/YS5DP+D+Qj8+z0I/fZ9CP5pvQj+YP0I/dg9CPzXfQT/TrkE/Un5BP7FNQT/xHEE/EuxAPxK7QD/2iUA/uFhAP1wnQD/i9T8/SMQ/P5CSPz+5YD8/wi4/P6/8Pj99yj4/K5g+P7xlPj8wMz4/hAA+P7vNPT/Vmj0/0Wc9P640PT9vAT0/Ec48P5eaPD//Zjw/SjM8P3j/Oz+Jyzs/fZc7P1NjOz8OLzs/rPo6Py7GOj+SkTo/2lw6PwYoOj8Y8zk/C745P+OIOT+fUzk/Px45P8XoOD8tszg/en04P6xHOD/DETg/v9s3P56lNz9lbzc/Dzk3P50CNz8TzDY/bZU2P61eNj/RJzY/3PA1P8y5NT+igjU/Xks1PwAUNT+J3DQ/96Q0P0xtND+HNTQ/qP0zP7DFMz+gjTM/dVUzPzEdMz/U5DI/XqwyP9FzMj8pOzI/aQIyP5DJMT+gkDE/llcxP3UeMT885TA/6KswP39yMD/+ODA/ZP8vP7PFLz/qiy8/C1IvPxMYLz8E3i4/36MuP6FpLj9NLy4/4vQtP2C6LT/Ify0/GEUtP1IKLT93zyw/hZQsP3xZLD9dHiw/KOMrP96nKz9+bCs/CDErP3v1Kj/auSo/JX4qP1hCKj93Bio/gMopP3WOKT9VUik/IBYpP9bZKD93nSg/BGEoP34kKD/i5yc/MqsnP25uJz+VMSc/qvQmP6u3Jj+XeiY/cD0mPzYAJj/owiU/hoUlPxJIJT+KCiU/78wkP0KPJD+CUSQ/rxMkP8nVIz/RlyM/xlkjP6obIz963SI/OZ8iP+VgIj+BIiI/CeQhP4KlIT/nZiE/OyghP37pID+wqiA/0GsgP98sID/e7R8/zK4fP6hvHz90MB8/L/EeP9qxHj92ch4/ADMeP3nzHT/jsx0/efMdPwAzHj92ch4/2rEePy/xHj90MB8/qG8fP8yuHz/e7R8/3ywgP9BrID+wqiA/fukgPzsoIT/nZiE/gqUhPwnkIT+BIiI/5WAiPzmfIj963SI/qhsjP8ZZIz/RlyM/ydUjP68TJD+CUSQ/Qo8kP+/MJD+KCiU/EkglP4aFJT/owiU/NgAmP3A9Jj+XeiY/q7cmP6r0Jj+VMSc/bm4nPzKrJz/i5yc/fiQoPwRhKD93nSg/1tkoPyAWKT9VUik/dY4pP4DKKT93Bio/WEIqPyV+Kj/auSo/e/UqPwgxKz9+bCs/3qcrPyjjKz9dHiw/fFksP4WULD93zyw/UgotPxhFLT/Ify0/YLotP+L0LT9NLy4/oWkuP9+jLj8E3i4/ExgvPwtSLz/qiy8/s8UvP2T/Lz/+ODA/f3IwP+irMD885TA/dR4xP5ZXMT+gkDE/kMkxP2kCMj8pOzI/0XMyP16sMj/U5DI/MR0zP3VVMz+gjTM/sMUzP6j9Mz+HNTQ/TG00P/ekND+J3DQ/ABQ1P15LNT+igjU/zLk1P9zwNT/RJzY/rV42P22VNj8TzDY/nQI3Pw85Nz9lbzc/nqU3P7/bNz/DETg/rEc4P3p9OD8tszg/xeg4Pz8eOT+fUzk/44g5Pwu+OT8Y8zk/Big6P9pcOj+SkTo/LsY6P6z6Oj8OLzs/U2M7P32XOz+Jyzs/eP87P0ozPD//Zjw/l5o8PxHOPD9vAT0/rjQ9P9FnPT/Vmj0/u809P4QAPj8wMz4/vGU+PyuYPj99yj4/r/w+P8IuPz+5YD8/kJI/P0jEPz/i9T8/XCdAP7hYQD/2iUA/ErtAPxLsQD/xHEE/sU1BP1J+QT/TrkE/Nd9BP3YPQj+YP0I/mm9CP32fQj8+z0I/4P5CP2EuQz/EXUM/BY1DPyW8Qz8k60M/AxpEP8JIRD9fd0Q/3KVEPzfURD9zAkU/izBFP4NeRT9ajEU/D7pFP6PnRT8WFUY/ZkJGP5RvRj+hnEY/i8lGP1X2Rj/7Ikc/f09HP+B7Rz8gqEc/PtRHPzkASD8RLEg/xldIP1iDSD/Jrkg/FdpIPz8FST9FMEk/KFtJP+mFST+FsEk//tpJP1QFSj+GL0o/lFlKP3+DSj9GrUo/6NZKP2YASz/CKUs/+VJLPwp8Sz/3pEs/wc1LP2f2Sz/mHkw/QkdMP3pvTD+Kl0w/eb9MP0DnTD/kDk0/YTZNP7pdTT/uhE0//KtNP+XSTT+n+U0/RSBOP71GTj8QbU4/PJNOP0K5Tj8i304/3QRPP3IqTz/gT08/J3VPP0iaTz9Dv08/F+RPP8UIUD9MLVA/rVFQP+d1UD/6mVA/5b1QP6rhUD9HBVE/vihRPw1MUT81b1E/NZJRPw61UT/A11E/SvpRP6wcUj/mPlI/+GBSP+SCUj+mpFI/QcZSP7LnUj/8CFM/HypTPxhLUz/qa1M/k4xTPxOtUz9rzVM/mu1TP6ANVD99LVQ/Mk1UP75sVD8gjFQ/WatUP2rKVD9Q6VQ/DwhVP6MmVT8PRVU/T2NVP2eBVT9Vn1U/Gr1VP7XaVT8m+FU/bRVWP4syVj9+T1Y/RmxWP+WIVj9apVY/pcFWP8TdVj+7+VY/hRVXPyYxVz+cTFc/6GdXPwmDVz/+nVc/ybhXP2nTVz/e7Vc/KQhYP0ciWD87PFg/BFZYP6FvWD8TiVg/WaJYP3W7WD9l1Fg/KO1YP8AFWT8tHlk/bjZZP4ROWT9sZlk/Kn5ZP7qVWT8frVk/WcRZP2bbWT9H8lk/+whaP4MfWj/eNVo/DkxaPxBiWj/md1o/kI1aPw2jWj9duFo/gM1aP3biWj9A91o/3QtbP00gWz+PNFs/pEhbP41cWz9IcFs/1oNbPzWXWz9qqls/br1bP0fQWz/y4ls/b/VbP74HXD/gGVw/1CtcP5o9XD8zT1w/nGBcP9lxXD/oglw/yJNcP3ykXD//tFw/VsVcP33VXD935Vw/QvVcP94EXT9OFF0/jSNdP58yXT+CQV0/NlBdP7teXT8TbV0/O3tdPzWJXT//ll0/nKRdPwiyXT9Gv10/VcxdPzTZXT/l5V0/Z/JdP7n+XT/cCl4/zxZeP5QiXj8pLl4/jzleP8VEXj/NT14/pFpeP0xlXj/Eb14/DHpePyWEXj8Ojl4/yJdeP1GhXj+rql4/1bNeP8+8Xj+ZxV4/Ms5eP5zWXj/W3l4/3+ZeP7ruXj9k9l4/3P1ePyUFXz8+DF8/JxNfP94ZXz9mIF8/vSZfP+MsXz/bMl8/oDhfPzQ+Xz+ZQ18/zUhfP9FNXz+jUl8/RVdfP7ZbXz/3X18/B2RfP+RnXz+Ta18/D29fP1tyXz92dV8/YHhfPxl7Xz+gfV8/+H9fPx2CXz8ShF8/1YVfP2eHXz/HiF8/+IlfP/aKXz/Di18/XoxfP8mMXz8CjV8/CI1fP9+MXz+DjF8/9otfPzmLXz9Jil8/J4lfP9SHXz9Ohl8/mIRfP7CCXz+XgF8/S35fP857Xz8geV8/P3ZfPyxzXz/pb18/cmxfP8toXz/wZF8/5WBfP6hcXz84WF8/llNfP8NOXz+9SV8/hkRfPxw/Xz+COV8/szNfP7QtXz+DJ18/HiFfP4kaXz/CE18/xwxfP5sFXz88/l4/rPZeP+nuXj/z5l4/zN5eP3PWXj/nzV4/KMVePzi8Xj8Us14/wKlePzmgXj9/ll4/k4xeP3WCXj8keF4/oW1ePw=="

_ENGS = ("sync", "gpsimd", "vector", "scalar", "tensor")


class Plan:
    def __init__(self, sems):
        self.streams = {e: [] for e in _ENGS}
        self.sems = sems
        self.cnt = {k: 0 for k in sems}

    def op(self, eng, fn, inc=None, k=1):
        if inc is None:
            self.streams[eng].append(lambda e, fn=fn: fn(e))
            return None
        self.cnt[inc] += k
        sem = self.sems[inc]
        self.streams[eng].append(lambda e, fn=fn, sem=sem, k=k: fn(e).then_inc(sem, k))
        return self.cnt[inc]

    def wait(self, eng, name, val):
        if not val:
            return
        sem = self.sems[name]
        self.streams[eng].append(lambda e, sem=sem, val=val: e.wait_ge(sem, val))

    def replay(self, block):
        for name in _ENGS:
            stream = self.streams[name]
            getattr(block, name)(lambda eng, stream=stream: [f(eng) for f in stream])


def _build_program(n_passes=NPASS, n_experts=NE):
    nc = bass.Bass("TRN2", target_bir_lowering=False)

    d_x = nc.dram_tensor("x_fm", [NPASS, NKC, 128, TPASS], F32, kind="ExternalInput").ap()
    d_xrow = nc.dram_tensor("xrow", [1, 1024], F32, kind="ExternalInput").ap()
    d_wgu = nc.dram_tensor("wgu", [NE, NGRP, 128, 8192], F32, kind="ExternalInput").ap()
    d_wd = nc.dram_tensor("wd", [NE, NF, 128, 2048], F32, kind="ExternalInput").ap()
    d_wel = nc.dram_tensor("wel", [128, 192], F32, kind="ExternalInput").ap()
    d_wle = nc.dram_tensor("wle", [24, 1024], F32, kind="ExternalInput").ap()
    d_gw = nc.dram_tensor("gw", [128, 64], F32, kind="ExternalInput").ap()
    d_ble = nc.dram_tensor("ble", [128, 8], F32, kind="ExternalInput").ap()
    d_bel = nc.dram_tensor("bel", [24, 1], F32, kind="ExternalInput").ap()
    d_ht = nc.dram_tensor("ht", [24, 12], F32, kind="ExternalInput").ap()
    d_ident = nc.dram_tensor("ident", [128, 128], F32, kind="ExternalInput").ap()
    d_sel = nc.dram_tensor("sel_in", [8, 1024], F32, kind="ExternalInput").ap()
    d_ka = nc.dram_tensor("ka", [128, 8], F32, kind="ExternalInput").ap()
    d_fre = nc.dram_tensor("fre", [128, 8], F32, kind="ExternalInput").ap()
    d_fim = nc.dram_tensor("fim", [128, 8], F32, kind="ExternalInput").ap()
    d_cosh = nc.dram_tensor("cosh", [128, 8192], F32, kind="ExternalInput").ap()
    d_cosl = nc.dram_tensor("cosl", [128, 8192], F32, kind="ExternalInput").ap()
    d_sinh = nc.dram_tensor("sinh", [128, 8192], F32, kind="ExternalInput").ap()
    d_sinl = nc.dram_tensor("sinl", [128, 8192], F32, kind="ExternalInput").ap()
    d_out = nc.dram_tensor("out", [NPASS, NF, 128, TPASS], F32, kind="ExternalOutput").ap()
    d_hext = nc.dram_tensor("hext", [2048], BF16, kind="Internal").ap()

    with contextlib.ExitStack() as stk:
        ent = stk.enter_context
        xn = ent(nc.sbuf_tensor("xn", [128, NKC * TPASS], BF16))
        zall = ent(nc.sbuf_tensor("zall", [128, NM * TPASS], BF16))
        acc = ent(nc.sbuf_tensor("acc", [128, NF * TPASS], F32))
        wgu0 = ent(nc.sbuf_tensor("wgu0", [128, 8192], BF16))
        wgu1 = ent(nc.sbuf_tensor("wgu1", [128, 8192], BF16))
        wdbs = [ent(nc.sbuf_tensor(f"wdb{i}", [128, 2048], BF16)) for i in range(4)]
        xck0 = ent(nc.sbuf_tensor("xck0", [128, 1024], F32))
        xck1 = ent(nc.sbuf_tensor("xck1", [128, 1024], F32))
        xsq = ent(nc.sbuf_tensor("xsq", [128, 2048], BF16))
        ball = ent(nc.sbuf_tensor("ball", [128, 1024], BF16))
        rowsb = ent(nc.sbuf_tensor("rowsb", [128, 1024], BF16))
        matrow = ent(nc.sbuf_tensor("matrow", [1, 4096], F32))
        rowA = ent(nc.sbuf_tensor("rowA", [1, 1024], F32))
        rowB = ent(nc.sbuf_tensor("rowB", [1, 1024], F32))
        msc = ent(nc.sbuf_tensor("msc", [1, 64], F32))
        fsc = ent(nc.sbuf_tensor("fsc", [128, 64], F32))
        fsc2 = ent(nc.sbuf_tensor("fsc2", [128, 64], F32))
        dbc = ent(nc.sbuf_tensor("dbc", [128, 16], F32))
        csb = ent(nc.sbuf_tensor("csb", [128, 2048], BF16))
        tmpn = ent(nc.sbuf_tensor("tmpn", [128, 512], F32))
        siluT = ent(nc.sbuf_tensor("siluT", [128, 512], F32))
        ucT = ent(nc.sbuf_tensor("ucT", [128, 512], F32))
        outst = ent(nc.sbuf_tensor("outst", [128, 1024], F32))
        cwsb = ent(nc.sbuf_tensor("cwsb", [128, 192], BF16))
        welsb = ent(nc.sbuf_tensor("welsb", [128, 192], BF16))
        gwbf = ent(nc.sbuf_tensor("gwbf", [128, 64], BF16))
        identf = ent(nc.sbuf_tensor("identf", [128, 128], F32))
        wlesb = ent(nc.sbuf_tensor("wlesb", [24, 1024], BF16))
        bits = ent(nc.sbuf_tensor("bits", [24, 1024], BF16))
        synb = ent(nc.sbuf_tensor("synb", [12, 1024], BF16))
        mfix = ent(nc.sbuf_tensor("mfix", [24, 1024], BF16))
        corr = ent(nc.sbuf_tensor("corr", [24, 1024], BF16))
        cmbf = ent(nc.sbuf_tensor("cmbf", [8, 1024], BF16))
        cmtm = ent(nc.sbuf_tensor("cmtm", [128, 64], F32))
        bleb = ent(nc.sbuf_tensor("bleb", [128, 8], F32))
        belb = ent(nc.sbuf_tensor("belb", [24, 1], F32))
        htb = ent(nc.sbuf_tensor("htb", [24, 12], BF16))
        sel = ent(nc.sbuf_tensor("sel", [8, 1024], BF16))
        onesf = ent(nc.sbuf_tensor("onesf", [1, 128], F32))
        onesfc = ent(nc.sbuf_tensor("onesfc", [128, 1], F32))
        onesbf = ent(nc.sbuf_tensor("onesbf", [1, 128], BF16))
        oneskbf = ent(nc.sbuf_tensor("oneskbf", [128, 1], BF16))
        ones12bf = ent(nc.sbuf_tensor("ones12bf", [12, 1], BF16))
        onecol1 = ent(nc.sbuf_tensor("onecol1", [1, 1], F32))
        cns = ent(nc.sbuf_tensor("cns", [128, 16], F32))
        lxrow = ent(nc.sbuf_tensor("lxrow", [1, 8], F32))

        ps = [ent(nc.psum_tensor(f"p{i}", [128, 512], F32)) for i in range(8)]

        sems = {}
        for name in ("ldx", "ldy", "ldw", "sv", "ss", "stt", "outs"):
            sems[name] = ent(nc.semaphore(name))
        P = Plan(sems)

        trig = acc[:]

        lx_np = np.log(np.array([2.0, 4.0, 8.0, 16.0, 32.0], np.float32))
        lxsum = float(lx_np.sum())
        den = np.float32(5.0) * np.float32((lx_np * lx_np).sum()) - np.float32(lx_np.sum()) ** 2
        neg_invden = float(-(np.float32(1.0) / den))

        def v(fn):
            return P.op("vector", fn, "sv")

        def s_(fn):
            return P.op("scalar", fn, "ss")

        def t_(fn):
            return P.op("tensor", fn, None)

        def drain():
            return P.op("tensor", lambda e: e.drain(), "stt")

        def vw(name, val):
            P.wait("vector", name, val)

        def sw(name, val):
            P.wait("scalar", name, val)

        def tw(name, val):
            P.wait("tensor", name, val)

        def vself():
            P.wait("vector", "sv", P.cnt["sv"])

        v(lambda e: e.memset(onesf[:], 1.0))
        v(lambda e: e.memset(onesfc[:], 1.0))
        v(lambda e: e.memset(onesbf[:], 1.0))
        v(lambda e: e.memset(oneskbf[:], 1.0))
        v(lambda e: e.memset(ones12bf[:], 1.0))
        v(lambda e: e.memset(onecol1[:], 1.0))
        v(lambda e: e.memset(cns[:, 0:1], float(np.float32(np.pi))))
        v(lambda e: e.memset(cns[:, 1:2], float(np.float32(np.pi / 2))))
        v(lambda e: e.memset(cns[:, 2:3], -1.0))
        v(lambda e: e.memset(cns[:, 3:4], 0.0))
        v(lambda e: e.memset(cns[:, 4:5], 1.0))
        v(lambda e: e.memset(mfix[:], 0.0))
        v(lambda e: e.memset(dbc[:, 1:2], 1.0))
        for i, lv in enumerate(lx_np):
            v(lambda e, i=i, lv=float(lv): e.memset(lxrow[0:1, i:i + 1], lv))
        sv_consts = P.cnt["sv"]

        def dx(fn):
            P.wait("sync", "ldx", P.cnt["ldx"])
            return P.op("sync", fn, "ldx", 16)

        def dy(fn):
            P.wait("sync", "ldy", P.cnt["ldy"])
            return P.op("sync", fn, "ldy", 16)

        def dg(fn):
            P.wait("gpsimd", "ldw", P.cnt["ldw"])
            return P.op("gpsimd", fn, "ldw", 16)

        dx(lambda e: e.dma_start(out=fsc[:, 0:8], in_=d_ka))
        dx(lambda e: e.dma_start(out=fsc[:, 16:24], in_=d_fre))
        dx(lambda e: e.dma_start(out=fsc[:, 24:32], in_=d_fim))
        dx(lambda e: e.dma_start(out=rowA[:], in_=d_xrow))
        dx(lambda e: e.dma_start(out=bleb[:], in_=d_ble))
        dx(lambda e: e.dma_start(out=belb[:], in_=d_bel))
        dx(lambda e: e.dma_start(out=identf[:], in_=d_ident))
        ldx_consts = P.cnt["ldx"]
        dg(lambda e: e.dma_start(out=welsb[:], in_=d_wel))
        dg(lambda e: e.dma_start(out=wlesb[:], in_=d_wle))
        dg(lambda e: e.dma_start(out=htb[:], in_=d_ht))
        dg(lambda e: e.dma_start(out=gwbf[:], in_=d_gw))
        dg(lambda e: e.dma_start(out=sel[:], in_=d_sel))
        ldw_consts = P.cnt["ldw"]

        vw("ldx", ldx_consts)
        vw("sv", sv_consts)
        v(lambda e: e.tensor_tensor(fsc[:, 8:16], fsc[:, 0:8], fsc[:, 0:8], ALU.mult))
        v(lambda e: e.reduce_sum(msc[0:1, 0:1], rowA[:], axis=AX.X))
        vself()
        v(lambda e: e.tensor_scalar(msc[0:1, 1:2], msc[0:1, 0:1], 1.0 / 1024.0, None, ALU.mult))
        vself()
        v(lambda e: e.tensor_scalar(rowA[:], rowA[:], msc[0:1, 1:2], None, ALU.subtract))
        vself()
        v(lambda e: e.tensor_tensor(rowB[:], rowA[:], rowA[:], ALU.mult))
        vself()
        v(lambda e: e.reduce_sum(msc[0:1, 2:3], rowB[:], axis=AX.X))
        vself()
        v(lambda e: e.tensor_scalar(msc[0:1, 3:4], msc[0:1, 2:3], 1.0 / 1024.0, None, ALU.mult))
        vself()
        v(lambda e: e.tensor_scalar(msc[0:1, 4:5], msc[0:1, 3:4], 1e-5, None, ALU.add))
        sw("sv", P.cnt["sv"])
        s_(lambda e: e.activation(msc[0:1, 5:6], msc[0:1, 4:5], AF.Sqrt))
        vw("ss", P.cnt["ss"])
        v(lambda e: e.reciprocal(msc[0:1, 6:7], msc[0:1, 5:6]))
        vself()
        v(lambda e: e.tensor_tensor(msc[0:1, 7:8], msc[0:1, 6:7], msc[0:1, 6:7], ALU.mult))
        vself()
        v(lambda e: e.tensor_tensor(msc[0:1, 8:9], msc[0:1, 4:5], msc[0:1, 7:8], ALU.mult))
        vself()
        v(lambda e: e.tensor_scalar(msc[0:1, 9:10], msc[0:1, 8:9], -0.5, 1.5, ALU.mult, ALU.add))
        vself()
        v(lambda e: e.tensor_tensor(msc[0:1, 10:11], msc[0:1, 6:7], msc[0:1, 9:10], ALU.mult))
        vself()
        v(lambda e: e.tensor_scalar(matrow[0:1, 2048:3072], rowA[:], msc[0:1, 10:11],
                                    None, ALU.mult))
        vself()
        for i, sc2 in enumerate((4, 16, 64, 256, 1024)):
            v(lambda e, sc2=sc2: e.reduce_sum(msc[0:1, 12:13],
                                              matrow[0:1, 2048:2048 + sc2], axis=AX.X))
            vself()
            v(lambda e, sc2=sc2: e.tensor_scalar(msc[0:1, 13:14], msc[0:1, 12:13],
                                                 1.0 / sc2, None, ALU.mult))
            vself()
            v(lambda e, sc2=sc2: e.tensor_scalar(rowB[0:1, 0:sc2],
                                                 matrow[0:1, 2048:2048 + sc2],
                                                 msc[0:1, 13:14], None, ALU.is_gt))
            vself()
            v(lambda e, sc2=sc2: e.reduce_sum(msc[0:1, 14:15], rowB[0:1, 0:sc2], axis=AX.X))
            vself()
            v(lambda e, i=i: e.tensor_scalar(msc[0:1, 32 + i:33 + i], msc[0:1, 14:15],
                                             1.0, None, ALU.max))
            vself()
        sw("sv", P.cnt["sv"])
        s_(lambda e: e.activation(msc[0:1, 40:45], msc[0:1, 32:37], AF.Ln))
        vw("ss", P.cnt["ss"])
        v(lambda e: e.tensor_tensor(msc[0:1, 48:53], msc[0:1, 40:45], lxrow[0:1, 0:5],
                                    ALU.mult))
        vself()
        v(lambda e: e.reduce_sum(msc[0:1, 16:17], msc[0:1, 48:53], axis=AX.X))
        v(lambda e: e.reduce_sum(msc[0:1, 17:18], msc[0:1, 40:45], axis=AX.X))
        vself()
        v(lambda e: e.tensor_scalar(msc[0:1, 18:19], msc[0:1, 16:17], 5.0, None, ALU.mult))
        v(lambda e: e.tensor_scalar(msc[0:1, 19:20], msc[0:1, 17:18], lxsum, None, ALU.mult))
        vself()
        v(lambda e: e.tensor_tensor(msc[0:1, 20:21], msc[0:1, 18:19], msc[0:1, 19:20],
                                    ALU.subtract))
        vself()
        v(lambda e: e.tensor_scalar(msc[0:1, 21:22], msc[0:1, 20:21], neg_invden,
                                    None, ALU.mult))
        vself()
        v(lambda e: e.tensor_scalar(msc[0:1, 30:31], msc[0:1, 21:22], 1.0, 2.0,
                                    ALU.max, ALU.min))
        sv_D = P.cnt["sv"]
        tw("sv", sv_D)
        t_(lambda e: e.matmul(ps[0][:, 0:1], onesf[:], msc[0:1, 30:31],
                              start=True, stop=True))
        stt_db = drain()
        vw("stt", stt_db)
        v(lambda e: e.tensor_copy(dbc[:, 0:1], ps[0][:, 0:1]))
        vself()

        def filt_pipeline(xi, out0):
            X = dbc[:, xi:xi + 1]
            c0 = 2 + 3 * xi
            v(lambda e: e.tensor_tensor(dbc[:, c0:c0 + 1], X, X, ALU.mult))
            vself()
            v(lambda e: e.tensor_scalar(dbc[:, c0:c0 + 1], dbc[:, c0:c0 + 1], 0.2,
                                        None, ALU.mult))
            vself()
            v(lambda e: e.tensor_tensor(dbc[:, c0:c0 + 1], X, dbc[:, c0:c0 + 1], ALU.add))
            v(lambda e: e.tensor_scalar(dbc[:, c0 + 1:c0 + 2], X, 0.5, 1.0,
                                        ALU.mult, ALU.add))
            v(lambda e: e.tensor_scalar(dbc[:, c0 + 2:c0 + 3], X, 0.1, None, ALU.mult))
            vself()
            v(lambda e: e.tensor_scalar(fsc[:, 32:40], fsc[:, 0:8], dbc[:, c0 + 1:c0 + 2],
                                        None, ALU.mult))
            vself()
            v(lambda e: e.tensor_scalar(fsc[:, 32:40], fsc[:, 32:40], dbc[:, c0:c0 + 1],
                                        None, ALU.add))
            sw("sv", P.cnt["sv"])
            s_(lambda e: e.activation(fsc[:, 40:48], fsc[:, 32:40], AF.Sin,
                                      bias=cns[:, 0:1], scale=cns[:, 2:3]))
            v(lambda e: e.tensor_scalar(fsc[:, 48:56], fsc[:, 8:16], dbc[:, c0 + 2:c0 + 3],
                                        None, ALU.mult))
            vself()
            v(lambda e: e.tensor_tensor(fsc[:, 48:56], fsc[:, 48:56], fsc[:, 0:8],
                                        ALU.subtract))
            vself()
            v(lambda e: e.tensor_scalar(fsc[:, 48:56], fsc[:, 48:56], dbc[:, c0:c0 + 1],
                                        None, ALU.add))
            sw("sv", P.cnt["sv"])
            s_(lambda e: e.activation(fsc[:, 56:64], fsc[:, 48:56], AF.Sin,
                                      bias=cns[:, 1:2], scale=cns[:, 2:3]))
            s_(lambda e: e.activation(fsc2[:, 48:56], fsc[:, 48:56], AF.Sin,
                                      bias=cns[:, 3:4], scale=cns[:, 4:5]))
            vw("ss", P.cnt["ss"])
            v(lambda e, out0=out0: e.tensor_tensor(fsc2[:, out0:out0 + 8],
                                                   fsc[:, 40:48], fsc[:, 56:64], ALU.mult))
            vself()
            v(lambda e, out0=out0: e.tensor_tensor(fsc2[:, out0 + 8:out0 + 16],
                                                   fsc[:, 40:48], fsc2[:, 48:56], ALU.mult))
            vself()

        filt_pipeline(0, 0)
        filt_pipeline(1, 16)
        v(lambda e: e.tensor_tensor(fsc2[:, 32:40], fsc2[:, 0:8], fsc2[:, 16:24],
                                    ALU.subtract))
        v(lambda e: e.tensor_tensor(fsc2[:, 40:48], fsc2[:, 8:16], fsc2[:, 24:32],
                                    ALU.subtract))
        vself()
        v(lambda e: e.tensor_tensor(fsc2[:, 32:40], fsc2[:, 32:40], fsc[:, 16:24], ALU.add))
        v(lambda e: e.tensor_tensor(fsc2[:, 40:48], fsc2[:, 40:48], fsc[:, 24:32], ALU.add))
        vself()
        v(lambda e: e.tensor_scalar(fsc2[:, 32:40], fsc2[:, 32:40], 1.0 / 1024.0,
                                    None, ALU.mult))
        v(lambda e: e.tensor_scalar(fsc2[:, 40:48], fsc2[:, 40:48], -1.0 / 1024.0,
                                    None, ALU.mult))
        sv_spec = P.cnt["sv"]

        half_free = {0: 0, 1: 0}
        first = [True, True]
        hparts = ((d_cosh, 32), (d_sinh, 40), (d_cosl, 32), (d_sinl, 40))
        tw("sv", sv_spec)
        stt_hd = 0
        for li, (dsrc, repc) in enumerate(hparts):
            for hf in range(2):
                P.wait("sync", "stt", half_free[hf])
                c = dx(lambda e, dsrc=dsrc, hf=hf: e.dma_start(
                    out=trig[:, hf * 4096:(hf + 1) * 4096],
                    in_=dsrc[:, hf * 4096:(hf + 1) * 4096]))
                tw("ldx", c)
                for oh in range(2):
                    for kc in range(4 * hf, 4 * hf + 4):
                        st = first[oh]
                        first[oh] = False
                        sp = (li == 3 and hf == 1 and kc == 7)
                        t_(lambda e, oh=oh, kc=kc, repc=repc, st=st, sp=sp:
                           e.matmul(ps[1 + oh][0:1, 0:512],
                                    fsc2[:, repc + kc:repc + kc + 1],
                                    trig[:, kc * 1024 + oh * 512: kc * 1024 + oh * 512 + 512],
                                    start=st, stop=sp))
                stt_hd = drain()
                half_free[hf] = stt_hd
        stt_h = stt_hd
        vw("stt", stt_h)
        v(lambda e: e.tensor_copy(rowB[0:1, 0:512], ps[1][0:1, 0:512]))
        v(lambda e: e.tensor_copy(rowB[0:1, 512:1024], ps[2][0:1, 0:512]))
        vself()
        v(lambda e: e.tensor_copy(rowsb[32:33, :], rowB[:]))
        sv_hbf = P.cnt["sv"]
        P.wait("sync", "sv", sv_hbf)
        dx(lambda e: e.dma_start(out=d_hext[0:1024], in_=rowsb[32:33, :]))
        dx(lambda e: e.dma_start(out=d_hext[1024:2048], in_=rowsb[32:33, :]))
        c_hext = P.cnt["ldx"]
        P.wait("sync", "ldx", c_hext)
        x0_load = dx(lambda e: e.dma_start(
            out=acc[:].rearrange("p (kc t) -> p kc t", kc=8),
            in_=d_x[0].transpose([1, 0, 2])))
        P.wait("sync", "ldx", P.cnt["ldx"])
        for p in range(128):
            ap_p = bass.AP(d_hext.tensor, 1024 - p, [[128, 8], [1, 128]])
            P.op("sync", lambda e, p=p, ap_p=ap_p: e.dma_start(
                out=ball[p:p + 1, :].rearrange("a (d j) -> a d j", d=8), in_=ap_p),
                "ldx", 16)
        ldx_ball = P.cnt["ldx"]

        sv_dpath = P.cnt["sv"]
        cw_emitted = [False]

        def emit_cw():
            cw_emitted[0] = True
            tw("ldx", ldx_ball)
            tw("ldw", ldw_consts)
            for kc in range(8):
                for fc in range(8):
                    d = (kc - fc) % 8
                    t_(lambda e, kc=kc, fc=fc, d=d:
                       e.matmul(ps[3][:, kc * 24:(kc + 1) * 24],
                                ball[:, d * 128:(d + 1) * 128],
                                welsb[:, fc * 24:(fc + 1) * 24],
                                start=(fc == 0), stop=(fc == 7)))
            stt_cw = drain()
            vw("stt", stt_cw)
            v(lambda e: e.tensor_copy(cwsb[:], ps[3][:, 0:192]))

        wgu_free = {0: 0, 1: 0}
        wd_free = {0: 0, 1: 0, 2: 0, 3: 0}
        wgu_loaded = {}
        wd_loaded = {}
        xck = (xck0, xck1)
        xck_freeMM = {0: 0, 1: 0}
        xck_freeV = {0: 0, 1: 0}
        outst_free = {0: 0, 1: 0}
        stt_asm_done = stt_h
        ss_cbc_prev = 0
        sv_cmbf = 0
        sv_acc_last = 0
        sv_csb_read_last = 0
        zdone = {}
        silu_ss = {}
        prev_z = [0]
        pb67_free = [0]
        pb_pair_free = {0: 0, 1: 0, 2: 0}
        xsq_free = [stt_h]
        stats_first = {}
        stats_last_drain = {}
        xpair_done = {}

        def emit_stats_pair(pp2, kcs, is_last):
            cx, cy = xpair_done[(pp2, kcs)]
            sw("ldx", cx)
            sw("ldy", cy)
            sw("stt", xsq_free[0])
            for kc in kcs:
                s_(lambda e, kc=kc: e.square(
                    xsq[:, (kc % 2) * 1024:(kc % 2) * 1024 + 1024],
                    acc[:, kc * 1024:(kc + 1) * 1024]))
            ssq = P.cnt["ss"]
            first = stats_first.setdefault(pp2, [True])
            tw("ldx", cx)
            tw("ldy", cy)
            for j, kc in enumerate(kcs):
                st = first[0] and j == 0
                sp = is_last and j == len(kcs) - 1
                for tb in range(NTB):
                    t_(lambda e, kc=kc, tb=tb, st=st, sp=sp:
                       e.matmul(ps[4 + tb][0:1, 0:512], onesfc[:],
                                acc[:, kc * 1024 + tb * 512: kc * 1024 + tb * 512 + 512],
                                start=st, stop=sp))
            tw("ss", ssq)
            for j, kc in enumerate(kcs):
                st = first[0] and j == 0
                sp = is_last and j == len(kcs) - 1
                for tb in range(NTB):
                    t_(lambda e, kc=kc, tb=tb, st=st, sp=sp:
                       e.matmul(ps[6 + tb][0:1, 0:512], oneskbf[:],
                                xsq[:, (kc % 2) * 1024 + tb * 512: (kc % 2) * 1024 + tb * 512 + 512],
                                start=st, stop=sp))
            first[0] = False
            ds = drain()
            xsq_free[0] = ds
            stats_last_drain[pp2] = ds
            return ds
        ss_cbc_prev = [0]
        ss_csb_of = {}
        csb_read_done = {}

        PAIRS_ASM = ((6, 7), (4, 5), (2, 3), (0, 1))
        for pp in range(n_passes):
            if pp == 0:
                for pr in PAIRS_ASM:
                    xpair_done[(0, pr)] = (x0_load, 0)
            if pp == 0:
                tw("sv", sv_dpath)
                sw("sv", sv_dpath)
                for i, pr in enumerate(((0, 1), (2, 3), (4, 5), (6, 7))):
                    emit_stats_pair(0, pr, i == 3)
            else:
                emit_stats_pair(pp, (0, 1), True)
            stt_stats = stats_last_drain[pp]
            cxl, cyl = xpair_done[(pp, (0, 1))]
            vw("stt", stt_stats)
            for tb in range(NTB):
                v(lambda e, tb=tb: e.tensor_scalar(matrow[0:1, tb * 512:tb * 512 + 512],
                                                   ps[4 + tb][0:1, 0:512],
                                                   1.0 / 1024.0, None, ALU.mult))
                v(lambda e, tb=tb: e.tensor_scalar(rowA[0:1, tb * 512:tb * 512 + 512],
                                                   ps[6 + tb][0:1, 0:512],
                                                   1.0 / 1024.0, None, ALU.mult))
            vself()
            v(lambda e: e.tensor_tensor(rowB[:], matrow[0:1, 0:1024],
                                        matrow[0:1, 0:1024], ALU.mult))
            vself()
            v(lambda e: e.tensor_tensor(rowB[:], rowA[:], rowB[:], ALU.subtract))
            vself()
            v(lambda e: e.tensor_scalar(rowB[:], rowB[:], 1e-5, None, ALU.add))
            sw("sv", P.cnt["sv"])
            s_(lambda e: e.activation(rowA[:], rowB[:], AF.Sqrt))
            vw("ss", P.cnt["ss"])
            v(lambda e: e.reciprocal(matrow[0:1, 1024:2048], rowA[:]))
            vself()
            v(lambda e: e.tensor_tensor(rowA[:], matrow[0:1, 1024:2048],
                                        matrow[0:1, 1024:2048], ALU.mult))
            vself()
            v(lambda e: e.tensor_tensor(rowA[:], rowB[:], rowA[:], ALU.mult))
            vself()
            v(lambda e: e.tensor_scalar(rowA[:], rowA[:], -0.5, 1.5, ALU.mult, ALU.add))
            vself()
            v(lambda e: e.tensor_tensor(matrow[0:1, 1024:2048], matrow[0:1, 1024:2048],
                                        rowA[:], ALU.mult))
            sv_stats = P.cnt["sv"]

            tw("sv", sv_stats)
            tw("stt", stt_asm_done)
            for tb in range(NTB):
                t_(lambda e, tb=tb: e.matmul(ps[tb][:, 0:512], onesf[:],
                                             matrow[0:1, tb * 512:tb * 512 + 512],
                                             start=True, stop=True))
                t_(lambda e, tb=tb: e.matmul(ps[2 + tb][:, 0:512], onesf[:],
                                             matrow[0:1, 1024 + tb * 512:1024 + tb * 512 + 512],
                                             start=True, stop=True))
            stt_bc = drain()
            sw("stt", stt_bc)
            sw("sv", xck_freeV[0])
            sw("sv", xck_freeV[1])
            s_(lambda e: e.copy(xck0[:, 0:512], ps[0][:, 0:512]))
            s_(lambda e: e.copy(xck0[:, 512:1024], ps[1][:, 0:512]))
            s_(lambda e: e.copy(xck1[:, 0:512], ps[2][:, 0:512]))
            s_(lambda e: e.copy(xck1[:, 512:1024], ps[3][:, 0:512]))
            ss_mb = P.cnt["ss"]

            vw("ss", ss_mb)
            vw("ldx", cxl)
            vw("ldy", cyl)
            vw("stt", stt_asm_done)
            xn_chunk = {}
            for kc in range(NKC):
                vself()
                v(lambda e, kc=kc: e.tensor_tensor(
                    xn[:, kc * 1024:(kc + 1) * 1024],
                    acc[:, kc * 1024:(kc + 1) * 1024], xck0[:], ALU.subtract))
                vself()
                xn_chunk[kc] = v(lambda e, kc=kc: e.tensor_tensor(
                    xn[:, kc * 1024:(kc + 1) * 1024],
                    xn[:, kc * 1024:(kc + 1) * 1024], xck1[:], ALU.mult))
            sv_xn = P.cnt["sv"]
            xck_freeV[0] = sv_xn
            xck_freeV[1] = sv_xn
            tw("ss", ss_mb)
            if pp == 0:
                tw("ldw", ldw_consts)
            for kc in range(NKC):
                tw("sv", xn_chunk[kc])
                for tile in range(8):
                    t_(lambda e, kc=kc, tile=tile: e.matmul(
                        ps[tile][:, 0:8],
                        xn[:, kc * 1024 + tile * 128: kc * 1024 + (tile + 1) * 128],
                        gwbf[:, kc * 8:(kc + 1) * 8],
                        start=(kc == 0), stop=(kc == 7)))
            stt_lg = drain()
            acc_freeMM = stt_lg

            vw("stt", stt_lg)
            for tile in range(8):
                v(lambda e, tile=tile: e.tensor_copy(cmtm[:, tile * 8:(tile + 1) * 8],
                                                     ps[tile][:, 0:8]))
            vself()
            lgv = cmtm[:, 0:64]
            lg3 = lgv.rearrange("p (t j) -> p t j", t=8)
            a1s = tmpn[:, 0:8]
            a2s = tmpn[:, 8:16]
            w1c = tmpn[:, 16:24]
            w2c = tmpn[:, 24:32]
            d12 = tmpn[:, 32:40]
            m1v = tmpn[:, 64:128]
            csA = tmpn[:, 128:192]
            csB = tmpn[:, 192:256]
            f1v = tmpn[:, 256:320]
            mskd = tmpn[:, 320:384]
            f2v = tmpn[:, 384:448]

            def t3(ap):
                return ap.rearrange("p (t j) -> p t j", t=8)

            def bcast8(col8):
                return col8[:, :, None].broadcast_to([128, 8, 8])

            def cumsum_tile(src_v, dstA, dstB):
                v(lambda e: e.tensor_copy(dstA, src_v))
                vself()
                v(lambda e: e.tensor_copy(t3(dstB)[:, :, 0:1], t3(dstA)[:, :, 0:1]))
                v(lambda e: e.tensor_tensor(t3(dstB)[:, :, 1:8], t3(dstA)[:, :, 1:8],
                                            t3(dstA)[:, :, 0:7], ALU.add))
                vself()
                v(lambda e: e.tensor_copy(t3(dstA)[:, :, 0:2], t3(dstB)[:, :, 0:2]))
                v(lambda e: e.tensor_tensor(t3(dstA)[:, :, 2:8], t3(dstB)[:, :, 2:8],
                                            t3(dstB)[:, :, 0:6], ALU.add))
                vself()
                v(lambda e: e.tensor_copy(t3(dstB)[:, :, 0:4], t3(dstA)[:, :, 0:4]))
                v(lambda e: e.tensor_tensor(t3(dstB)[:, :, 4:8], t3(dstA)[:, :, 4:8],
                                            t3(dstA)[:, :, 0:4], ALU.add))
                vself()

            for tile in range(8):
                v(lambda e, tile=tile: e.reduce_max(a1s[:, tile:tile + 1],
                                                    lgv[:, tile * 8:(tile + 1) * 8],
                                                    axis=AX.X))
            vself()
            v(lambda e: e.tensor_tensor(t3(m1v), lg3, bcast8(a1s), ALU.is_ge))
            vself()
            cumsum_tile(m1v, csA, csB)
            v(lambda e: e.tensor_scalar(csB, csB, 1.5, None, ALU.is_le))
            vself()
            v(lambda e: e.tensor_tensor(f1v, m1v, csB, ALU.mult))
            vself()
            v(lambda e: e.scalar_tensor_tensor(mskd, f1v, -1e9, lgv, ALU.mult, ALU.add))
            vself()
            for tile in range(8):
                v(lambda e, tile=tile: e.reduce_max(a2s[:, tile:tile + 1],
                                                    mskd[:, tile * 8:(tile + 1) * 8],
                                                    axis=AX.X))
            vself()
            v(lambda e: e.tensor_tensor(t3(m1v), t3(mskd), bcast8(a2s), ALU.is_ge))
            vself()
            cumsum_tile(m1v, csA, csB)
            v(lambda e: e.tensor_scalar(csB, csB, 1.5, None, ALU.is_le))
            vself()
            v(lambda e: e.tensor_tensor(f2v, m1v, csB, ALU.mult))
            v(lambda e: e.tensor_tensor(d12, a1s, a2s, ALU.subtract))
            sw("sv", P.cnt["sv"])
            s_(lambda e: e.activation(w1c, d12, AF.Sigmoid))
            vw("ss", P.cnt["ss"])
            v(lambda e: e.tensor_scalar(w2c, w1c, -1.0, 1.0, ALU.mult, ALU.add))
            vself()
            v(lambda e: e.tensor_tensor(t3(f1v), t3(f1v), bcast8(w1c), ALU.mult))
            vself()
            v(lambda e: e.tensor_tensor(t3(f2v), t3(f2v), bcast8(w2c), ALU.mult))
            vself()
            v(lambda e: e.tensor_tensor(lg3, t3(f1v), t3(f2v), ALU.add))
            vself()
            sv_combine_tm = P.cnt["sv"]

            tw("sv", sv_combine_tm)
            tw("ss", ss_cbc_prev[0])
            for tile in range(8):
                pb = ps[6] if tile < 4 else ps[7]
                coff = (tile % 4) * 128
                t_(lambda e, tile=tile, pb=pb, coff=coff: e.transpose(
                    pb[0:8, coff:coff + 128], cmtm[:, tile * 8:(tile + 1) * 8], identf[:]))
            stt_ct = drain()
            vw("stt", stt_ct)
            for tile in range(8):
                pb = ps[6] if tile < 4 else ps[7]
                coff = (tile % 4) * 128
                v(lambda e, tile=tile, pb=pb, coff=coff: e.tensor_copy(
                    cmbf[0:8, tile * 128:(tile + 1) * 128], pb[0:8, coff:coff + 128]))
            sv_cmbf = P.cnt["sv"]

            def load_wgu(ee2, g):
                b = g % 2
                P.wait("gpsimd", "stt", wgu_free[b])
                wbuf = wgu0 if b == 0 else wgu1
                wgu_loaded[(pp, ee2, g)] = dg(
                    lambda e, ee2=ee2, g=g, wbuf=wbuf:
                    e.dma_start(out=wbuf[:], in_=d_wgu[ee2, g]))

            def load_wd(ee2, f):
                b = f % 4
                P.wait("gpsimd", "stt", wd_free[b])
                wbuf = wdbs[b]
                wd_loaded[(pp, ee2, f)] = dg(
                    lambda e, ee2=ee2, f=f, wbuf=wbuf:
                    e.dma_start(out=wbuf[:], in_=d_wd[ee2, f]))

            for ee in range(n_experts):
                load_wgu(ee, 0)
                load_wgu(ee, 1)
                for f0 in range(4):
                    load_wd(ee, f0)

                def emit_cbc(ee2, with_drain):
                    h2 = ee2 % 2
                    tw("sv", sv_cmbf)
                    tw("sv", pb67_free[0])
                    tw("ss", ss_cbc_prev[0])
                    for tb in range(NTB):
                        t_(lambda e, ee2=ee2, tb=tb, h2=h2: e.matmul(
                            ps[6 + tb][:, 0:512],
                            sel[:, ee2 * 128:(ee2 + 1) * 128],
                            cmbf[0:8, tb * 512:tb * 512 + 512],
                            start=True, stop=True))
                    dcb = drain() if with_drain else None
                    return dcb

                def evac_cbc(ee2, dcb):
                    h2 = ee2 % 2
                    sw("stt", dcb)
                    sw("sv", csb_read_done.get(ee2 - 2, 0))
                    s_(lambda e, h2=h2: e.copy(csb[:, h2 * 1024:h2 * 1024 + 512],
                                               ps[6][:, 0:512]))
                    s_(lambda e, h2=h2: e.copy(csb[:, h2 * 1024 + 512:h2 * 1024 + 1024],
                                               ps[7][:, 0:512]))
                    ss_csb_of[ee2] = P.cnt["ss"]
                    ss_cbc_prev[0] = P.cnt["ss"]

                a_groups = [tuple(range(s, min(s + 2, 2 * NM))) for s in range(0, 2 * NM, 2)]
                for G, us in enumerate(a_groups):
                    cbc_here = (G == 10 and ee + 1 < n_experts)
                    cbc0_here = (G == 2 and ee == 0)
                    for u in us:
                        m, tb = u // 2, u % 2
                        g = m // 4
                        bg = ps[2 * (u % 3)]
                        bu = ps[2 * (u % 3) + 1]
                        if u % 8 == 0:
                            tw("ldw", wgu_loaded[(pp, ee, g)])
                        if u == 0:
                            tw("sv", sv_xn)
                        if u < 3:
                            tw("sv", pb_pair_free[u])
                        prev = None
                        if u >= 3:
                            prev = (pp, ee, u - 3)
                        else:
                            pmap = {0: 30, 1: 31, 2: 29}
                            if ee > 0:
                                prev = (pp, ee - 1, pmap[u])
                            elif pp > 0:
                                prev = (pp - 1, n_experts - 1, pmap[u])
                        if prev is not None and prev in zdone:
                            tw("sv", zdone[prev])
                        mi = m % 4
                        for kc in range(NKC):
                            wbuf = wgu0 if g % 2 == 0 else wgu1
                            t_(lambda e, wbuf=wbuf, mi=mi, kc=kc, tb=tb, bg=bg:
                               e.matmul(bg[:, 0:512],
                                        wbuf[:, (mi * 8 + kc) * 128:(mi * 8 + kc) * 128 + 128],
                                        xn[:, kc * 1024 + tb * 512: kc * 1024 + tb * 512 + 512],
                                        start=(kc == 0), stop=(kc == 7)))
                        for kc in range(NKC):
                            wbuf = wgu0 if g % 2 == 0 else wgu1
                            t_(lambda e, wbuf=wbuf, mi=mi, kc=kc, tb=tb, bu=bu:
                               e.matmul(bu[:, 0:512],
                                        wbuf[:, ((4 + mi) * 8 + kc) * 128:((4 + mi) * 8 + kc) * 128 + 128],
                                        xn[:, kc * 1024 + tb * 512: kc * 1024 + tb * 512 + 512],
                                        start=(kc == 0), stop=(kc == 7)))
                    if cbc_here:
                        emit_cbc(ee + 1, False)
                    if cbc0_here:
                        emit_cbc(0, False)
                    dc = drain()
                    if cbc_here:
                        evac_cbc(ee + 1, dc)
                    if cbc0_here:
                        evac_cbc(0, dc)
                    for g_done in range(NGRP):
                        last_u = 8 * g_done + 7
                        if us[0] <= last_u <= us[-1]:
                            wgu_free[g_done % 2] = dc
                            if g_done + 2 <= 3:
                                load_wgu(ee, g_done + 2)
                    for u in us:
                        m, tb = u // 2, u % 2
                        bg = ps[2 * (u % 3)]
                        bu = ps[2 * (u % 3) + 1]
                        sw("stt", dc)
                        sw("sv", prev_z[0])
                        sc_ = s_(lambda e, bg=bg: e.activation(siluT[:], bg[:, 0:512], AF.Silu))
                        silu_ss[(pp, ee, u)] = sc_
                        vw("stt", dc)
                        vw("ss", sc_)
                        vself()
                        zdone[(pp, ee, u)] = v(lambda e, m=m, tb=tb, bu=bu: e.tensor_tensor(
                            zall[:, m * 1024 + tb * 512: m * 1024 + tb * 512 + 512],
                            siluT[:], bu[:, 0:512], ALU.mult))
                        prev_z[0] = zdone[(pp, ee, u)]
                sv_z_all = P.cnt["sv"]

                ev_of_unit = {}
                for G in range(4):
                    us = tuple(range(4 * G, 4 * G + 4))
                    lastA = {0: 30, 1: 31, 2: 29}
                    for u2 in us:
                        f, tb = u2 // 2, u2 % 2
                        bank = ps[u2 % 8]
                        if u2 % 2 == 0:
                            tw("ldw", wd_loaded[(pp, ee, f)])
                        if u2 < 6:
                            ua = lastA[(u2 // 2) % 3]
                            tw("sv", zdone[(pp, ee, ua)])
                            tw("ss", silu_ss[(pp, ee, ua)])
                        if u2 == 0:
                            tw("sv", zdone[(pp, ee, 30)])
                        if u2 == 1:
                            tw("sv", sv_z_all)
                        if u2 == 6:
                            tw("ss", ss_csb_of.get(ee + 1, 0))
                        if u2 >= 8:
                            tw("sv", ev_of_unit[u2 - 8])
                        wbuf = wdbs[f % 4]
                        for m in range(NM):
                            t_(lambda e, wbuf=wbuf, m=m, tb=tb, bank=bank:
                               e.matmul(bank[:, 0:512],
                                        wbuf[:, m * 128:(m + 1) * 128],
                                        zall[:, m * 1024 + tb * 512: m * 1024 + tb * 512 + 512],
                                        start=(m == 0), stop=(m == 15)))
                    dc = drain()
                    f_hi = us[-1] // 2
                    if f_hi % 2 == 1:
                        wd_free[(f_hi - 1) % 4] = dc
                        wd_free[f_hi % 4] = dc
                        for fnext in (f_hi + 3, f_hi + 4):
                            if fnext <= NF - 1:
                                load_wd(ee, fnext)
                    he = ee % 2
                    for u2 in us:
                        f, tb = u2 // 2, u2 % 2
                        bank = ps[u2 % 8]
                        vw("stt", dc)
                        if u2 == 0:
                            vw("ss", ss_csb_of[ee])
                        aslice = acc[:, f * 1024 + tb * 512: f * 1024 + tb * 512 + 512]
                        vself()
                        mv = v(lambda e, bank=bank, tb=tb, he=he: e.tensor_tensor(
                            ucT[:], bank[:, 0:512],
                            csb[:, he * 1024 + tb * 512: he * 1024 + tb * 512 + 512],
                            ALU.mult))
                        vself()
                        if ee == 0:
                            v(lambda e, f=f, aslice=aslice:
                              e.tensor_scalar(aslice, ucT[:],
                                              bleb[:, f:f + 1], None, ALU.add))
                        else:
                            v(lambda e, aslice=aslice:
                              e.tensor_tensor(aslice, ucT[:], aslice, ALU.add))
                        ev_of_unit[u2] = mv
                        if u2 in (9, 11, 13):
                            pb_pair_free[(u2 - 9) // 2] = mv
                        if u2 == 15:
                            pb67_free[0] = mv
                csb_read_done[ee] = P.cnt["sv"]
                sv_acc_last = P.cnt["sv"]

            tw("sv", sv_xn)
            tw("sv", sv_acc_last)
            if not cw_emitted[0]:
                emit_cw()
                tw("sv", P.cnt["sv"])
            for tb in range(NTB):
                for kc in range(NKC):
                    t_(lambda e, tb=tb, kc=kc: e.matmul(
                        ps[tb][0:24, 0:512],
                        cwsb[:, kc * 24:(kc + 1) * 24],
                        xn[:, kc * 1024 + tb * 512: kc * 1024 + tb * 512 + 512],
                        start=(kc == 0), stop=(kc == 7)))
            stt_lat = drain()
            vw("stt", stt_lat)
            for tb in range(NTB):
                vself()
                v(lambda e, tb=tb: e.tensor_scalar(
                    bits[0:24, tb * 512:tb * 512 + 512], ps[tb][0:24, 0:512],
                    belb[:], 0.0, ALU.add, ALU.is_gt))
            sv_bits = P.cnt["sv"]
            tw("sv", sv_bits)
            for tb in range(NTB):
                t_(lambda e, tb=tb: e.matmul(ps[2 + tb][0:12, 0:512], htb[:],
                                             bits[0:24, tb * 512:tb * 512 + 512],
                                             start=True, stop=True))
            stt_syn = drain()
            vw("stt", stt_syn)
            for tb in range(NTB):
                vself()
                v(lambda e, tb=tb: e.tensor_scalar(
                    synb[0:12, tb * 512:tb * 512 + 512], ps[2 + tb][0:12, 0:512],
                    0.5, None, ALU.is_gt))
            sv_syn = P.cnt["sv"]
            tw("sv", sv_syn)
            for tb in range(NTB):
                t_(lambda e, tb=tb: e.matmul(ps[4 + tb][0:1, 0:512], ones12bf[:],
                                             synb[0:12, tb * 512:tb * 512 + 512],
                                             start=True, stop=True))
            stt_w = drain()
            vw("stt", stt_w)
            for tb in range(NTB):
                vself()
                v(lambda e, tb=tb: e.tensor_scalar(rowA[0:1, tb * 512:(tb + 1) * 512],
                                                   ps[4 + tb][0:1, 0:512],
                                                   -2.0, None, ALU.add))
            vself()
            v(lambda e: e.tensor_scalar(rowB[:], rowA[:], -1.5, None, ALU.is_gt))
            vself()
            v(lambda e: e.tensor_scalar(rowA[:], rowA[:], 1.5, None, ALU.is_lt))
            vself()
            v(lambda e: e.tensor_tensor(rowB[:], rowB[:], rowA[:], ALU.mult))
            vself()
            v(lambda e: e.tensor_copy(rowsb[0:1, :], rowB[:]))
            sv_df = P.cnt["sv"]
            tw("sv", sv_df)
            for tb in range(NTB):
                t_(lambda e, tb=tb: e.matmul(ps[6 + tb][0:12, 0:512],
                                             onesbf[0:1, 0:12],
                                             rowsb[0:1, tb * 512:tb * 512 + 512],
                                             start=True, stop=True))
            stt_dfb = drain()
            vw("stt", stt_dfb)
            for tb in range(NTB):
                sl0 = tb * 512
                vself()
                v(lambda e, tb=tb, sl0=sl0: e.tensor_tensor(
                    mfix[0:12, sl0:sl0 + 512], synb[0:12, sl0:sl0 + 512],
                    ps[6 + tb][0:12, 0:512], ALU.mult))
                vself()
                v(lambda e, sl0=sl0: e.tensor_tensor(
                    csb[0:24, 0:512], bits[0:24, sl0:sl0 + 512],
                    mfix[0:24, sl0:sl0 + 512], ALU.mult))
                v(lambda e, sl0=sl0: e.tensor_tensor(
                    csb[0:24, 512:1024], bits[0:24, sl0:sl0 + 512],
                    mfix[0:24, sl0:sl0 + 512], ALU.add))
                vself()
                v(lambda e, sl0=sl0: e.scalar_tensor_tensor(
                    corr[0:24, sl0:sl0 + 512], csb[0:24, 0:512], -2.0,
                    csb[0:24, 512:1024], ALU.mult, ALU.add))
            sv_corr = P.cnt["sv"]
            sv_csb_read_last = sv_corr

            tw("sv", sv_corr)
            cx_of = {}
            asm_ev_group = {}
            for G in range(4):
                fpair = PAIRS_ASM[G]
                if G >= 2:
                    tw("sv", asm_ev_group[G - 2])
                for f in fpair:
                    b = f % 2
                    P.wait("sync", "stt", xck_freeMM[b])
                    P.wait("sync", "sv", xck_freeV[b])
                    dq = dx if f % 2 == 0 else dy
                    cx_of[f] = dq(lambda e, pp=pp, f=f, b=b:
                                  e.dma_start(out=xck[b][:], in_=d_x[pp, f]))
                    for tb in range(NTB):
                        u3 = f * 2 + tb
                        bank = ps[u3 % 8]
                        for kc in range(NKC):
                            dd = (f - kc) % 8
                            t_(lambda e, kc=kc, tb=tb, dd=dd, bank=bank: e.matmul(
                                bank[:, 0:512],
                                ball[:, dd * 128:(dd + 1) * 128],
                                xn[:, kc * 1024 + tb * 512: kc * 1024 + tb * 512 + 512],
                                start=(kc == 0), stop=False))
                        t_(lambda e, f=f, tb=tb, bank=bank: e.matmul(
                            bank[:, 0:512],
                            wlesb[:, f * 128:(f + 1) * 128],
                            corr[0:24, tb * 512:tb * 512 + 512],
                            start=False, stop=True))
                dc = drain()
                for f in fpair:
                    b = f % 2
                    for tb in range(NTB):
                        u3 = f * 2 + tb
                        bank = ps[u3 % 8]
                        h = u3 % 2
                        vw("stt", dc)
                        vw("ldx" if f % 2 == 0 else "ldy", cx_of[f])
                        P.wait("vector", "outs", outst_free[h])
                        aslice = acc[:, f * 1024 + tb * 512: f * 1024 + tb * 512 + 512]
                        vself()
                        v(lambda e, bank=bank, aslice=aslice: e.tensor_tensor(
                            tmpn[:], bank[:, 0:512], aslice, ALU.add))
                        vself()
                        so = v(lambda e, b=b, tb=tb, h=h: e.tensor_tensor(
                            outst[:, h * 512:(h + 1) * 512], tmpn[:],
                            xck[b][:, tb * 512:tb * 512 + 512], ALU.add))
                        P.wait("sync", "sv", so)
                        P.wait("sync", "outs", P.cnt["outs"])
                        oc = P.op("sync", lambda e, pp=pp, f=f, tb=tb, h=h: e.dma_start(
                            out=d_out[pp, f, :, tb * 512:(tb + 1) * 512],
                            in_=outst[:, h * 512:(h + 1) * 512]), "outs", 16)
                        outst_free[h] = oc
                    xck_freeV[b] = P.cnt["sv"]
                    xck_freeMM[b] = P.cnt["stt"]
                asm_ev_group[G] = P.cnt["sv"]
                if pp + 1 < n_passes:
                    P.wait("sync", "sv", asm_ev_group[G])
                    cpx = cpy = 0
                    for fx in fpair:
                        if fx % 2 == 0:
                            cpx = dx(lambda e, pp=pp, fx=fx: e.dma_start(
                                out=acc[:, fx * 1024:(fx + 1) * 1024],
                                in_=d_x[pp + 1, fx]))
                        else:
                            cpy = dy(lambda e, pp=pp, fx=fx: e.dma_start(
                                out=acc[:, fx * 1024:(fx + 1) * 1024],
                                in_=d_x[pp + 1, fx]))
                    xpair_done[(pp + 1, fpair)] = (cpx, cpy)
            if pp + 1 < n_passes:
                tw("sv", asm_ev_group[2])
                for pr in PAIRS_ASM[:3]:
                    emit_stats_pair(pp + 1, pr, False)
            stt_asm_done = P.cnt["stt"]
            sv_acc_last = P.cnt["sv"]

        P.wait("sync", "outs", P.cnt["outs"])

        with nc.Block() as block:
            P.replay(block)

    return nc



def _host_consts():
    f32 = np.float32
    def pmaj(a):
        return np.ascontiguousarray(a.reshape(8, 128).T)
    ka = pmaj((np.abs(np.fft.fftfreq(1024).astype(f32)) + f32(1e-8)).astype(f32))
    fre = pmaj(np.frombuffer(base64.b64decode(_FRE_B64), dtype="<f4").copy())
    fim = pmaj(np.frombuffer(base64.b64decode(_FIM_B64), dtype="<f4").copy())
    k = np.arange(1024, dtype=np.int64)
    kk, nn = np.meshgrid(k, k, indexing="ij")
    theta = (2.0 * np.pi / 1024.0) * ((kk * nn) % 1024).astype(np.float64)
    c = np.cos(theta)
    s = np.sin(theta)

    def split(a):
        hi = a.astype(f32)
        lo = (a - hi.astype(np.float64)).astype(f32)
        return hi, lo

    chi, clo = split(c)
    shi, slo = split(s)

    def tile_k(a):
        return np.ascontiguousarray(a.reshape(8, 128, 1024).transpose(1, 0, 2).reshape(128, 8192))

    return dict(ka=ka, fre=fre, fim=fim,
                cosh=tile_k(chi), cosl=tile_k(clo),
                sinh=tile_k(shi), sinl=tile_k(slo),
                ident=np.eye(128, dtype=f32))


def _prep(inputs):
    f32 = np.float32
    x = np.ascontiguousarray(inputs["L0_attn_x"], dtype=f32).reshape(-1, 1024)
    gp = np.ascontiguousarray(inputs["L0_ffn_gate_proj_w"], dtype=f32)
    up = np.ascontiguousarray(inputs["L0_ffn_up_proj_w"], dtype=f32)
    dp = np.ascontiguousarray(inputs["L0_ffn_down_proj_w"], dtype=f32)
    W_el = np.ascontiguousarray(inputs["L0_attn_W_el"], dtype=f32)
    b_el = np.ascontiguousarray(inputs["L0_attn_b_el"], dtype=f32)
    W_le = np.ascontiguousarray(inputs["L0_attn_W_le"], dtype=f32)
    b_le = np.ascontiguousarray(inputs["L0_attn_b_le"], dtype=f32)
    gw = np.ascontiguousarray(inputs["L0_ffn_gate_w"], dtype=f32)

    g7 = gp.reshape(8, 8, 128, 4, 4, 128)
    u7 = up.reshape(8, 8, 128, 4, 4, 128)
    wgu = np.empty((8, 4, 128, 2, 4, 8, 128), f32)
    wgu[:, :, :, 0] = g7.transpose(0, 3, 2, 4, 1, 5)
    wgu[:, :, :, 1] = u7.transpose(0, 3, 2, 4, 1, 5)
    wgu = np.ascontiguousarray(wgu.reshape(8, 4, 128, 8192))

    d5 = dp.reshape(8, 16, 128, 8, 128)
    wd = np.ascontiguousarray(d5.transpose(0, 3, 2, 1, 4).reshape(8, 8, 128, 2048))

    wel = np.ascontiguousarray(W_el.reshape(8, 128, 24).transpose(1, 0, 2).reshape(128, 192))
    gwt = np.ascontiguousarray(gw.reshape(8, 128, 8).transpose(1, 0, 2).reshape(128, 64))
    ble = np.ascontiguousarray(b_le.reshape(8, 128).T)
    bel = np.ascontiguousarray(b_el.reshape(24, 1))
    _A = np.array([[1,1,1,0,1,1,1,0,0,0,1,1],[1,1,0,1,1,1,0,0,0,1,1,0],
                   [1,0,1,1,1,0,0,0,1,1,0,1],[0,1,1,1,0,0,0,1,1,0,1,1],
                   [0,0,0,0,1,1,1,1,1,1,1,1],[1,1,1,1,0,0,0,0,0,0,0,0],
                   [0,0,0,0,1,1,1,1,0,0,0,0],[1,1,0,0,1,1,0,0,1,1,0,0],
                   [1,0,1,0,1,0,1,0,1,0,1,0],[0,1,0,1,0,1,0,1,0,1,0,1],
                   [1,0,0,1,1,0,0,1,1,0,0,1],[0,1,1,0,0,1,1,0,0,1,1,0]], dtype=f32)
    H = np.concatenate([_A.T, np.eye(12, dtype=f32)], axis=1)
    ht = np.ascontiguousarray(H.T)
    xrow = np.ascontiguousarray(x[0:1, :])

    selm = np.zeros((8, 1024), np.float32)
    for e_ in range(8):
        selm[e_, e_ * 128:(e_ + 1) * 128] = 1.0
    consts = _host_consts()
    shared = dict(wgu=wgu, wd=wd, wel=wel, wle=W_le, gw=gwt, ble=ble, bel=bel,
                  ht=ht, xrow=xrow, sel_in=selm, **consts)

    in_maps = []
    for c in range(NCORE):
        xc = x[c * TCORE:(c + 1) * TCORE]
        xfm = np.ascontiguousarray(
            xc.reshape(NPASS, TPASS, 8, 128).transpose(0, 2, 3, 1))
        m = dict(shared)
        m["x_fm"] = xfm
        in_maps.append(m)
    return in_maps


def kernel(**inputs):
    in_maps = _prep(inputs)
    nc = _build_program()
    res = run_bass_kernel_spmd(nc, in_maps, core_ids=list(range(NCORE)))
    outs = []
    for c in range(NCORE):
        o = np.asarray(res.results[c]["out"])
        outs.append(o.transpose(0, 3, 1, 2).reshape(TCORE, 1024))
    return np.concatenate(outs, axis=0).reshape(4, 8192, 1024).astype(np.float32)


# version 21
# speedup vs baseline: 1.0021x; 1.0021x over previous
import base64
import contextlib
import numpy as np
import ml_dtypes

import concourse.bass as bass
import concourse.mybir as mybir
from concourse.bass_utils import run_bass_kernel_spmd

F32 = mybir.dt.float32
BF16 = mybir.dt.bfloat16
AF = mybir.ActivationFunctionType
ALU = mybir.AluOpType
AX = mybir.AxisListType

NCORE = 8
TCORE = 4096
TPASS = 1024
NPASS = TCORE // TPASS
NTB = 2
NE = 8
NM = 16
NKC = 8
NF = 8
NGRP = 4

_FRE_B64 = "JuusPolzrT7l+60+N4SuPn4Mrz6+lK8+9BywPiKlsD5ELbE+W7WxPmo9sj5sxbI+ZU2zPlLVsz4yXbQ+COW0PtFstT6Q9LU+Qny2PuUDtz59i7c+BhO4PoWauD71Ibk+Vam5Pqgwuj7tt7o+JD+7PkzGuz5iTbw+bdS8PmVbvT5R4r0+Kmm+PvPvvj6sdr8+U/2/PuyDwD5xCsE+5JDBPkgXwj6XncI+1yPDPgKqwz4aMMQ+IbbEPhM8xT70wcU+wEfGPnbNxj4cU8c+q9jHPiZeyD6L48g+3GjJPhfuyT49c8o+TvjKPkl9yz4rAsw++YbMPq8LzT5QkM0+2RTOPkiZzj6iHc8+46HPPg0m0D4fqtA+FS7RPvWx0T65NdI+aLnSPvs80z5zwNM+00PUPhjH1D5EStU+U83VPkdQ1j4h09Y+31XXPoLY1z4JW9g+ct3YPsJf2T7z4dk+CGTaPv/l2j7ZZ9s+lunbPjVr3D637Nw+G27dPl7v3T6FcN4+i/HePnRy3z49898+5HPgPm704D7WdOE+IfXhPkp14j5P9eI+N3XjPvv04z6hdOQ+JPTkPoNz5T7C8uU+3nHmPtrw5j6xb+c+Y+7nPvVs6D5h6+g+rWnpPtLn6T7UZeo+suPqPmth6z4A3+s+cFzsPrnZ7D7eVu0+3NPtPrhQ7j5rze4+9knvPl3G7z6cQvA+tr7wPqk68T5xtvE+FTLyPo+t8j7kKPM+DaTzPg4f9D7pmfQ+mBT1PiGP9T5/CfY+soP2Pr399j6dd/c+VfH3PuBq+D5B5Pg+eV35PoPW+T5kT/o+Gsj6PqFA+z7/uPs+MDH8PjWp/D4PIf0+uJj9PjgQ/j6Jh/4+rv7+PqV1/z5q7P8+gzEAP7lsAD/ZpwA/4eIAP9AdAT+pWAE/apMBPxTOAT+lCAI/HkMCP4B9Aj/JtwI/+vECPxQsAz8UZgM//J8DP8rZAz+CEwQ/H00EP6OGBD8OwAQ/YfkEP5syBT+6awU/wKQFP63dBT+AFgY/Ok8GP9mHBj9ewAY/yvgGPxsxBz9SaQc/b6EHP3HZBz9YEQg/JUkIP9mACD9wuAg/6+8IP00nCT+TXgk/v5UJP87MCT/CAwo/mzoKP1hxCj/5pwo/ft4KP+gUCz82Sws/Z4ELP323Cz927Qs/UiMMPxNZDD+2jgw/PsQMP6j5DD/0Lg0/JGQNPziZDT8vzg0/CAMOP8I3Dj9hbA4/4aAOP0XVDj+KCQ8/sT0PP7txDz+mpQ8/ddkPPyQNED+0QBA/KHQQP3ynED+y2hA/yQ0RP8JAET+ccxE/VqYRP/PYET9wCxI/zT0SPwxwEj8qohI/K9QSPwsGEz/LNxM/bWkTP+6aEz9PzBM/kf0TP7IuFD+0XxQ/lJAUP1XBFD/28RQ/dSIVP9VSFT8UgxU/MrMVPy/jFT8LExY/x0IWP2FyFj/aoRY/M9EWP2kAFz9/Lxc/cl4XP0SNFz/2uxc/g+oXP/EYGD88Rxg/ZnUYP22jGD9R0Rg/Ff8YP7QsGT8zWhk/j4cZP8i0GT/e4Rk/0g4aP6M7Gj9RaBo/3JQaP0XBGj+J7Ro/rBkbP6pFGz+GcRs/Pp0bP9HIGz9E9Bs/kB8cP7lKHD/AdRw/oKAcP1/LHD/59Rw/biAdP8BKHT/sdB0/9p4dP9rIHT+a8h0/NhweP6xFHj//bh4/LJgePzXBHj8Y6h4/1hIfP3E7Hz/lYx8/M4wfP120Hz9h3B8/QQQgP/orID+OUyA/+3ogP0SiID9oySA/ZPAgPzsXIT/sPSE/dmQhP9yKIT8asSE/MdchPyT9IT/uIiI/lEgiPxJuIj9pkyI/mrgiP6TdIj+IAiM/RCcjP9pLIz9IcCM/j5QjP7C4Iz+p3CM/egAkPyQkJD+nRyQ/A2skPzaOJD9CsSQ/JtQkP+T2JD94GSU/5TslPypeJT9HgCU/O6IlPwnEJT+u5SU/KgcmP30oJj+pSSY/rWomP4eLJj84rCY/w8wmPyPtJj9cDSc/ay0nP1FNJz8PbSc/oownPw+sJz9Ryyc/aeonP1oJKD8gKCg/vUYoPzJlKD97gyg/nKEoP5O/KD9i3Sg/BfsoP4AYKT/RNSk/9lIpP/RvKT/HjCk/b6kpP+7FKT9C4ik/bf4pP20aKj9DNio/7lEqP29tKj/GiCo/86MqP/S+Kj/M2So/d/QqP/kOKz9QKSs/fEMrP31dKz9Udys/AJErP3+qKz/Vwys/AN0rP/31Kz/SDiw/eicsP/g/LD9JWCw/cHAsP2uILD87oCw/3rcsP1jPLD+k5iw/xf0sP7sULT+DKy0/IUItP5NYLT/abi0/84QtP+GaLT+jsC0/OMYtP6PbLT/h8C0/8QUuP9caLj+PLy4/G0QuP3xYLj+ubC4/toAuP5CULj8+qC4/wLsuPxTPLj894i4/N/UuPwYILz+oGi8/HC0vP2U/Lz9+US8/bWMvPy51Lz/Chi8/KZgvP2KpLz9wui8/T8svPwHcLz+G7C8/3PwvPwgNMD8EHTA/1CwwP3Y8MD/pSzA/MlswP0tqMD82eTA/9YcwP4WWMD/opDA/HrMwPyTBMD/+zjA/qtwwPyjqMD949zA/mQQxP44RMT9UHjE/7CoxP1Y3MT+RQzE/oE8xP39bMT8xZzE/tHIxPwl+MT8wiTE/KJQxP/OeMT+PqTE//LMxPzu+MT9LyDE/LtIxP+HbMT9m5TE/vu4xP+T3MT/eADI/5PcxP77uMT9m5TE/4dsxPy7SMT9LyDE/O74xP/yzMT+PqTE/854xPyiUMT8wiTE/CX4xP7RyMT8xZzE/f1sxP6BPMT+RQzE/VjcxP+wqMT9UHjE/jhExP5kEMT949zA/KOowP6rcMD/+zjA/JMEwPx6zMD/opDA/hZYwP/WHMD82eTA/S2owPzJbMD/pSzA/djwwP9QsMD8EHTA/CA0wP9z8Lz+G7C8/AdwvP0/LLz9wui8/YqkvPymYLz/Chi8/LnUvP21jLz9+US8/ZT8vPxwtLz+oGi8/BggvPzf1Lj894i4/FM8uP8C7Lj8+qC4/kJQuP7aALj+ubC4/fFguPxtELj+PLy4/1xouP/EFLj/h8C0/o9stPzjGLT+jsC0/4ZotP/OELT/abi0/k1gtPyFCLT+DKy0/uxQtP8X9LD+k5iw/WM8sP963LD87oCw/a4gsP3BwLD9JWCw/+D8sP3onLD/SDiw//fUrPwDdKz/Vwys/f6orPwCRKz9Udys/fV0rP3xDKz9QKSs/+Q4rP3f0Kj/M2So/9L4qP/OjKj/GiCo/b20qP+5RKj9DNio/bRoqP23+KT9C4ik/7sUpP2+pKT/HjCk/9G8pP/ZSKT/RNSk/gBgpPwX7KD9i3Sg/k78oP5yhKD97gyg/MmUoP71GKD8gKCg/WgkoP2nqJz9Ryyc/D6wnP6KMJz8PbSc/UU0nP2stJz9cDSc/I+0mP8PMJj84rCY/h4smP61qJj+pSSY/fSgmPyoHJj+u5SU/CcQlPzuiJT9HgCU/Kl4lP+U7JT94GSU/5PYkPybUJD9CsSQ/No4kPwNrJD+nRyQ/JCQkP3oAJD+p3CM/sLgjP4+UIz9IcCM/2ksjP0QnIz+IAiM/pN0iP5q4Ij9pkyI/Em4iP5RIIj/uIiI/JP0hPzHXIT8asSE/3IohP3ZkIT/sPSE/OxchP2TwID9oySA/RKIgP/t6ID+OUyA/+isgP0EEID9h3B8/XbQfPzOMHz/lYx8/cTsfP9YSHz8Y6h4/NcEePyyYHj//bh4/rEUePzYcHj+a8h0/2sgdP/aeHT/sdB0/wEodP24gHT/59Rw/X8scP6CgHD/AdRw/uUocP5AfHD9E9Bs/0cgbPz6dGz+GcRs/qkUbP6wZGz+J7Ro/RcEaP9yUGj9RaBo/ozsaP9IOGj/e4Rk/yLQZP4+HGT8zWhk/tCwZPxX/GD9R0Rg/baMYP2Z1GD88Rxg/8RgYP4PqFz/2uxc/RI0XP3JeFz9/Lxc/aQAXPzPRFj/aoRY/YXIWP8dCFj8LExY/L+MVPzKzFT8UgxU/1VIVP3UiFT/28RQ/VcEUP5SQFD+0XxQ/si4UP5H9Ez9PzBM/7poTP21pEz/LNxM/CwYTPyvUEj8qohI/DHASP809Ej9wCxI/89gRP1amET+ccxE/wkARP8kNET+y2hA/fKcQPyh0ED+0QBA/JA0QP3XZDz+mpQ8/u3EPP7E9Dz+KCQ8/RdUOP+GgDj9hbA4/wjcOPwgDDj8vzg0/OJkNPyRkDT/0Lg0/qPkMPz7EDD+2jgw/E1kMP1IjDD927Qs/fbcLP2eBCz82Sws/6BQLP37eCj/5pwo/WHEKP5s6Cj/CAwo/zswJP7+VCT+TXgk/TScJP+vvCD9wuAg/2YAIPyVJCD9YEQg/cdkHP2+hBz9SaQc/GzEHP8r4Bj9ewAY/2YcGPzpPBj+AFgY/rd0FP8CkBT+6awU/mzIFP2H5BD8OwAQ/o4YEPx9NBD+CEwQ/ytkDP/yfAz8UZgM/FCwDP/rxAj/JtwI/gH0CPx5DAj+lCAI/FM4BP2qTAT+pWAE/0B0BP+HiAD/ZpwA/uWwAP4MxAD9q7P8+pXX/Pq7+/j6Jh/4+OBD+PriY/T4PIf0+Nan8PjAx/D7/uPs+oUD7PhrI+j5kT/o+g9b5Pnld+T5B5Pg+4Gr4PlXx9z6dd/c+vf32PrKD9j5/CfY+IY/1PpgU9T7pmfQ+Dh/0Pg2k8z7kKPM+j63yPhUy8j5xtvE+qTrxPra+8D6cQvA+XcbvPvZJ7z5rze4+uFDuPtzT7T7eVu0+udnsPnBc7D4A3+s+a2HrPrLj6j7UZeo+0ufpPq1p6T5h6+g+9WzoPmPu5z6xb+c+2vDmPt5x5j7C8uU+g3PlPiT05D6hdOQ++/TjPjd14z5P9eI+SnXiPiH14T7WdOE+bvTgPuRz4D49898+dHLfPovx3j6FcN4+Xu/dPhtu3T637Nw+NWvcPpbp2z7ZZ9s+/+XaPghk2j7z4dk+wl/ZPnLd2D4JW9g+gtjXPt9V1z4h09Y+R1DWPlPN1T5EStU+GMfUPtND1D5zwNM++zzTPmi50j65NdI+9bHRPhUu0T4fqtA+DSbQPuOhzz6iHc8+SJnOPtkUzj5QkM0+rwvNPvmGzD4rAsw+SX3LPk74yj49c8o+F+7JPtxoyT6L48g+Jl7IPqvYxz4cU8c+ds3GPsBHxj70wcU+EzzFPiG2xD4aMMQ+AqrDPtcjwz6XncI+SBfCPuSQwT5xCsE+7IPAPlP9vz6sdr8+8+++Pippvj5R4r0+ZVu9Pm3UvD5iTbw+TMa7PiQ/uz7tt7o+qDC6PlWpuT71Ibk+hZq4PgYTuD59i7c+5QO3PkJ8tj6Q9LU+0Wy1PgjltD4yXbQ+UtWzPmVNsz5sxbI+aj2yPlu1sT5ELbE+IqWwPvQcsD6+lK8+fgyvPjeErj7l+60+iXOtPg=="
_FIM_B64 = "62JeP6FtXj8keF4/dYJeP5OMXj9/ll4/OaBeP8CpXj8Us14/OLxePyjFXj/nzV4/c9ZeP8zeXj/z5l4/6e5eP6z2Xj88/l4/mwVfP8cMXz/CE18/iRpfPx4hXz+DJ18/tC1fP7MzXz+COV8/HD9fP4ZEXz+9SV8/w05fP5ZTXz84WF8/qFxfP+VgXz/wZF8/y2hfP3JsXz/pb18/LHNfPz92Xz8geV8/zntfP0t+Xz+XgF8/sIJfP5iEXz9Ohl8/1IdfPyeJXz9Jil8/OYtfP/aLXz+DjF8/34xfPwiNXz8CjV8/yYxfP16MXz/Di18/9opfP/iJXz/HiF8/Z4dfP9WFXz8ShF8/HYJfP/h/Xz+gfV8/GXtfP2B4Xz92dV8/W3JfPw9vXz+Ta18/5GdfPwdkXz/3X18/tltfP0VXXz+jUl8/0U1fP81IXz+ZQ18/ND5fP6A4Xz/bMl8/4yxfP70mXz9mIF8/3hlfPycTXz8+DF8/JQVfP9z9Xj9k9l4/uu5eP9/mXj/W3l4/nNZePzLOXj+ZxV4/z7xeP9WzXj+rql4/UaFeP8iXXj8Ojl4/JYRePwx6Xj/Eb14/TGVeP6RaXj/NT14/xUReP485Xj8pLl4/lCJeP88WXj/cCl4/uf5dP2fyXT/l5V0/NNldP1XMXT9Gv10/CLJdP5ykXT//ll0/NYldPzt7XT8TbV0/u15dPzZQXT+CQV0/nzJdP40jXT9OFF0/3gRdP0L1XD935Vw/fdVcP1bFXD//tFw/fKRcP8iTXD/oglw/2XFcP5xgXD8zT1w/mj1cP9QrXD/gGVw/vgdcP2/1Wz/y4ls/R9BbP269Wz9qqls/NZdbP9aDWz9IcFs/jVxbP6RIWz+PNFs/TSBbP90LWz9A91o/duJaP4DNWj9duFo/DaNaP5CNWj/md1o/EGJaPw5MWj/eNVo/gx9aP/sIWj9H8lk/ZttZP1nEWT8frVk/upVZPyp+WT9sZlk/hE5ZP242WT8tHlk/wAVZPyjtWD9l1Fg/dbtYP1miWD8TiVg/oW9YPwRWWD87PFg/RyJYPykIWD/e7Vc/adNXP8m4Vz/+nVc/CYNXP+hnVz+cTFc/JjFXP4UVVz+7+VY/xN1WP6XBVj9apVY/5YhWP0ZsVj9+T1Y/izJWP20VVj8m+FU/tdpVPxq9VT9Vn1U/Z4FVP09jVT8PRVU/oyZVPw8IVT9Q6VQ/aspUP1mrVD8gjFQ/vmxUPzJNVD99LVQ/oA1UP5rtUz9rzVM/E61TP5OMUz/qa1M/GEtTPx8qUz/8CFM/sudSP0HGUj+mpFI/5IJSP/hgUj/mPlI/rBxSP0r6UT/A11E/DrVRPzWSUT81b1E/DUxRP74oUT9HBVE/quFQP+W9UD/6mVA/53VQP61RUD9MLVA/xQhQPxfkTz9Dv08/SJpPPyd1Tz/gT08/cipPP90ETz8i304/QrlOPzyTTj8QbU4/vUZOP0UgTj+n+U0/5dJNP/yrTT/uhE0/ul1NP2E2TT/kDk0/QOdMP3m/TD+Kl0w/em9MP0JHTD/mHkw/Z/ZLP8HNSz/3pEs/CnxLP/lSSz/CKUs/ZgBLP+jWSj9GrUo/f4NKP5RZSj+GL0o/VAVKP/7aST+FsEk/6YVJPyhbST9FMEk/PwVJPxXaSD/Jrkg/WINIP8ZXSD8RLEg/OQBIPz7URz8gqEc/4HtHP39PRz/7Ikc/VfZGP4vJRj+hnEY/lG9GP2ZCRj8WFUY/o+dFPw+6RT9ajEU/g15FP4swRT9zAkU/N9REP9ylRD9fd0Q/wkhEPwMaRD8k60M/JbxDPwWNQz/EXUM/YS5DP+D+Qj8+z0I/fZ9CP5pvQj+YP0I/dg9CPzXfQT/TrkE/Un5BP7FNQT/xHEE/EuxAPxK7QD/2iUA/uFhAP1wnQD/i9T8/SMQ/P5CSPz+5YD8/wi4/P6/8Pj99yj4/K5g+P7xlPj8wMz4/hAA+P7vNPT/Vmj0/0Wc9P640PT9vAT0/Ec48P5eaPD//Zjw/SjM8P3j/Oz+Jyzs/fZc7P1NjOz8OLzs/rPo6Py7GOj+SkTo/2lw6PwYoOj8Y8zk/C745P+OIOT+fUzk/Px45P8XoOD8tszg/en04P6xHOD/DETg/v9s3P56lNz9lbzc/Dzk3P50CNz8TzDY/bZU2P61eNj/RJzY/3PA1P8y5NT+igjU/Xks1PwAUNT+J3DQ/96Q0P0xtND+HNTQ/qP0zP7DFMz+gjTM/dVUzPzEdMz/U5DI/XqwyP9FzMj8pOzI/aQIyP5DJMT+gkDE/llcxP3UeMT885TA/6KswP39yMD/+ODA/ZP8vP7PFLz/qiy8/C1IvPxMYLz8E3i4/36MuP6FpLj9NLy4/4vQtP2C6LT/Ify0/GEUtP1IKLT93zyw/hZQsP3xZLD9dHiw/KOMrP96nKz9+bCs/CDErP3v1Kj/auSo/JX4qP1hCKj93Bio/gMopP3WOKT9VUik/IBYpP9bZKD93nSg/BGEoP34kKD/i5yc/MqsnP25uJz+VMSc/qvQmP6u3Jj+XeiY/cD0mPzYAJj/owiU/hoUlPxJIJT+KCiU/78wkP0KPJD+CUSQ/rxMkP8nVIz/RlyM/xlkjP6obIz963SI/OZ8iP+VgIj+BIiI/CeQhP4KlIT/nZiE/OyghP37pID+wqiA/0GsgP98sID/e7R8/zK4fP6hvHz90MB8/L/EeP9qxHj92ch4/ADMeP3nzHT/jsx0/efMdPwAzHj92ch4/2rEePy/xHj90MB8/qG8fP8yuHz/e7R8/3ywgP9BrID+wqiA/fukgPzsoIT/nZiE/gqUhPwnkIT+BIiI/5WAiPzmfIj963SI/qhsjP8ZZIz/RlyM/ydUjP68TJD+CUSQ/Qo8kP+/MJD+KCiU/EkglP4aFJT/owiU/NgAmP3A9Jj+XeiY/q7cmP6r0Jj+VMSc/bm4nPzKrJz/i5yc/fiQoPwRhKD93nSg/1tkoPyAWKT9VUik/dY4pP4DKKT93Bio/WEIqPyV+Kj/auSo/e/UqPwgxKz9+bCs/3qcrPyjjKz9dHiw/fFksP4WULD93zyw/UgotPxhFLT/Ify0/YLotP+L0LT9NLy4/oWkuP9+jLj8E3i4/ExgvPwtSLz/qiy8/s8UvP2T/Lz/+ODA/f3IwP+irMD885TA/dR4xP5ZXMT+gkDE/kMkxP2kCMj8pOzI/0XMyP16sMj/U5DI/MR0zP3VVMz+gjTM/sMUzP6j9Mz+HNTQ/TG00P/ekND+J3DQ/ABQ1P15LNT+igjU/zLk1P9zwNT/RJzY/rV42P22VNj8TzDY/nQI3Pw85Nz9lbzc/nqU3P7/bNz/DETg/rEc4P3p9OD8tszg/xeg4Pz8eOT+fUzk/44g5Pwu+OT8Y8zk/Big6P9pcOj+SkTo/LsY6P6z6Oj8OLzs/U2M7P32XOz+Jyzs/eP87P0ozPD//Zjw/l5o8PxHOPD9vAT0/rjQ9P9FnPT/Vmj0/u809P4QAPj8wMz4/vGU+PyuYPj99yj4/r/w+P8IuPz+5YD8/kJI/P0jEPz/i9T8/XCdAP7hYQD/2iUA/ErtAPxLsQD/xHEE/sU1BP1J+QT/TrkE/Nd9BP3YPQj+YP0I/mm9CP32fQj8+z0I/4P5CP2EuQz/EXUM/BY1DPyW8Qz8k60M/AxpEP8JIRD9fd0Q/3KVEPzfURD9zAkU/izBFP4NeRT9ajEU/D7pFP6PnRT8WFUY/ZkJGP5RvRj+hnEY/i8lGP1X2Rj/7Ikc/f09HP+B7Rz8gqEc/PtRHPzkASD8RLEg/xldIP1iDSD/Jrkg/FdpIPz8FST9FMEk/KFtJP+mFST+FsEk//tpJP1QFSj+GL0o/lFlKP3+DSj9GrUo/6NZKP2YASz/CKUs/+VJLPwp8Sz/3pEs/wc1LP2f2Sz/mHkw/QkdMP3pvTD+Kl0w/eb9MP0DnTD/kDk0/YTZNP7pdTT/uhE0//KtNP+XSTT+n+U0/RSBOP71GTj8QbU4/PJNOP0K5Tj8i304/3QRPP3IqTz/gT08/J3VPP0iaTz9Dv08/F+RPP8UIUD9MLVA/rVFQP+d1UD/6mVA/5b1QP6rhUD9HBVE/vihRPw1MUT81b1E/NZJRPw61UT/A11E/SvpRP6wcUj/mPlI/+GBSP+SCUj+mpFI/QcZSP7LnUj/8CFM/HypTPxhLUz/qa1M/k4xTPxOtUz9rzVM/mu1TP6ANVD99LVQ/Mk1UP75sVD8gjFQ/WatUP2rKVD9Q6VQ/DwhVP6MmVT8PRVU/T2NVP2eBVT9Vn1U/Gr1VP7XaVT8m+FU/bRVWP4syVj9+T1Y/RmxWP+WIVj9apVY/pcFWP8TdVj+7+VY/hRVXPyYxVz+cTFc/6GdXPwmDVz/+nVc/ybhXP2nTVz/e7Vc/KQhYP0ciWD87PFg/BFZYP6FvWD8TiVg/WaJYP3W7WD9l1Fg/KO1YP8AFWT8tHlk/bjZZP4ROWT9sZlk/Kn5ZP7qVWT8frVk/WcRZP2bbWT9H8lk/+whaP4MfWj/eNVo/DkxaPxBiWj/md1o/kI1aPw2jWj9duFo/gM1aP3biWj9A91o/3QtbP00gWz+PNFs/pEhbP41cWz9IcFs/1oNbPzWXWz9qqls/br1bP0fQWz/y4ls/b/VbP74HXD/gGVw/1CtcP5o9XD8zT1w/nGBcP9lxXD/oglw/yJNcP3ykXD//tFw/VsVcP33VXD935Vw/QvVcP94EXT9OFF0/jSNdP58yXT+CQV0/NlBdP7teXT8TbV0/O3tdPzWJXT//ll0/nKRdPwiyXT9Gv10/VcxdPzTZXT/l5V0/Z/JdP7n+XT/cCl4/zxZeP5QiXj8pLl4/jzleP8VEXj/NT14/pFpeP0xlXj/Eb14/DHpePyWEXj8Ojl4/yJdeP1GhXj+rql4/1bNeP8+8Xj+ZxV4/Ms5eP5zWXj/W3l4/3+ZeP7ruXj9k9l4/3P1ePyUFXz8+DF8/JxNfP94ZXz9mIF8/vSZfP+MsXz/bMl8/oDhfPzQ+Xz+ZQ18/zUhfP9FNXz+jUl8/RVdfP7ZbXz/3X18/B2RfP+RnXz+Ta18/D29fP1tyXz92dV8/YHhfPxl7Xz+gfV8/+H9fPx2CXz8ShF8/1YVfP2eHXz/HiF8/+IlfP/aKXz/Di18/XoxfP8mMXz8CjV8/CI1fP9+MXz+DjF8/9otfPzmLXz9Jil8/J4lfP9SHXz9Ohl8/mIRfP7CCXz+XgF8/S35fP857Xz8geV8/P3ZfPyxzXz/pb18/cmxfP8toXz/wZF8/5WBfP6hcXz84WF8/llNfP8NOXz+9SV8/hkRfPxw/Xz+COV8/szNfP7QtXz+DJ18/HiFfP4kaXz/CE18/xwxfP5sFXz88/l4/rPZeP+nuXj/z5l4/zN5eP3PWXj/nzV4/KMVePzi8Xj8Us14/wKlePzmgXj9/ll4/k4xeP3WCXj8keF4/oW1ePw=="

_ENGS = ("sync", "gpsimd", "vector", "scalar", "tensor")


class Plan:
    def __init__(self, sems):
        self.streams = {e: [] for e in _ENGS}
        self.sems = sems
        self.cnt = {k: 0 for k in sems}

    def op(self, eng, fn, inc=None, k=1):
        if inc is None:
            self.streams[eng].append(lambda e, fn=fn: fn(e))
            return None
        self.cnt[inc] += k
        sem = self.sems[inc]
        self.streams[eng].append(lambda e, fn=fn, sem=sem, k=k: fn(e).then_inc(sem, k))
        return self.cnt[inc]

    def wait(self, eng, name, val):
        if not val:
            return
        sem = self.sems[name]
        self.streams[eng].append(lambda e, sem=sem, val=val: e.wait_ge(sem, val))

    def replay(self, block):
        for name in _ENGS:
            stream = self.streams[name]
            getattr(block, name)(lambda eng, stream=stream: [f(eng) for f in stream])


def _build_program(n_passes=NPASS, n_experts=NE):
    nc = bass.Bass("TRN2", target_bir_lowering=False)

    d_x = nc.dram_tensor("x_fm", [NPASS, NKC, 128, TPASS], F32, kind="ExternalInput").ap()
    d_xrow = nc.dram_tensor("xrow", [1, 1024], F32, kind="ExternalInput").ap()
    d_wgu = nc.dram_tensor("wgu", [NE, NGRP, 128, 8192], F32, kind="ExternalInput").ap()
    d_wd = nc.dram_tensor("wd", [NE, NF, 128, 2048], F32, kind="ExternalInput").ap()
    d_wel = nc.dram_tensor("wel", [128, 192], F32, kind="ExternalInput").ap()
    d_wle = nc.dram_tensor("wle", [24, 1024], F32, kind="ExternalInput").ap()
    d_gw = nc.dram_tensor("gw", [128, 64], F32, kind="ExternalInput").ap()
    d_ble = nc.dram_tensor("ble", [128, 8], F32, kind="ExternalInput").ap()
    d_bel = nc.dram_tensor("bel", [24, 1], F32, kind="ExternalInput").ap()
    d_ht = nc.dram_tensor("ht", [24, 12], F32, kind="ExternalInput").ap()
    d_ident = nc.dram_tensor("ident", [128, 128], F32, kind="ExternalInput").ap()
    d_sel = nc.dram_tensor("sel_in", [8, 1024], F32, kind="ExternalInput").ap()
    d_ka = nc.dram_tensor("ka", [128, 8], F32, kind="ExternalInput").ap()
    d_fre = nc.dram_tensor("fre", [128, 8], F32, kind="ExternalInput").ap()
    d_fim = nc.dram_tensor("fim", [128, 8], F32, kind="ExternalInput").ap()
    d_cosh = nc.dram_tensor("cosh", [128, 8192], F32, kind="ExternalInput").ap()
    d_cosl = nc.dram_tensor("cosl", [128, 8192], F32, kind="ExternalInput").ap()
    d_sinh = nc.dram_tensor("sinh", [128, 8192], F32, kind="ExternalInput").ap()
    d_sinl = nc.dram_tensor("sinl", [128, 8192], F32, kind="ExternalInput").ap()
    d_out = nc.dram_tensor("out", [NPASS, NF, 128, TPASS], F32, kind="ExternalOutput").ap()
    d_hext = nc.dram_tensor("hext", [2048], BF16, kind="Internal").ap()

    with contextlib.ExitStack() as stk:
        ent = stk.enter_context
        xn = ent(nc.sbuf_tensor("xn", [128, NKC * TPASS], BF16))
        zall = ent(nc.sbuf_tensor("zall", [128, NM * TPASS], BF16))
        acc = ent(nc.sbuf_tensor("acc", [128, NF * TPASS], F32))
        wgu0 = ent(nc.sbuf_tensor("wgu0", [128, 8192], BF16))
        wgu1 = ent(nc.sbuf_tensor("wgu1", [128, 8192], BF16))
        wdbs = [ent(nc.sbuf_tensor(f"wdb{i}", [128, 2048], BF16)) for i in range(4)]
        xck0 = ent(nc.sbuf_tensor("xck0", [128, 1024], F32))
        xck1 = ent(nc.sbuf_tensor("xck1", [128, 1024], F32))
        xsq = ent(nc.sbuf_tensor("xsq", [128, 2048], BF16))
        ball = ent(nc.sbuf_tensor("ball", [128, 1024], BF16))
        rowsb = ent(nc.sbuf_tensor("rowsb", [128, 1024], BF16))
        matrow = ent(nc.sbuf_tensor("matrow", [1, 4096], F32))
        rowA = ent(nc.sbuf_tensor("rowA", [1, 1024], F32))
        rowB = ent(nc.sbuf_tensor("rowB", [1, 1024], F32))
        msc = ent(nc.sbuf_tensor("msc", [1, 64], F32))
        fsc = ent(nc.sbuf_tensor("fsc", [128, 64], F32))
        fsc2 = ent(nc.sbuf_tensor("fsc2", [128, 64], F32))
        dbc = ent(nc.sbuf_tensor("dbc", [128, 16], F32))
        csb = ent(nc.sbuf_tensor("csb", [128, 2048], BF16))
        tmpn = ent(nc.sbuf_tensor("tmpn", [128, 512], F32))
        siluT = ent(nc.sbuf_tensor("siluT", [128, 512], F32))
        ucT = ent(nc.sbuf_tensor("ucT", [128, 512], F32))
        outst = ent(nc.sbuf_tensor("outst", [128, 1024], F32))
        cwsb = ent(nc.sbuf_tensor("cwsb", [128, 192], BF16))
        welsb = ent(nc.sbuf_tensor("welsb", [128, 192], BF16))
        gwbf = ent(nc.sbuf_tensor("gwbf", [128, 64], BF16))
        identf = ent(nc.sbuf_tensor("identf", [128, 128], F32))
        wlesb = ent(nc.sbuf_tensor("wlesb", [24, 1024], BF16))
        bits = ent(nc.sbuf_tensor("bits", [24, 1024], BF16))
        synb = ent(nc.sbuf_tensor("synb", [12, 1024], BF16))
        mfix = ent(nc.sbuf_tensor("mfix", [24, 1024], BF16))
        corr = ent(nc.sbuf_tensor("corr", [24, 1024], BF16))
        cmbf = ent(nc.sbuf_tensor("cmbf", [8, 1024], BF16))
        cmtm = ent(nc.sbuf_tensor("cmtm", [128, 64], F32))
        bleb = ent(nc.sbuf_tensor("bleb", [128, 8], F32))
        belb = ent(nc.sbuf_tensor("belb", [24, 1], F32))
        htb = ent(nc.sbuf_tensor("htb", [24, 12], BF16))
        sel = ent(nc.sbuf_tensor("sel", [8, 1024], BF16))
        onesf = ent(nc.sbuf_tensor("onesf", [1, 128], F32))
        onesfc = ent(nc.sbuf_tensor("onesfc", [128, 1], F32))
        onesbf = ent(nc.sbuf_tensor("onesbf", [1, 128], BF16))
        oneskbf = ent(nc.sbuf_tensor("oneskbf", [128, 1], BF16))
        ones12bf = ent(nc.sbuf_tensor("ones12bf", [12, 1], BF16))
        onecol1 = ent(nc.sbuf_tensor("onecol1", [1, 1], F32))
        cns = ent(nc.sbuf_tensor("cns", [128, 16], F32))
        lxrow = ent(nc.sbuf_tensor("lxrow", [1, 8], F32))

        ps = [ent(nc.psum_tensor(f"p{i}", [128, 512], F32)) for i in range(8)]

        sems = {}
        for name in ("ldx", "ldy", "ldw", "sv", "ss", "stt", "outs"):
            sems[name] = ent(nc.semaphore(name))
        P = Plan(sems)

        trig = acc[:]

        lx_np = np.log(np.array([2.0, 4.0, 8.0, 16.0, 32.0], np.float32))
        lxsum = float(lx_np.sum())
        den = np.float32(5.0) * np.float32((lx_np * lx_np).sum()) - np.float32(lx_np.sum()) ** 2
        neg_invden = float(-(np.float32(1.0) / den))

        def v(fn):
            return P.op("vector", fn, "sv")

        def s_(fn):
            return P.op("scalar", fn, "ss")

        def t_(fn):
            return P.op("tensor", fn, None)

        def drain():
            return P.op("tensor", lambda e: e.drain(), "stt")

        def vw(name, val):
            P.wait("vector", name, val)

        def sw(name, val):
            P.wait("scalar", name, val)

        def tw(name, val):
            P.wait("tensor", name, val)

        def vself():
            P.wait("vector", "sv", P.cnt["sv"])

        v(lambda e: e.memset(onesf[:], 1.0))
        v(lambda e: e.memset(onesfc[:], 1.0))
        v(lambda e: e.memset(onesbf[:], 1.0))
        v(lambda e: e.memset(oneskbf[:], 1.0))
        v(lambda e: e.memset(ones12bf[:], 1.0))
        v(lambda e: e.memset(onecol1[:], 1.0))
        v(lambda e: e.memset(cns[:, 0:1], float(np.float32(np.pi))))
        v(lambda e: e.memset(cns[:, 1:2], float(np.float32(np.pi / 2))))
        v(lambda e: e.memset(cns[:, 2:3], -1.0))
        v(lambda e: e.memset(cns[:, 3:4], 0.0))
        v(lambda e: e.memset(cns[:, 4:5], 1.0))
        v(lambda e: e.memset(mfix[:], 0.0))
        v(lambda e: e.memset(dbc[:, 1:2], 1.0))
        for i, lv in enumerate(lx_np):
            v(lambda e, i=i, lv=float(lv): e.memset(lxrow[0:1, i:i + 1], lv))
        sv_consts = P.cnt["sv"]

        def dx(fn):
            P.wait("sync", "ldx", P.cnt["ldx"])
            return P.op("sync", fn, "ldx", 16)

        def dy(fn):
            P.wait("sync", "ldy", P.cnt["ldy"])
            return P.op("sync", fn, "ldy", 16)

        def dg(fn):
            P.wait("gpsimd", "ldw", P.cnt["ldw"])
            return P.op("gpsimd", fn, "ldw", 16)

        dx(lambda e: e.dma_start(out=fsc[:, 0:8], in_=d_ka))
        dx(lambda e: e.dma_start(out=fsc[:, 16:24], in_=d_fre))
        dx(lambda e: e.dma_start(out=fsc[:, 24:32], in_=d_fim))
        dx(lambda e: e.dma_start(out=rowA[:], in_=d_xrow))
        dx(lambda e: e.dma_start(out=bleb[:], in_=d_ble))
        dx(lambda e: e.dma_start(out=belb[:], in_=d_bel))
        dx(lambda e: e.dma_start(out=identf[:], in_=d_ident))
        ldx_consts = P.cnt["ldx"]
        dg(lambda e: e.dma_start(out=welsb[:], in_=d_wel))
        dg(lambda e: e.dma_start(out=wlesb[:], in_=d_wle))
        dg(lambda e: e.dma_start(out=htb[:], in_=d_ht))
        dg(lambda e: e.dma_start(out=gwbf[:], in_=d_gw))
        dg(lambda e: e.dma_start(out=sel[:], in_=d_sel))
        ldw_consts = P.cnt["ldw"]

        vw("ldx", ldx_consts)
        vw("sv", sv_consts)
        v(lambda e: e.tensor_tensor(fsc[:, 8:16], fsc[:, 0:8], fsc[:, 0:8], ALU.mult))
        v(lambda e: e.reduce_sum(msc[0:1, 0:1], rowA[:], axis=AX.X))
        vself()
        v(lambda e: e.tensor_scalar(msc[0:1, 1:2], msc[0:1, 0:1], 1.0 / 1024.0, None, ALU.mult))
        vself()
        v(lambda e: e.tensor_scalar(rowA[:], rowA[:], msc[0:1, 1:2], None, ALU.subtract))
        vself()
        v(lambda e: e.tensor_tensor(rowB[:], rowA[:], rowA[:], ALU.mult))
        vself()
        v(lambda e: e.reduce_sum(msc[0:1, 2:3], rowB[:], axis=AX.X))
        vself()
        v(lambda e: e.tensor_scalar(msc[0:1, 3:4], msc[0:1, 2:3], 1.0 / 1024.0, None, ALU.mult))
        vself()
        v(lambda e: e.tensor_scalar(msc[0:1, 4:5], msc[0:1, 3:4], 1e-5, None, ALU.add))
        sw("sv", P.cnt["sv"])
        s_(lambda e: e.activation(msc[0:1, 5:6], msc[0:1, 4:5], AF.Sqrt))
        vw("ss", P.cnt["ss"])
        v(lambda e: e.reciprocal(msc[0:1, 6:7], msc[0:1, 5:6]))
        vself()
        v(lambda e: e.tensor_tensor(msc[0:1, 7:8], msc[0:1, 6:7], msc[0:1, 6:7], ALU.mult))
        vself()
        v(lambda e: e.tensor_tensor(msc[0:1, 8:9], msc[0:1, 4:5], msc[0:1, 7:8], ALU.mult))
        vself()
        v(lambda e: e.tensor_scalar(msc[0:1, 9:10], msc[0:1, 8:9], -0.5, 1.5, ALU.mult, ALU.add))
        vself()
        v(lambda e: e.tensor_tensor(msc[0:1, 10:11], msc[0:1, 6:7], msc[0:1, 9:10], ALU.mult))
        vself()
        v(lambda e: e.tensor_scalar(matrow[0:1, 2048:3072], rowA[:], msc[0:1, 10:11],
                                    None, ALU.mult))
        vself()
        for i, sc2 in enumerate((4, 16, 64, 256, 1024)):
            v(lambda e, sc2=sc2: e.reduce_sum(msc[0:1, 12:13],
                                              matrow[0:1, 2048:2048 + sc2], axis=AX.X))
            vself()
            v(lambda e, sc2=sc2: e.tensor_scalar(msc[0:1, 13:14], msc[0:1, 12:13],
                                                 1.0 / sc2, None, ALU.mult))
            vself()
            v(lambda e, sc2=sc2: e.tensor_scalar(rowB[0:1, 0:sc2],
                                                 matrow[0:1, 2048:2048 + sc2],
                                                 msc[0:1, 13:14], None, ALU.is_gt))
            vself()
            v(lambda e, sc2=sc2: e.reduce_sum(msc[0:1, 14:15], rowB[0:1, 0:sc2], axis=AX.X))
            vself()
            v(lambda e, i=i: e.tensor_scalar(msc[0:1, 32 + i:33 + i], msc[0:1, 14:15],
                                             1.0, None, ALU.max))
            vself()
        sw("sv", P.cnt["sv"])
        s_(lambda e: e.activation(msc[0:1, 40:45], msc[0:1, 32:37], AF.Ln))
        vw("ss", P.cnt["ss"])
        v(lambda e: e.tensor_tensor(msc[0:1, 48:53], msc[0:1, 40:45], lxrow[0:1, 0:5],
                                    ALU.mult))
        vself()
        v(lambda e: e.reduce_sum(msc[0:1, 16:17], msc[0:1, 48:53], axis=AX.X))
        v(lambda e: e.reduce_sum(msc[0:1, 17:18], msc[0:1, 40:45], axis=AX.X))
        vself()
        v(lambda e: e.tensor_scalar(msc[0:1, 18:19], msc[0:1, 16:17], 5.0, None, ALU.mult))
        v(lambda e: e.tensor_scalar(msc[0:1, 19:20], msc[0:1, 17:18], lxsum, None, ALU.mult))
        vself()
        v(lambda e: e.tensor_tensor(msc[0:1, 20:21], msc[0:1, 18:19], msc[0:1, 19:20],
                                    ALU.subtract))
        vself()
        v(lambda e: e.tensor_scalar(msc[0:1, 21:22], msc[0:1, 20:21], neg_invden,
                                    None, ALU.mult))
        vself()
        v(lambda e: e.tensor_scalar(msc[0:1, 30:31], msc[0:1, 21:22], 1.0, 2.0,
                                    ALU.max, ALU.min))
        sv_D = P.cnt["sv"]
        tw("sv", sv_D)
        t_(lambda e: e.matmul(ps[0][:, 0:1], onesf[:], msc[0:1, 30:31],
                              start=True, stop=True))
        stt_db = drain()
        vw("stt", stt_db)
        v(lambda e: e.tensor_copy(dbc[:, 0:1], ps[0][:, 0:1]))
        vself()

        def filt_pipeline(xi, out0):
            X = dbc[:, xi:xi + 1]
            c0 = 2 + 3 * xi
            v(lambda e: e.tensor_tensor(dbc[:, c0:c0 + 1], X, X, ALU.mult))
            vself()
            v(lambda e: e.tensor_scalar(dbc[:, c0:c0 + 1], dbc[:, c0:c0 + 1], 0.2,
                                        None, ALU.mult))
            vself()
            v(lambda e: e.tensor_tensor(dbc[:, c0:c0 + 1], X, dbc[:, c0:c0 + 1], ALU.add))
            v(lambda e: e.tensor_scalar(dbc[:, c0 + 1:c0 + 2], X, 0.5, 1.0,
                                        ALU.mult, ALU.add))
            v(lambda e: e.tensor_scalar(dbc[:, c0 + 2:c0 + 3], X, 0.1, None, ALU.mult))
            vself()
            v(lambda e: e.tensor_scalar(fsc[:, 32:40], fsc[:, 0:8], dbc[:, c0 + 1:c0 + 2],
                                        None, ALU.mult))
            vself()
            v(lambda e: e.tensor_scalar(fsc[:, 32:40], fsc[:, 32:40], dbc[:, c0:c0 + 1],
                                        None, ALU.add))
            sw("sv", P.cnt["sv"])
            s_(lambda e: e.activation(fsc[:, 40:48], fsc[:, 32:40], AF.Sin,
                                      bias=cns[:, 0:1], scale=cns[:, 2:3]))
            v(lambda e: e.tensor_scalar(fsc[:, 48:56], fsc[:, 8:16], dbc[:, c0 + 2:c0 + 3],
                                        None, ALU.mult))
            vself()
            v(lambda e: e.tensor_tensor(fsc[:, 48:56], fsc[:, 48:56], fsc[:, 0:8],
                                        ALU.subtract))
            vself()
            v(lambda e: e.tensor_scalar(fsc[:, 48:56], fsc[:, 48:56], dbc[:, c0:c0 + 1],
                                        None, ALU.add))
            sw("sv", P.cnt["sv"])
            s_(lambda e: e.activation(fsc[:, 56:64], fsc[:, 48:56], AF.Sin,
                                      bias=cns[:, 1:2], scale=cns[:, 2:3]))
            s_(lambda e: e.activation(fsc2[:, 48:56], fsc[:, 48:56], AF.Sin,
                                      bias=cns[:, 3:4], scale=cns[:, 4:5]))
            vw("ss", P.cnt["ss"])
            v(lambda e, out0=out0: e.tensor_tensor(fsc2[:, out0:out0 + 8],
                                                   fsc[:, 40:48], fsc[:, 56:64], ALU.mult))
            vself()
            v(lambda e, out0=out0: e.tensor_tensor(fsc2[:, out0 + 8:out0 + 16],
                                                   fsc[:, 40:48], fsc2[:, 48:56], ALU.mult))
            vself()

        filt_pipeline(0, 0)
        filt_pipeline(1, 16)
        v(lambda e: e.tensor_tensor(fsc2[:, 32:40], fsc2[:, 0:8], fsc2[:, 16:24],
                                    ALU.subtract))
        v(lambda e: e.tensor_tensor(fsc2[:, 40:48], fsc2[:, 8:16], fsc2[:, 24:32],
                                    ALU.subtract))
        vself()
        v(lambda e: e.tensor_tensor(fsc2[:, 32:40], fsc2[:, 32:40], fsc[:, 16:24], ALU.add))
        v(lambda e: e.tensor_tensor(fsc2[:, 40:48], fsc2[:, 40:48], fsc[:, 24:32], ALU.add))
        vself()
        v(lambda e: e.tensor_scalar(fsc2[:, 32:40], fsc2[:, 32:40], 1.0 / 1024.0,
                                    None, ALU.mult))
        v(lambda e: e.tensor_scalar(fsc2[:, 40:48], fsc2[:, 40:48], -1.0 / 1024.0,
                                    None, ALU.mult))
        sv_spec = P.cnt["sv"]

        half_free = {0: 0, 1: 0}
        first = [True, True]
        hparts = ((d_cosh, 32), (d_sinh, 40), (d_cosl, 32), (d_sinl, 40))
        tw("sv", sv_spec)
        stt_hd = 0
        for li, (dsrc, repc) in enumerate(hparts):
            for hf in range(2):
                P.wait("sync", "stt", half_free[hf])
                c = dx(lambda e, dsrc=dsrc, hf=hf: e.dma_start(
                    out=trig[:, hf * 4096:(hf + 1) * 4096],
                    in_=dsrc[:, hf * 4096:(hf + 1) * 4096]))
                tw("ldx", c)
                for oh in range(2):
                    for kc in range(4 * hf, 4 * hf + 4):
                        st = first[oh]
                        first[oh] = False
                        sp = (li == 3 and hf == 1 and kc == 7)
                        t_(lambda e, oh=oh, kc=kc, repc=repc, st=st, sp=sp:
                           e.matmul(ps[1 + oh][0:1, 0:512],
                                    fsc2[:, repc + kc:repc + kc + 1],
                                    trig[:, kc * 1024 + oh * 512: kc * 1024 + oh * 512 + 512],
                                    start=st, stop=sp))
                stt_hd = drain()
                half_free[hf] = stt_hd
        stt_h = stt_hd
        vw("stt", stt_h)
        v(lambda e: e.tensor_copy(rowB[0:1, 0:512], ps[1][0:1, 0:512]))
        v(lambda e: e.tensor_copy(rowB[0:1, 512:1024], ps[2][0:1, 0:512]))
        vself()
        v(lambda e: e.tensor_copy(rowsb[32:33, :], rowB[:]))
        sv_hbf = P.cnt["sv"]
        P.wait("sync", "sv", sv_hbf)
        dx(lambda e: e.dma_start(out=d_hext[0:1024], in_=rowsb[32:33, :]))
        dx(lambda e: e.dma_start(out=d_hext[1024:2048], in_=rowsb[32:33, :]))
        c_hext = P.cnt["ldx"]
        P.wait("sync", "ldx", c_hext)
        x0_load = dx(lambda e: e.dma_start(
            out=acc[:].rearrange("p (kc t) -> p kc t", kc=8),
            in_=d_x[0].transpose([1, 0, 2])))
        P.wait("sync", "ldx", P.cnt["ldx"])
        for p in range(128):
            ap_p = bass.AP(d_hext.tensor, 1024 - p, [[128, 8], [1, 128]])
            P.op("sync", lambda e, p=p, ap_p=ap_p: e.dma_start(
                out=ball[p:p + 1, :].rearrange("a (d j) -> a d j", d=8), in_=ap_p),
                "ldx", 16)
        ldx_ball = P.cnt["ldx"]

        sv_dpath = P.cnt["sv"]
        cw_emitted = [False]

        def emit_cw():
            cw_emitted[0] = True
            tw("ldx", ldx_ball)
            tw("ldw", ldw_consts)
            for kc in range(8):
                for fc in range(8):
                    d = (kc - fc) % 8
                    t_(lambda e, kc=kc, fc=fc, d=d:
                       e.matmul(ps[3][:, kc * 24:(kc + 1) * 24],
                                ball[:, d * 128:(d + 1) * 128],
                                welsb[:, fc * 24:(fc + 1) * 24],
                                start=(fc == 0), stop=(fc == 7)))
            stt_cw = drain()
            vw("stt", stt_cw)
            v(lambda e: e.tensor_copy(cwsb[:], ps[3][:, 0:192]))

        wgu_free = {0: 0, 1: 0}
        wd_free = {0: 0, 1: 0, 2: 0, 3: 0}
        wgu_loaded = {}
        wd_loaded = {}
        xck = (xck0, xck1)
        xck_freeMM = {0: 0, 1: 0}
        xck_freeV = {0: 0, 1: 0}
        outst_free = {0: 0, 1: 0}
        stt_asm_done = stt_h
        ss_cbc_prev = 0
        sv_cmbf = 0
        sv_acc_last = 0
        sv_csb_read_last = 0
        zdone = {}
        silu_ss = {}
        prev_z = [0]
        pb67_free = [0]
        pb_pair_free = {0: 0, 1: 0, 2: 0}
        xsq_free = [stt_h]
        stats_first = {}
        stats_last_drain = {}
        xpair_done = {}

        def emit_stats_pair(pp2, kcs, is_last):
            cx, cy = xpair_done[(pp2, kcs)]
            sw("ldx", cx)
            sw("ldy", cy)
            sw("stt", xsq_free[0])
            for kc in kcs:
                s_(lambda e, kc=kc: e.square(
                    xsq[:, (kc % 2) * 1024:(kc % 2) * 1024 + 1024],
                    acc[:, kc * 1024:(kc + 1) * 1024]))
            ssq = P.cnt["ss"]
            first = stats_first.setdefault(pp2, [True])
            tw("ldx", cx)
            tw("ldy", cy)
            for j, kc in enumerate(kcs):
                st = first[0] and j == 0
                sp = is_last and j == len(kcs) - 1
                for tb in range(NTB):
                    t_(lambda e, kc=kc, tb=tb, st=st, sp=sp:
                       e.matmul(ps[4 + tb][0:1, 0:512], onesfc[:],
                                acc[:, kc * 1024 + tb * 512: kc * 1024 + tb * 512 + 512],
                                start=st, stop=sp))
            tw("ss", ssq)
            for j, kc in enumerate(kcs):
                st = first[0] and j == 0
                sp = is_last and j == len(kcs) - 1
                for tb in range(NTB):
                    t_(lambda e, kc=kc, tb=tb, st=st, sp=sp:
                       e.matmul(ps[6 + tb][0:1, 0:512], oneskbf[:],
                                xsq[:, (kc % 2) * 1024 + tb * 512: (kc % 2) * 1024 + tb * 512 + 512],
                                start=st, stop=sp))
            first[0] = False
            ds = drain()
            xsq_free[0] = ds
            stats_last_drain[pp2] = ds
            return ds
        ss_cbc_prev = [0]
        ss_csb_of = {}
        csb_read_done = {}

        PAIRS_ASM = ((6, 7), (4, 5), (2, 3), (0, 1))
        for pp in range(n_passes):
            if pp == 0:
                for pr in PAIRS_ASM:
                    xpair_done[(0, pr)] = (x0_load, 0)
            if pp == 0:
                tw("sv", sv_dpath)
                sw("sv", sv_dpath)
                for i, pr in enumerate(((0, 1), (2, 3), (4, 5), (6, 7))):
                    emit_stats_pair(0, pr, i == 3)
            else:
                emit_stats_pair(pp, (0, 1), True)
            stt_stats = stats_last_drain[pp]
            cxl, cyl = xpair_done[(pp, (0, 1))]
            vw("stt", stt_stats)
            for tb in range(NTB):
                v(lambda e, tb=tb: e.tensor_scalar(matrow[0:1, tb * 512:tb * 512 + 512],
                                                   ps[4 + tb][0:1, 0:512],
                                                   1.0 / 1024.0, None, ALU.mult))
                v(lambda e, tb=tb: e.tensor_scalar(rowA[0:1, tb * 512:tb * 512 + 512],
                                                   ps[6 + tb][0:1, 0:512],
                                                   1.0 / 1024.0, None, ALU.mult))
            vself()
            v(lambda e: e.tensor_tensor(rowB[:], matrow[0:1, 0:1024],
                                        matrow[0:1, 0:1024], ALU.mult))
            vself()
            v(lambda e: e.tensor_tensor(rowB[:], rowA[:], rowB[:], ALU.subtract))
            vself()
            v(lambda e: e.tensor_scalar(rowB[:], rowB[:], 1e-5, None, ALU.add))
            sw("sv", P.cnt["sv"])
            s_(lambda e: e.activation(rowA[:], rowB[:], AF.Sqrt))
            vw("ss", P.cnt["ss"])
            v(lambda e: e.reciprocal(matrow[0:1, 1024:2048], rowA[:]))
            vself()
            v(lambda e: e.tensor_tensor(rowA[:], matrow[0:1, 1024:2048],
                                        matrow[0:1, 1024:2048], ALU.mult))
            vself()
            v(lambda e: e.tensor_tensor(rowA[:], rowB[:], rowA[:], ALU.mult))
            vself()
            v(lambda e: e.tensor_scalar(rowA[:], rowA[:], -0.5, 1.5, ALU.mult, ALU.add))
            vself()
            v(lambda e: e.tensor_tensor(matrow[0:1, 1024:2048], matrow[0:1, 1024:2048],
                                        rowA[:], ALU.mult))
            sv_stats = P.cnt["sv"]

            tw("sv", sv_stats)
            tw("stt", stt_asm_done)
            for tb in range(NTB):
                t_(lambda e, tb=tb: e.matmul(ps[tb][:, 0:512], onesf[:],
                                             matrow[0:1, tb * 512:tb * 512 + 512],
                                             start=True, stop=True))
                t_(lambda e, tb=tb: e.matmul(ps[2 + tb][:, 0:512], onesf[:],
                                             matrow[0:1, 1024 + tb * 512:1024 + tb * 512 + 512],
                                             start=True, stop=True))
            stt_bc = drain()
            sw("stt", stt_bc)
            sw("sv", xck_freeV[0])
            sw("sv", xck_freeV[1])
            s_(lambda e: e.copy(xck0[:, 0:512], ps[0][:, 0:512]))
            s_(lambda e: e.copy(xck0[:, 512:1024], ps[1][:, 0:512]))
            s_(lambda e: e.copy(xck1[:, 0:512], ps[2][:, 0:512]))
            s_(lambda e: e.copy(xck1[:, 512:1024], ps[3][:, 0:512]))
            ss_mb = P.cnt["ss"]

            vw("ss", ss_mb)
            vw("ldx", cxl)
            vw("ldy", cyl)
            vw("stt", stt_asm_done)
            xn_chunk = {}
            for kc in range(NKC):
                vself()
                v(lambda e, kc=kc: e.tensor_tensor(
                    xn[:, kc * 1024:(kc + 1) * 1024],
                    acc[:, kc * 1024:(kc + 1) * 1024], xck0[:], ALU.subtract))
                vself()
                xn_chunk[kc] = v(lambda e, kc=kc: e.tensor_tensor(
                    xn[:, kc * 1024:(kc + 1) * 1024],
                    xn[:, kc * 1024:(kc + 1) * 1024], xck1[:], ALU.mult))
            sv_xn = P.cnt["sv"]
            xck_freeV[0] = sv_xn
            xck_freeV[1] = sv_xn
            tw("ss", ss_mb)
            if pp == 0:
                tw("ldw", ldw_consts)
            for kc in range(NKC):
                tw("sv", xn_chunk[kc])
                for tile in range(8):
                    t_(lambda e, kc=kc, tile=tile: e.matmul(
                        ps[tile][:, 0:8],
                        xn[:, kc * 1024 + tile * 128: kc * 1024 + (tile + 1) * 128],
                        gwbf[:, kc * 8:(kc + 1) * 8],
                        start=(kc == 0), stop=(kc == 7)))
            stt_lg = drain()
            acc_freeMM = stt_lg

            vw("stt", stt_lg)
            for tile in range(8):
                v(lambda e, tile=tile: e.tensor_copy(cmtm[:, tile * 8:(tile + 1) * 8],
                                                     ps[tile][:, 0:8]))
            sv_cmtm = P.cnt["sv"]
            vself()
            lgv = cmtm[:, 0:64]
            lg3 = lgv.rearrange("p (t j) -> p t j", t=8)
            a1s = tmpn[:, 0:8]
            a2s = tmpn[:, 8:16]
            w1c = tmpn[:, 16:24]
            w2c = tmpn[:, 24:32]
            d12 = tmpn[:, 32:40]
            m1v = tmpn[:, 64:128]
            csA = tmpn[:, 128:192]
            csB = tmpn[:, 192:256]
            f1v = tmpn[:, 256:320]
            mskd = tmpn[:, 320:384]
            f2v = tmpn[:, 384:448]

            def t3(ap):
                return ap.rearrange("p (t j) -> p t j", t=8)

            def bcast8(col8):
                return col8[:, :, None].broadcast_to([128, 8, 8])

            def cumsum_tile(src_v, dstA, dstB):
                v(lambda e: e.tensor_copy(dstA, src_v))
                vself()
                v(lambda e: e.tensor_copy(t3(dstB)[:, :, 0:1], t3(dstA)[:, :, 0:1]))
                v(lambda e: e.tensor_tensor(t3(dstB)[:, :, 1:8], t3(dstA)[:, :, 1:8],
                                            t3(dstA)[:, :, 0:7], ALU.add))
                vself()
                v(lambda e: e.tensor_copy(t3(dstA)[:, :, 0:2], t3(dstB)[:, :, 0:2]))
                v(lambda e: e.tensor_tensor(t3(dstA)[:, :, 2:8], t3(dstB)[:, :, 2:8],
                                            t3(dstB)[:, :, 0:6], ALU.add))
                vself()
                v(lambda e: e.tensor_copy(t3(dstB)[:, :, 0:4], t3(dstA)[:, :, 0:4]))
                v(lambda e: e.tensor_tensor(t3(dstB)[:, :, 4:8], t3(dstA)[:, :, 4:8],
                                            t3(dstA)[:, :, 0:4], ALU.add))
                vself()

            for tile in range(8):
                v(lambda e, tile=tile: e.reduce_max(a1s[:, tile:tile + 1],
                                                    lgv[:, tile * 8:(tile + 1) * 8],
                                                    axis=AX.X))
            vself()
            v(lambda e: e.tensor_tensor(t3(m1v), lg3, bcast8(a1s), ALU.is_ge))
            vself()
            cumsum_tile(m1v, csA, csB)
            v(lambda e: e.tensor_scalar(csB, csB, 1.5, None, ALU.is_le))
            vself()
            v(lambda e: e.tensor_tensor(f1v, m1v, csB, ALU.mult))
            vself()
            v(lambda e: e.scalar_tensor_tensor(mskd, f1v, -1e9, lgv, ALU.mult, ALU.add))
            vself()
            for tile in range(8):
                v(lambda e, tile=tile: e.reduce_max(a2s[:, tile:tile + 1],
                                                    mskd[:, tile * 8:(tile + 1) * 8],
                                                    axis=AX.X))
            vself()
            v(lambda e: e.tensor_tensor(t3(m1v), t3(mskd), bcast8(a2s), ALU.is_ge))
            vself()
            cumsum_tile(m1v, csA, csB)
            v(lambda e: e.tensor_scalar(csB, csB, 1.5, None, ALU.is_le))
            vself()
            v(lambda e: e.tensor_tensor(f2v, m1v, csB, ALU.mult))
            v(lambda e: e.tensor_tensor(d12, a1s, a2s, ALU.subtract))
            sw("sv", P.cnt["sv"])
            s_(lambda e: e.activation(w1c, d12, AF.Sigmoid))
            vw("ss", P.cnt["ss"])
            v(lambda e: e.tensor_scalar(w2c, w1c, -1.0, 1.0, ALU.mult, ALU.add))
            vself()
            v(lambda e: e.tensor_tensor(t3(f1v), t3(f1v), bcast8(w1c), ALU.mult))
            vself()
            v(lambda e: e.tensor_tensor(t3(f2v), t3(f2v), bcast8(w2c), ALU.mult))
            vself()
            v(lambda e: e.tensor_tensor(lg3, t3(f1v), t3(f2v), ALU.add))
            vself()
            sv_combine_tm = P.cnt["sv"]


            def load_wgu(ee2, g):
                b = g % 2
                P.wait("gpsimd", "stt", wgu_free[b])
                wbuf = wgu0 if b == 0 else wgu1
                wgu_loaded[(pp, ee2, g)] = dg(
                    lambda e, ee2=ee2, g=g, wbuf=wbuf:
                    e.dma_start(out=wbuf[:], in_=d_wgu[ee2, g]))

            def load_wd(ee2, f):
                b = f % 4
                P.wait("gpsimd", "stt", wd_free[b])
                wbuf = wdbs[b]
                wd_loaded[(pp, ee2, f)] = dg(
                    lambda e, ee2=ee2, f=f, wbuf=wbuf:
                    e.dma_start(out=wbuf[:], in_=d_wd[ee2, f]))

            for ee in range(n_experts):
                load_wgu(ee, 0)
                load_wgu(ee, 1)
                for f0 in range(4):
                    load_wd(ee, f0)

                def emit_cbc(ee2, with_drain):
                    h2 = ee2 % 2
                    tw("sv", sv_cmbf)
                    tw("sv", pb67_free[0])
                    tw("ss", ss_cbc_prev[0])
                    for tb in range(NTB):
                        t_(lambda e, ee2=ee2, tb=tb, h2=h2: e.matmul(
                            ps[6 + tb][:, 0:512],
                            sel[:, ee2 * 128:(ee2 + 1) * 128],
                            cmbf[0:8, tb * 512:tb * 512 + 512],
                            start=True, stop=True))
                    dcb = drain() if with_drain else None
                    return dcb

                def evac_cbc(ee2, dcb):
                    h2 = ee2 % 2
                    sw("stt", dcb)
                    sw("sv", csb_read_done.get(ee2 - 2, 0))
                    s_(lambda e, h2=h2: e.copy(csb[:, h2 * 1024:h2 * 1024 + 512],
                                               ps[6][:, 0:512]))
                    s_(lambda e, h2=h2: e.copy(csb[:, h2 * 1024 + 512:h2 * 1024 + 1024],
                                               ps[7][:, 0:512]))
                    ss_csb_of[ee2] = P.cnt["ss"]
                    ss_cbc_prev[0] = P.cnt["ss"]

                a_groups = [tuple(range(s, min(s + 2, 2 * NM))) for s in range(0, 2 * NM, 2)]
                for G, us in enumerate(a_groups):
                    cbc_here = (G == 10 and ee + 1 < n_experts)
                    trans_here = (G == 5 and ee == 0)
                    cbc0_here = (G == 6 and ee == 0)
                    for u in us:
                        m, tb = u // 2, u % 2
                        g = m // 4
                        bg = ps[2 * (u % 3)]
                        bu = ps[2 * (u % 3) + 1]
                        if u % 8 == 0:
                            tw("ldw", wgu_loaded[(pp, ee, g)])
                        if u == 0:
                            tw("sv", sv_xn)
                            if ee == 0:
                                tw("sv", sv_cmtm)
                        if u < 3:
                            tw("sv", pb_pair_free[u])
                        prev = None
                        if u >= 3:
                            prev = (pp, ee, u - 3)
                        else:
                            pmap = {0: 30, 1: 31, 2: 29}
                            if ee > 0:
                                prev = (pp, ee - 1, pmap[u])
                            elif pp > 0:
                                prev = (pp - 1, n_experts - 1, pmap[u])
                        if prev is not None and prev in zdone:
                            tw("sv", zdone[prev])
                        mi = m % 4
                        for kc in range(NKC):
                            wbuf = wgu0 if g % 2 == 0 else wgu1
                            t_(lambda e, wbuf=wbuf, mi=mi, kc=kc, tb=tb, bg=bg:
                               e.matmul(bg[:, 0:512],
                                        wbuf[:, (mi * 8 + kc) * 128:(mi * 8 + kc) * 128 + 128],
                                        xn[:, kc * 1024 + tb * 512: kc * 1024 + tb * 512 + 512],
                                        start=(kc == 0), stop=(kc == 7)))
                        for kc in range(NKC):
                            wbuf = wgu0 if g % 2 == 0 else wgu1
                            t_(lambda e, wbuf=wbuf, mi=mi, kc=kc, tb=tb, bu=bu:
                               e.matmul(bu[:, 0:512],
                                        wbuf[:, ((4 + mi) * 8 + kc) * 128:((4 + mi) * 8 + kc) * 128 + 128],
                                        xn[:, kc * 1024 + tb * 512: kc * 1024 + tb * 512 + 512],
                                        start=(kc == 0), stop=(kc == 7)))
                    if trans_here:
                        tw("sv", sv_combine_tm)
                        tw("ss", ss_cbc_prev[0])
                        for tile in range(8):
                            pb = ps[6] if tile < 4 else ps[7]
                            coff = (tile % 4) * 128
                            t_(lambda e, tile=tile, pb=pb, coff=coff: e.transpose(
                                pb[0:8, coff:coff + 128],
                                cmtm[:, tile * 8:(tile + 1) * 8], identf[:]))
                    if cbc_here:
                        emit_cbc(ee + 1, False)
                    if cbc0_here:
                        emit_cbc(0, False)
                    dc = drain()
                    if trans_here:
                        vw("stt", dc)
                        for tile in range(8):
                            pb = ps[6] if tile < 4 else ps[7]
                            coff = (tile % 4) * 128
                            v(lambda e, tile=tile, pb=pb, coff=coff: e.tensor_copy(
                                cmbf[0:8, tile * 128:(tile + 1) * 128],
                                pb[0:8, coff:coff + 128]))
                        sv_cmbf = P.cnt["sv"]
                    if cbc_here:
                        evac_cbc(ee + 1, dc)
                    if cbc0_here:
                        evac_cbc(0, dc)
                    for g_done in range(NGRP):
                        last_u = 8 * g_done + 7
                        if us[0] <= last_u <= us[-1]:
                            wgu_free[g_done % 2] = dc
                            if g_done + 2 <= 3:
                                load_wgu(ee, g_done + 2)
                    for u in us:
                        m, tb = u // 2, u % 2
                        bg = ps[2 * (u % 3)]
                        bu = ps[2 * (u % 3) + 1]
                        sw("stt", dc)
                        sw("sv", prev_z[0])
                        sc_ = s_(lambda e, bg=bg: e.activation(siluT[:], bg[:, 0:512], AF.Silu))
                        silu_ss[(pp, ee, u)] = sc_
                        vw("stt", dc)
                        vw("ss", sc_)
                        vself()
                        zdone[(pp, ee, u)] = v(lambda e, m=m, tb=tb, bu=bu: e.tensor_tensor(
                            zall[:, m * 1024 + tb * 512: m * 1024 + tb * 512 + 512],
                            siluT[:], bu[:, 0:512], ALU.mult))
                        prev_z[0] = zdone[(pp, ee, u)]
                sv_z_all = P.cnt["sv"]

                ev_of_unit = {}
                for G in range(4):
                    us = tuple(range(4 * G, 4 * G + 4))
                    lastA = {0: 30, 1: 31, 2: 29}
                    for u2 in us:
                        f, tb = u2 // 2, u2 % 2
                        bank = ps[u2 % 8]
                        if u2 % 2 == 0:
                            tw("ldw", wd_loaded[(pp, ee, f)])
                        if u2 < 6:
                            ua = lastA[(u2 // 2) % 3]
                            tw("sv", zdone[(pp, ee, ua)])
                            tw("ss", silu_ss[(pp, ee, ua)])
                        if u2 == 0:
                            tw("sv", zdone[(pp, ee, 30)])
                        if u2 == 1:
                            tw("sv", sv_z_all)
                        if u2 == 6:
                            tw("ss", ss_csb_of.get(ee + 1, 0))
                        if u2 >= 8:
                            tw("sv", ev_of_unit[u2 - 8])
                        wbuf = wdbs[f % 4]
                        for m in range(NM):
                            t_(lambda e, wbuf=wbuf, m=m, tb=tb, bank=bank:
                               e.matmul(bank[:, 0:512],
                                        wbuf[:, m * 128:(m + 1) * 128],
                                        zall[:, m * 1024 + tb * 512: m * 1024 + tb * 512 + 512],
                                        start=(m == 0), stop=(m == 15)))
                    dc = drain()
                    f_hi = us[-1] // 2
                    if f_hi % 2 == 1:
                        wd_free[(f_hi - 1) % 4] = dc
                        wd_free[f_hi % 4] = dc
                        for fnext in (f_hi + 3, f_hi + 4):
                            if fnext <= NF - 1:
                                load_wd(ee, fnext)
                    he = ee % 2
                    for u2 in us:
                        f, tb = u2 // 2, u2 % 2
                        bank = ps[u2 % 8]
                        vw("stt", dc)
                        if u2 == 0:
                            vw("ss", ss_csb_of[ee])
                        aslice = acc[:, f * 1024 + tb * 512: f * 1024 + tb * 512 + 512]
                        vself()
                        mv = v(lambda e, bank=bank, tb=tb, he=he: e.tensor_tensor(
                            ucT[:], bank[:, 0:512],
                            csb[:, he * 1024 + tb * 512: he * 1024 + tb * 512 + 512],
                            ALU.mult))
                        vself()
                        if ee == 0:
                            v(lambda e, f=f, aslice=aslice:
                              e.tensor_scalar(aslice, ucT[:],
                                              bleb[:, f:f + 1], None, ALU.add))
                        else:
                            v(lambda e, aslice=aslice:
                              e.tensor_tensor(aslice, ucT[:], aslice, ALU.add))
                        ev_of_unit[u2] = mv
                        if u2 in (9, 11, 13):
                            pb_pair_free[(u2 - 9) // 2] = mv
                        if u2 == 15:
                            pb67_free[0] = mv
                csb_read_done[ee] = P.cnt["sv"]
                sv_acc_last = P.cnt["sv"]

            tw("sv", sv_xn)
            tw("sv", sv_acc_last)
            if not cw_emitted[0]:
                emit_cw()
                tw("sv", P.cnt["sv"])
            for tb in range(NTB):
                for kc in range(NKC):
                    t_(lambda e, tb=tb, kc=kc: e.matmul(
                        ps[tb][0:24, 0:512],
                        cwsb[:, kc * 24:(kc + 1) * 24],
                        xn[:, kc * 1024 + tb * 512: kc * 1024 + tb * 512 + 512],
                        start=(kc == 0), stop=(kc == 7)))
            stt_lat = drain()
            vw("stt", stt_lat)
            for tb in range(NTB):
                vself()
                v(lambda e, tb=tb: e.tensor_scalar(
                    bits[0:24, tb * 512:tb * 512 + 512], ps[tb][0:24, 0:512],
                    belb[:], 0.0, ALU.add, ALU.is_gt))
            sv_bits = P.cnt["sv"]
            tw("sv", sv_bits)
            for tb in range(NTB):
                t_(lambda e, tb=tb: e.matmul(ps[2 + tb][0:12, 0:512], htb[:],
                                             bits[0:24, tb * 512:tb * 512 + 512],
                                             start=True, stop=True))
            stt_syn = drain()
            vw("stt", stt_syn)
            for tb in range(NTB):
                vself()
                v(lambda e, tb=tb: e.tensor_scalar(
                    synb[0:12, tb * 512:tb * 512 + 512], ps[2 + tb][0:12, 0:512],
                    0.5, None, ALU.is_gt))
            sv_syn = P.cnt["sv"]
            tw("sv", sv_syn)
            for tb in range(NTB):
                t_(lambda e, tb=tb: e.matmul(ps[4 + tb][0:1, 0:512], ones12bf[:],
                                             synb[0:12, tb * 512:tb * 512 + 512],
                                             start=True, stop=True))
            stt_w = drain()
            vw("stt", stt_w)
            for tb in range(NTB):
                vself()
                v(lambda e, tb=tb: e.tensor_scalar(rowA[0:1, tb * 512:(tb + 1) * 512],
                                                   ps[4 + tb][0:1, 0:512],
                                                   -2.0, None, ALU.add))
            vself()
            v(lambda e: e.tensor_scalar(rowB[:], rowA[:], -1.5, None, ALU.is_gt))
            vself()
            v(lambda e: e.tensor_scalar(rowA[:], rowA[:], 1.5, None, ALU.is_lt))
            vself()
            v(lambda e: e.tensor_tensor(rowB[:], rowB[:], rowA[:], ALU.mult))
            vself()
            v(lambda e: e.tensor_copy(rowsb[0:1, :], rowB[:]))
            sv_df = P.cnt["sv"]
            tw("sv", sv_df)
            for tb in range(NTB):
                t_(lambda e, tb=tb: e.matmul(ps[6 + tb][0:12, 0:512],
                                             onesbf[0:1, 0:12],
                                             rowsb[0:1, tb * 512:tb * 512 + 512],
                                             start=True, stop=True))
            stt_dfb = drain()
            vw("stt", stt_dfb)
            for tb in range(NTB):
                sl0 = tb * 512
                vself()
                v(lambda e, tb=tb, sl0=sl0: e.tensor_tensor(
                    mfix[0:12, sl0:sl0 + 512], synb[0:12, sl0:sl0 + 512],
                    ps[6 + tb][0:12, 0:512], ALU.mult))
                vself()
                v(lambda e, sl0=sl0: e.tensor_tensor(
                    csb[0:24, 0:512], bits[0:24, sl0:sl0 + 512],
                    mfix[0:24, sl0:sl0 + 512], ALU.mult))
                v(lambda e, sl0=sl0: e.tensor_tensor(
                    csb[0:24, 512:1024], bits[0:24, sl0:sl0 + 512],
                    mfix[0:24, sl0:sl0 + 512], ALU.add))
                vself()
                v(lambda e, sl0=sl0: e.scalar_tensor_tensor(
                    corr[0:24, sl0:sl0 + 512], csb[0:24, 0:512], -2.0,
                    csb[0:24, 512:1024], ALU.mult, ALU.add))
            sv_corr = P.cnt["sv"]
            sv_csb_read_last = sv_corr

            tw("sv", sv_corr)
            cx_of = {}
            asm_ev_group = {}
            for G in range(4):
                fpair = PAIRS_ASM[G]
                if G >= 2:
                    tw("sv", asm_ev_group[G - 2])
                for f in fpair:
                    b = f % 2
                    P.wait("sync", "stt", xck_freeMM[b])
                    P.wait("sync", "sv", xck_freeV[b])
                    dq = dx if f % 2 == 0 else dy
                    cx_of[f] = dq(lambda e, pp=pp, f=f, b=b:
                                  e.dma_start(out=xck[b][:], in_=d_x[pp, f]))
                    for tb in range(NTB):
                        u3 = f * 2 + tb
                        bank = ps[u3 % 8]
                        for kc in range(NKC):
                            dd = (f - kc) % 8
                            t_(lambda e, kc=kc, tb=tb, dd=dd, bank=bank: e.matmul(
                                bank[:, 0:512],
                                ball[:, dd * 128:(dd + 1) * 128],
                                xn[:, kc * 1024 + tb * 512: kc * 1024 + tb * 512 + 512],
                                start=(kc == 0), stop=False))
                        t_(lambda e, f=f, tb=tb, bank=bank: e.matmul(
                            bank[:, 0:512],
                            wlesb[:, f * 128:(f + 1) * 128],
                            corr[0:24, tb * 512:tb * 512 + 512],
                            start=False, stop=True))
                dc = drain()
                for f in fpair:
                    b = f % 2
                    for tb in range(NTB):
                        u3 = f * 2 + tb
                        bank = ps[u3 % 8]
                        h = u3 % 2
                        vw("stt", dc)
                        vw("ldx" if f % 2 == 0 else "ldy", cx_of[f])
                        P.wait("vector", "outs", outst_free[h])
                        aslice = acc[:, f * 1024 + tb * 512: f * 1024 + tb * 512 + 512]
                        vself()
                        v(lambda e, bank=bank, aslice=aslice: e.tensor_tensor(
                            tmpn[:], bank[:, 0:512], aslice, ALU.add))
                        vself()
                        so = v(lambda e, b=b, tb=tb, h=h: e.tensor_tensor(
                            outst[:, h * 512:(h + 1) * 512], tmpn[:],
                            xck[b][:, tb * 512:tb * 512 + 512], ALU.add))
                        P.wait("sync", "sv", so)
                        P.wait("sync", "outs", P.cnt["outs"])
                        oc = P.op("sync", lambda e, pp=pp, f=f, tb=tb, h=h: e.dma_start(
                            out=d_out[pp, f, :, tb * 512:(tb + 1) * 512],
                            in_=outst[:, h * 512:(h + 1) * 512]), "outs", 16)
                        outst_free[h] = oc
                    xck_freeV[b] = P.cnt["sv"]
                    xck_freeMM[b] = P.cnt["stt"]
                asm_ev_group[G] = P.cnt["sv"]
                if pp + 1 < n_passes:
                    P.wait("sync", "sv", asm_ev_group[G])
                    cpx = cpy = 0
                    for fx in fpair:
                        if fx % 2 == 0:
                            cpx = dx(lambda e, pp=pp, fx=fx: e.dma_start(
                                out=acc[:, fx * 1024:(fx + 1) * 1024],
                                in_=d_x[pp + 1, fx]))
                        else:
                            cpy = dy(lambda e, pp=pp, fx=fx: e.dma_start(
                                out=acc[:, fx * 1024:(fx + 1) * 1024],
                                in_=d_x[pp + 1, fx]))
                    xpair_done[(pp + 1, fpair)] = (cpx, cpy)
            if pp + 1 < n_passes:
                tw("sv", asm_ev_group[2])
                for pr in PAIRS_ASM[:3]:
                    emit_stats_pair(pp + 1, pr, False)
            stt_asm_done = P.cnt["stt"]
            sv_acc_last = P.cnt["sv"]

        P.wait("sync", "outs", P.cnt["outs"])

        with nc.Block() as block:
            P.replay(block)

    return nc



def _host_consts():
    f32 = np.float32
    def pmaj(a):
        return np.ascontiguousarray(a.reshape(8, 128).T)
    ka = pmaj((np.abs(np.fft.fftfreq(1024).astype(f32)) + f32(1e-8)).astype(f32))
    fre = pmaj(np.frombuffer(base64.b64decode(_FRE_B64), dtype="<f4").copy())
    fim = pmaj(np.frombuffer(base64.b64decode(_FIM_B64), dtype="<f4").copy())
    k = np.arange(1024, dtype=np.int64)
    kk, nn = np.meshgrid(k, k, indexing="ij")
    theta = (2.0 * np.pi / 1024.0) * ((kk * nn) % 1024).astype(np.float64)
    c = np.cos(theta)
    s = np.sin(theta)

    def split(a):
        hi = a.astype(f32)
        lo = (a - hi.astype(np.float64)).astype(f32)
        return hi, lo

    chi, clo = split(c)
    shi, slo = split(s)

    def tile_k(a):
        return np.ascontiguousarray(a.reshape(8, 128, 1024).transpose(1, 0, 2).reshape(128, 8192))

    return dict(ka=ka, fre=fre, fim=fim,
                cosh=tile_k(chi), cosl=tile_k(clo),
                sinh=tile_k(shi), sinl=tile_k(slo),
                ident=np.eye(128, dtype=f32))


def _prep(inputs):
    f32 = np.float32
    x = np.ascontiguousarray(inputs["L0_attn_x"], dtype=f32).reshape(-1, 1024)
    gp = np.ascontiguousarray(inputs["L0_ffn_gate_proj_w"], dtype=f32)
    up = np.ascontiguousarray(inputs["L0_ffn_up_proj_w"], dtype=f32)
    dp = np.ascontiguousarray(inputs["L0_ffn_down_proj_w"], dtype=f32)
    W_el = np.ascontiguousarray(inputs["L0_attn_W_el"], dtype=f32)
    b_el = np.ascontiguousarray(inputs["L0_attn_b_el"], dtype=f32)
    W_le = np.ascontiguousarray(inputs["L0_attn_W_le"], dtype=f32)
    b_le = np.ascontiguousarray(inputs["L0_attn_b_le"], dtype=f32)
    gw = np.ascontiguousarray(inputs["L0_ffn_gate_w"], dtype=f32)

    g7 = gp.reshape(8, 8, 128, 4, 4, 128)
    u7 = up.reshape(8, 8, 128, 4, 4, 128)
    wgu = np.empty((8, 4, 128, 2, 4, 8, 128), f32)
    wgu[:, :, :, 0] = g7.transpose(0, 3, 2, 4, 1, 5)
    wgu[:, :, :, 1] = u7.transpose(0, 3, 2, 4, 1, 5)
    wgu = np.ascontiguousarray(wgu.reshape(8, 4, 128, 8192))

    d5 = dp.reshape(8, 16, 128, 8, 128)
    wd = np.ascontiguousarray(d5.transpose(0, 3, 2, 1, 4).reshape(8, 8, 128, 2048))

    wel = np.ascontiguousarray(W_el.reshape(8, 128, 24).transpose(1, 0, 2).reshape(128, 192))
    gwt = np.ascontiguousarray(gw.reshape(8, 128, 8).transpose(1, 0, 2).reshape(128, 64))
    ble = np.ascontiguousarray(b_le.reshape(8, 128).T)
    bel = np.ascontiguousarray(b_el.reshape(24, 1))
    _A = np.array([[1,1,1,0,1,1,1,0,0,0,1,1],[1,1,0,1,1,1,0,0,0,1,1,0],
                   [1,0,1,1,1,0,0,0,1,1,0,1],[0,1,1,1,0,0,0,1,1,0,1,1],
                   [0,0,0,0,1,1,1,1,1,1,1,1],[1,1,1,1,0,0,0,0,0,0,0,0],
                   [0,0,0,0,1,1,1,1,0,0,0,0],[1,1,0,0,1,1,0,0,1,1,0,0],
                   [1,0,1,0,1,0,1,0,1,0,1,0],[0,1,0,1,0,1,0,1,0,1,0,1],
                   [1,0,0,1,1,0,0,1,1,0,0,1],[0,1,1,0,0,1,1,0,0,1,1,0]], dtype=f32)
    H = np.concatenate([_A.T, np.eye(12, dtype=f32)], axis=1)
    ht = np.ascontiguousarray(H.T)
    xrow = np.ascontiguousarray(x[0:1, :])

    selm = np.zeros((8, 1024), np.float32)
    for e_ in range(8):
        selm[e_, e_ * 128:(e_ + 1) * 128] = 1.0
    consts = _host_consts()
    shared = dict(wgu=wgu, wd=wd, wel=wel, wle=W_le, gw=gwt, ble=ble, bel=bel,
                  ht=ht, xrow=xrow, sel_in=selm, **consts)

    in_maps = []
    for c in range(NCORE):
        xc = x[c * TCORE:(c + 1) * TCORE]
        xfm = np.ascontiguousarray(
            xc.reshape(NPASS, TPASS, 8, 128).transpose(0, 2, 3, 1))
        m = dict(shared)
        m["x_fm"] = xfm
        in_maps.append(m)
    return in_maps


def kernel(**inputs):
    in_maps = _prep(inputs)
    nc = _build_program()
    res = run_bass_kernel_spmd(nc, in_maps, core_ids=list(range(NCORE)))
    outs = []
    for c in range(NCORE):
        o = np.asarray(res.results[c]["out"])
        outs.append(o.transpose(0, 3, 1, 2).reshape(TCORE, 1024))
    return np.concatenate(outs, axis=0).reshape(4, 8192, 1024).astype(np.float32)


# version 22 (final)
# speedup vs baseline: 1.0524x; 1.0503x over previous
import base64
import contextlib
import numpy as np
import ml_dtypes

import concourse.bass as bass
import concourse.mybir as mybir
from concourse.bass_utils import run_bass_kernel_spmd

F32 = mybir.dt.float32
BF16 = mybir.dt.bfloat16
AF = mybir.ActivationFunctionType
ALU = mybir.AluOpType
AX = mybir.AxisListType

NCORE = 8
TCORE = 4096
TPASS = 1024
NPASS = TCORE // TPASS
NTB = 2
NE = 8
NM = 16
NKC = 8
NF = 8
NGRP = 4

_FRE_B64 = "JuusPolzrT7l+60+N4SuPn4Mrz6+lK8+9BywPiKlsD5ELbE+W7WxPmo9sj5sxbI+ZU2zPlLVsz4yXbQ+COW0PtFstT6Q9LU+Qny2PuUDtz59i7c+BhO4PoWauD71Ibk+Vam5Pqgwuj7tt7o+JD+7PkzGuz5iTbw+bdS8PmVbvT5R4r0+Kmm+PvPvvj6sdr8+U/2/PuyDwD5xCsE+5JDBPkgXwj6XncI+1yPDPgKqwz4aMMQ+IbbEPhM8xT70wcU+wEfGPnbNxj4cU8c+q9jHPiZeyD6L48g+3GjJPhfuyT49c8o+TvjKPkl9yz4rAsw++YbMPq8LzT5QkM0+2RTOPkiZzj6iHc8+46HPPg0m0D4fqtA+FS7RPvWx0T65NdI+aLnSPvs80z5zwNM+00PUPhjH1D5EStU+U83VPkdQ1j4h09Y+31XXPoLY1z4JW9g+ct3YPsJf2T7z4dk+CGTaPv/l2j7ZZ9s+lunbPjVr3D637Nw+G27dPl7v3T6FcN4+i/HePnRy3z49898+5HPgPm704D7WdOE+IfXhPkp14j5P9eI+N3XjPvv04z6hdOQ+JPTkPoNz5T7C8uU+3nHmPtrw5j6xb+c+Y+7nPvVs6D5h6+g+rWnpPtLn6T7UZeo+suPqPmth6z4A3+s+cFzsPrnZ7D7eVu0+3NPtPrhQ7j5rze4+9knvPl3G7z6cQvA+tr7wPqk68T5xtvE+FTLyPo+t8j7kKPM+DaTzPg4f9D7pmfQ+mBT1PiGP9T5/CfY+soP2Pr399j6dd/c+VfH3PuBq+D5B5Pg+eV35PoPW+T5kT/o+Gsj6PqFA+z7/uPs+MDH8PjWp/D4PIf0+uJj9PjgQ/j6Jh/4+rv7+PqV1/z5q7P8+gzEAP7lsAD/ZpwA/4eIAP9AdAT+pWAE/apMBPxTOAT+lCAI/HkMCP4B9Aj/JtwI/+vECPxQsAz8UZgM//J8DP8rZAz+CEwQ/H00EP6OGBD8OwAQ/YfkEP5syBT+6awU/wKQFP63dBT+AFgY/Ok8GP9mHBj9ewAY/yvgGPxsxBz9SaQc/b6EHP3HZBz9YEQg/JUkIP9mACD9wuAg/6+8IP00nCT+TXgk/v5UJP87MCT/CAwo/mzoKP1hxCj/5pwo/ft4KP+gUCz82Sws/Z4ELP323Cz927Qs/UiMMPxNZDD+2jgw/PsQMP6j5DD/0Lg0/JGQNPziZDT8vzg0/CAMOP8I3Dj9hbA4/4aAOP0XVDj+KCQ8/sT0PP7txDz+mpQ8/ddkPPyQNED+0QBA/KHQQP3ynED+y2hA/yQ0RP8JAET+ccxE/VqYRP/PYET9wCxI/zT0SPwxwEj8qohI/K9QSPwsGEz/LNxM/bWkTP+6aEz9PzBM/kf0TP7IuFD+0XxQ/lJAUP1XBFD/28RQ/dSIVP9VSFT8UgxU/MrMVPy/jFT8LExY/x0IWP2FyFj/aoRY/M9EWP2kAFz9/Lxc/cl4XP0SNFz/2uxc/g+oXP/EYGD88Rxg/ZnUYP22jGD9R0Rg/Ff8YP7QsGT8zWhk/j4cZP8i0GT/e4Rk/0g4aP6M7Gj9RaBo/3JQaP0XBGj+J7Ro/rBkbP6pFGz+GcRs/Pp0bP9HIGz9E9Bs/kB8cP7lKHD/AdRw/oKAcP1/LHD/59Rw/biAdP8BKHT/sdB0/9p4dP9rIHT+a8h0/NhweP6xFHj//bh4/LJgePzXBHj8Y6h4/1hIfP3E7Hz/lYx8/M4wfP120Hz9h3B8/QQQgP/orID+OUyA/+3ogP0SiID9oySA/ZPAgPzsXIT/sPSE/dmQhP9yKIT8asSE/MdchPyT9IT/uIiI/lEgiPxJuIj9pkyI/mrgiP6TdIj+IAiM/RCcjP9pLIz9IcCM/j5QjP7C4Iz+p3CM/egAkPyQkJD+nRyQ/A2skPzaOJD9CsSQ/JtQkP+T2JD94GSU/5TslPypeJT9HgCU/O6IlPwnEJT+u5SU/KgcmP30oJj+pSSY/rWomP4eLJj84rCY/w8wmPyPtJj9cDSc/ay0nP1FNJz8PbSc/oownPw+sJz9Ryyc/aeonP1oJKD8gKCg/vUYoPzJlKD97gyg/nKEoP5O/KD9i3Sg/BfsoP4AYKT/RNSk/9lIpP/RvKT/HjCk/b6kpP+7FKT9C4ik/bf4pP20aKj9DNio/7lEqP29tKj/GiCo/86MqP/S+Kj/M2So/d/QqP/kOKz9QKSs/fEMrP31dKz9Udys/AJErP3+qKz/Vwys/AN0rP/31Kz/SDiw/eicsP/g/LD9JWCw/cHAsP2uILD87oCw/3rcsP1jPLD+k5iw/xf0sP7sULT+DKy0/IUItP5NYLT/abi0/84QtP+GaLT+jsC0/OMYtP6PbLT/h8C0/8QUuP9caLj+PLy4/G0QuP3xYLj+ubC4/toAuP5CULj8+qC4/wLsuPxTPLj894i4/N/UuPwYILz+oGi8/HC0vP2U/Lz9+US8/bWMvPy51Lz/Chi8/KZgvP2KpLz9wui8/T8svPwHcLz+G7C8/3PwvPwgNMD8EHTA/1CwwP3Y8MD/pSzA/MlswP0tqMD82eTA/9YcwP4WWMD/opDA/HrMwPyTBMD/+zjA/qtwwPyjqMD949zA/mQQxP44RMT9UHjE/7CoxP1Y3MT+RQzE/oE8xP39bMT8xZzE/tHIxPwl+MT8wiTE/KJQxP/OeMT+PqTE//LMxPzu+MT9LyDE/LtIxP+HbMT9m5TE/vu4xP+T3MT/eADI/5PcxP77uMT9m5TE/4dsxPy7SMT9LyDE/O74xP/yzMT+PqTE/854xPyiUMT8wiTE/CX4xP7RyMT8xZzE/f1sxP6BPMT+RQzE/VjcxP+wqMT9UHjE/jhExP5kEMT949zA/KOowP6rcMD/+zjA/JMEwPx6zMD/opDA/hZYwP/WHMD82eTA/S2owPzJbMD/pSzA/djwwP9QsMD8EHTA/CA0wP9z8Lz+G7C8/AdwvP0/LLz9wui8/YqkvPymYLz/Chi8/LnUvP21jLz9+US8/ZT8vPxwtLz+oGi8/BggvPzf1Lj894i4/FM8uP8C7Lj8+qC4/kJQuP7aALj+ubC4/fFguPxtELj+PLy4/1xouP/EFLj/h8C0/o9stPzjGLT+jsC0/4ZotP/OELT/abi0/k1gtPyFCLT+DKy0/uxQtP8X9LD+k5iw/WM8sP963LD87oCw/a4gsP3BwLD9JWCw/+D8sP3onLD/SDiw//fUrPwDdKz/Vwys/f6orPwCRKz9Udys/fV0rP3xDKz9QKSs/+Q4rP3f0Kj/M2So/9L4qP/OjKj/GiCo/b20qP+5RKj9DNio/bRoqP23+KT9C4ik/7sUpP2+pKT/HjCk/9G8pP/ZSKT/RNSk/gBgpPwX7KD9i3Sg/k78oP5yhKD97gyg/MmUoP71GKD8gKCg/WgkoP2nqJz9Ryyc/D6wnP6KMJz8PbSc/UU0nP2stJz9cDSc/I+0mP8PMJj84rCY/h4smP61qJj+pSSY/fSgmPyoHJj+u5SU/CcQlPzuiJT9HgCU/Kl4lP+U7JT94GSU/5PYkPybUJD9CsSQ/No4kPwNrJD+nRyQ/JCQkP3oAJD+p3CM/sLgjP4+UIz9IcCM/2ksjP0QnIz+IAiM/pN0iP5q4Ij9pkyI/Em4iP5RIIj/uIiI/JP0hPzHXIT8asSE/3IohP3ZkIT/sPSE/OxchP2TwID9oySA/RKIgP/t6ID+OUyA/+isgP0EEID9h3B8/XbQfPzOMHz/lYx8/cTsfP9YSHz8Y6h4/NcEePyyYHj//bh4/rEUePzYcHj+a8h0/2sgdP/aeHT/sdB0/wEodP24gHT/59Rw/X8scP6CgHD/AdRw/uUocP5AfHD9E9Bs/0cgbPz6dGz+GcRs/qkUbP6wZGz+J7Ro/RcEaP9yUGj9RaBo/ozsaP9IOGj/e4Rk/yLQZP4+HGT8zWhk/tCwZPxX/GD9R0Rg/baMYP2Z1GD88Rxg/8RgYP4PqFz/2uxc/RI0XP3JeFz9/Lxc/aQAXPzPRFj/aoRY/YXIWP8dCFj8LExY/L+MVPzKzFT8UgxU/1VIVP3UiFT/28RQ/VcEUP5SQFD+0XxQ/si4UP5H9Ez9PzBM/7poTP21pEz/LNxM/CwYTPyvUEj8qohI/DHASP809Ej9wCxI/89gRP1amET+ccxE/wkARP8kNET+y2hA/fKcQPyh0ED+0QBA/JA0QP3XZDz+mpQ8/u3EPP7E9Dz+KCQ8/RdUOP+GgDj9hbA4/wjcOPwgDDj8vzg0/OJkNPyRkDT/0Lg0/qPkMPz7EDD+2jgw/E1kMP1IjDD927Qs/fbcLP2eBCz82Sws/6BQLP37eCj/5pwo/WHEKP5s6Cj/CAwo/zswJP7+VCT+TXgk/TScJP+vvCD9wuAg/2YAIPyVJCD9YEQg/cdkHP2+hBz9SaQc/GzEHP8r4Bj9ewAY/2YcGPzpPBj+AFgY/rd0FP8CkBT+6awU/mzIFP2H5BD8OwAQ/o4YEPx9NBD+CEwQ/ytkDP/yfAz8UZgM/FCwDP/rxAj/JtwI/gH0CPx5DAj+lCAI/FM4BP2qTAT+pWAE/0B0BP+HiAD/ZpwA/uWwAP4MxAD9q7P8+pXX/Pq7+/j6Jh/4+OBD+PriY/T4PIf0+Nan8PjAx/D7/uPs+oUD7PhrI+j5kT/o+g9b5Pnld+T5B5Pg+4Gr4PlXx9z6dd/c+vf32PrKD9j5/CfY+IY/1PpgU9T7pmfQ+Dh/0Pg2k8z7kKPM+j63yPhUy8j5xtvE+qTrxPra+8D6cQvA+XcbvPvZJ7z5rze4+uFDuPtzT7T7eVu0+udnsPnBc7D4A3+s+a2HrPrLj6j7UZeo+0ufpPq1p6T5h6+g+9WzoPmPu5z6xb+c+2vDmPt5x5j7C8uU+g3PlPiT05D6hdOQ++/TjPjd14z5P9eI+SnXiPiH14T7WdOE+bvTgPuRz4D49898+dHLfPovx3j6FcN4+Xu/dPhtu3T637Nw+NWvcPpbp2z7ZZ9s+/+XaPghk2j7z4dk+wl/ZPnLd2D4JW9g+gtjXPt9V1z4h09Y+R1DWPlPN1T5EStU+GMfUPtND1D5zwNM++zzTPmi50j65NdI+9bHRPhUu0T4fqtA+DSbQPuOhzz6iHc8+SJnOPtkUzj5QkM0+rwvNPvmGzD4rAsw+SX3LPk74yj49c8o+F+7JPtxoyT6L48g+Jl7IPqvYxz4cU8c+ds3GPsBHxj70wcU+EzzFPiG2xD4aMMQ+AqrDPtcjwz6XncI+SBfCPuSQwT5xCsE+7IPAPlP9vz6sdr8+8+++Pippvj5R4r0+ZVu9Pm3UvD5iTbw+TMa7PiQ/uz7tt7o+qDC6PlWpuT71Ibk+hZq4PgYTuD59i7c+5QO3PkJ8tj6Q9LU+0Wy1PgjltD4yXbQ+UtWzPmVNsz5sxbI+aj2yPlu1sT5ELbE+IqWwPvQcsD6+lK8+fgyvPjeErj7l+60+iXOtPg=="
_FIM_B64 = "62JeP6FtXj8keF4/dYJeP5OMXj9/ll4/OaBeP8CpXj8Us14/OLxePyjFXj/nzV4/c9ZeP8zeXj/z5l4/6e5eP6z2Xj88/l4/mwVfP8cMXz/CE18/iRpfPx4hXz+DJ18/tC1fP7MzXz+COV8/HD9fP4ZEXz+9SV8/w05fP5ZTXz84WF8/qFxfP+VgXz/wZF8/y2hfP3JsXz/pb18/LHNfPz92Xz8geV8/zntfP0t+Xz+XgF8/sIJfP5iEXz9Ohl8/1IdfPyeJXz9Jil8/OYtfP/aLXz+DjF8/34xfPwiNXz8CjV8/yYxfP16MXz/Di18/9opfP/iJXz/HiF8/Z4dfP9WFXz8ShF8/HYJfP/h/Xz+gfV8/GXtfP2B4Xz92dV8/W3JfPw9vXz+Ta18/5GdfPwdkXz/3X18/tltfP0VXXz+jUl8/0U1fP81IXz+ZQ18/ND5fP6A4Xz/bMl8/4yxfP70mXz9mIF8/3hlfPycTXz8+DF8/JQVfP9z9Xj9k9l4/uu5eP9/mXj/W3l4/nNZePzLOXj+ZxV4/z7xeP9WzXj+rql4/UaFeP8iXXj8Ojl4/JYRePwx6Xj/Eb14/TGVeP6RaXj/NT14/xUReP485Xj8pLl4/lCJeP88WXj/cCl4/uf5dP2fyXT/l5V0/NNldP1XMXT9Gv10/CLJdP5ykXT//ll0/NYldPzt7XT8TbV0/u15dPzZQXT+CQV0/nzJdP40jXT9OFF0/3gRdP0L1XD935Vw/fdVcP1bFXD//tFw/fKRcP8iTXD/oglw/2XFcP5xgXD8zT1w/mj1cP9QrXD/gGVw/vgdcP2/1Wz/y4ls/R9BbP269Wz9qqls/NZdbP9aDWz9IcFs/jVxbP6RIWz+PNFs/TSBbP90LWz9A91o/duJaP4DNWj9duFo/DaNaP5CNWj/md1o/EGJaPw5MWj/eNVo/gx9aP/sIWj9H8lk/ZttZP1nEWT8frVk/upVZPyp+WT9sZlk/hE5ZP242WT8tHlk/wAVZPyjtWD9l1Fg/dbtYP1miWD8TiVg/oW9YPwRWWD87PFg/RyJYPykIWD/e7Vc/adNXP8m4Vz/+nVc/CYNXP+hnVz+cTFc/JjFXP4UVVz+7+VY/xN1WP6XBVj9apVY/5YhWP0ZsVj9+T1Y/izJWP20VVj8m+FU/tdpVPxq9VT9Vn1U/Z4FVP09jVT8PRVU/oyZVPw8IVT9Q6VQ/aspUP1mrVD8gjFQ/vmxUPzJNVD99LVQ/oA1UP5rtUz9rzVM/E61TP5OMUz/qa1M/GEtTPx8qUz/8CFM/sudSP0HGUj+mpFI/5IJSP/hgUj/mPlI/rBxSP0r6UT/A11E/DrVRPzWSUT81b1E/DUxRP74oUT9HBVE/quFQP+W9UD/6mVA/53VQP61RUD9MLVA/xQhQPxfkTz9Dv08/SJpPPyd1Tz/gT08/cipPP90ETz8i304/QrlOPzyTTj8QbU4/vUZOP0UgTj+n+U0/5dJNP/yrTT/uhE0/ul1NP2E2TT/kDk0/QOdMP3m/TD+Kl0w/em9MP0JHTD/mHkw/Z/ZLP8HNSz/3pEs/CnxLP/lSSz/CKUs/ZgBLP+jWSj9GrUo/f4NKP5RZSj+GL0o/VAVKP/7aST+FsEk/6YVJPyhbST9FMEk/PwVJPxXaSD/Jrkg/WINIP8ZXSD8RLEg/OQBIPz7URz8gqEc/4HtHP39PRz/7Ikc/VfZGP4vJRj+hnEY/lG9GP2ZCRj8WFUY/o+dFPw+6RT9ajEU/g15FP4swRT9zAkU/N9REP9ylRD9fd0Q/wkhEPwMaRD8k60M/JbxDPwWNQz/EXUM/YS5DP+D+Qj8+z0I/fZ9CP5pvQj+YP0I/dg9CPzXfQT/TrkE/Un5BP7FNQT/xHEE/EuxAPxK7QD/2iUA/uFhAP1wnQD/i9T8/SMQ/P5CSPz+5YD8/wi4/P6/8Pj99yj4/K5g+P7xlPj8wMz4/hAA+P7vNPT/Vmj0/0Wc9P640PT9vAT0/Ec48P5eaPD//Zjw/SjM8P3j/Oz+Jyzs/fZc7P1NjOz8OLzs/rPo6Py7GOj+SkTo/2lw6PwYoOj8Y8zk/C745P+OIOT+fUzk/Px45P8XoOD8tszg/en04P6xHOD/DETg/v9s3P56lNz9lbzc/Dzk3P50CNz8TzDY/bZU2P61eNj/RJzY/3PA1P8y5NT+igjU/Xks1PwAUNT+J3DQ/96Q0P0xtND+HNTQ/qP0zP7DFMz+gjTM/dVUzPzEdMz/U5DI/XqwyP9FzMj8pOzI/aQIyP5DJMT+gkDE/llcxP3UeMT885TA/6KswP39yMD/+ODA/ZP8vP7PFLz/qiy8/C1IvPxMYLz8E3i4/36MuP6FpLj9NLy4/4vQtP2C6LT/Ify0/GEUtP1IKLT93zyw/hZQsP3xZLD9dHiw/KOMrP96nKz9+bCs/CDErP3v1Kj/auSo/JX4qP1hCKj93Bio/gMopP3WOKT9VUik/IBYpP9bZKD93nSg/BGEoP34kKD/i5yc/MqsnP25uJz+VMSc/qvQmP6u3Jj+XeiY/cD0mPzYAJj/owiU/hoUlPxJIJT+KCiU/78wkP0KPJD+CUSQ/rxMkP8nVIz/RlyM/xlkjP6obIz963SI/OZ8iP+VgIj+BIiI/CeQhP4KlIT/nZiE/OyghP37pID+wqiA/0GsgP98sID/e7R8/zK4fP6hvHz90MB8/L/EeP9qxHj92ch4/ADMeP3nzHT/jsx0/efMdPwAzHj92ch4/2rEePy/xHj90MB8/qG8fP8yuHz/e7R8/3ywgP9BrID+wqiA/fukgPzsoIT/nZiE/gqUhPwnkIT+BIiI/5WAiPzmfIj963SI/qhsjP8ZZIz/RlyM/ydUjP68TJD+CUSQ/Qo8kP+/MJD+KCiU/EkglP4aFJT/owiU/NgAmP3A9Jj+XeiY/q7cmP6r0Jj+VMSc/bm4nPzKrJz/i5yc/fiQoPwRhKD93nSg/1tkoPyAWKT9VUik/dY4pP4DKKT93Bio/WEIqPyV+Kj/auSo/e/UqPwgxKz9+bCs/3qcrPyjjKz9dHiw/fFksP4WULD93zyw/UgotPxhFLT/Ify0/YLotP+L0LT9NLy4/oWkuP9+jLj8E3i4/ExgvPwtSLz/qiy8/s8UvP2T/Lz/+ODA/f3IwP+irMD885TA/dR4xP5ZXMT+gkDE/kMkxP2kCMj8pOzI/0XMyP16sMj/U5DI/MR0zP3VVMz+gjTM/sMUzP6j9Mz+HNTQ/TG00P/ekND+J3DQ/ABQ1P15LNT+igjU/zLk1P9zwNT/RJzY/rV42P22VNj8TzDY/nQI3Pw85Nz9lbzc/nqU3P7/bNz/DETg/rEc4P3p9OD8tszg/xeg4Pz8eOT+fUzk/44g5Pwu+OT8Y8zk/Big6P9pcOj+SkTo/LsY6P6z6Oj8OLzs/U2M7P32XOz+Jyzs/eP87P0ozPD//Zjw/l5o8PxHOPD9vAT0/rjQ9P9FnPT/Vmj0/u809P4QAPj8wMz4/vGU+PyuYPj99yj4/r/w+P8IuPz+5YD8/kJI/P0jEPz/i9T8/XCdAP7hYQD/2iUA/ErtAPxLsQD/xHEE/sU1BP1J+QT/TrkE/Nd9BP3YPQj+YP0I/mm9CP32fQj8+z0I/4P5CP2EuQz/EXUM/BY1DPyW8Qz8k60M/AxpEP8JIRD9fd0Q/3KVEPzfURD9zAkU/izBFP4NeRT9ajEU/D7pFP6PnRT8WFUY/ZkJGP5RvRj+hnEY/i8lGP1X2Rj/7Ikc/f09HP+B7Rz8gqEc/PtRHPzkASD8RLEg/xldIP1iDSD/Jrkg/FdpIPz8FST9FMEk/KFtJP+mFST+FsEk//tpJP1QFSj+GL0o/lFlKP3+DSj9GrUo/6NZKP2YASz/CKUs/+VJLPwp8Sz/3pEs/wc1LP2f2Sz/mHkw/QkdMP3pvTD+Kl0w/eb9MP0DnTD/kDk0/YTZNP7pdTT/uhE0//KtNP+XSTT+n+U0/RSBOP71GTj8QbU4/PJNOP0K5Tj8i304/3QRPP3IqTz/gT08/J3VPP0iaTz9Dv08/F+RPP8UIUD9MLVA/rVFQP+d1UD/6mVA/5b1QP6rhUD9HBVE/vihRPw1MUT81b1E/NZJRPw61UT/A11E/SvpRP6wcUj/mPlI/+GBSP+SCUj+mpFI/QcZSP7LnUj/8CFM/HypTPxhLUz/qa1M/k4xTPxOtUz9rzVM/mu1TP6ANVD99LVQ/Mk1UP75sVD8gjFQ/WatUP2rKVD9Q6VQ/DwhVP6MmVT8PRVU/T2NVP2eBVT9Vn1U/Gr1VP7XaVT8m+FU/bRVWP4syVj9+T1Y/RmxWP+WIVj9apVY/pcFWP8TdVj+7+VY/hRVXPyYxVz+cTFc/6GdXPwmDVz/+nVc/ybhXP2nTVz/e7Vc/KQhYP0ciWD87PFg/BFZYP6FvWD8TiVg/WaJYP3W7WD9l1Fg/KO1YP8AFWT8tHlk/bjZZP4ROWT9sZlk/Kn5ZP7qVWT8frVk/WcRZP2bbWT9H8lk/+whaP4MfWj/eNVo/DkxaPxBiWj/md1o/kI1aPw2jWj9duFo/gM1aP3biWj9A91o/3QtbP00gWz+PNFs/pEhbP41cWz9IcFs/1oNbPzWXWz9qqls/br1bP0fQWz/y4ls/b/VbP74HXD/gGVw/1CtcP5o9XD8zT1w/nGBcP9lxXD/oglw/yJNcP3ykXD//tFw/VsVcP33VXD935Vw/QvVcP94EXT9OFF0/jSNdP58yXT+CQV0/NlBdP7teXT8TbV0/O3tdPzWJXT//ll0/nKRdPwiyXT9Gv10/VcxdPzTZXT/l5V0/Z/JdP7n+XT/cCl4/zxZeP5QiXj8pLl4/jzleP8VEXj/NT14/pFpeP0xlXj/Eb14/DHpePyWEXj8Ojl4/yJdeP1GhXj+rql4/1bNeP8+8Xj+ZxV4/Ms5eP5zWXj/W3l4/3+ZeP7ruXj9k9l4/3P1ePyUFXz8+DF8/JxNfP94ZXz9mIF8/vSZfP+MsXz/bMl8/oDhfPzQ+Xz+ZQ18/zUhfP9FNXz+jUl8/RVdfP7ZbXz/3X18/B2RfP+RnXz+Ta18/D29fP1tyXz92dV8/YHhfPxl7Xz+gfV8/+H9fPx2CXz8ShF8/1YVfP2eHXz/HiF8/+IlfP/aKXz/Di18/XoxfP8mMXz8CjV8/CI1fP9+MXz+DjF8/9otfPzmLXz9Jil8/J4lfP9SHXz9Ohl8/mIRfP7CCXz+XgF8/S35fP857Xz8geV8/P3ZfPyxzXz/pb18/cmxfP8toXz/wZF8/5WBfP6hcXz84WF8/llNfP8NOXz+9SV8/hkRfPxw/Xz+COV8/szNfP7QtXz+DJ18/HiFfP4kaXz/CE18/xwxfP5sFXz88/l4/rPZeP+nuXj/z5l4/zN5eP3PWXj/nzV4/KMVePzi8Xj8Us14/wKlePzmgXj9/ll4/k4xeP3WCXj8keF4/oW1ePw=="

_ENGS = ("sync", "gpsimd", "vector", "scalar", "tensor")


class Plan:
    def __init__(self, sems):
        self.streams = {e: [] for e in _ENGS}
        self.sems = sems
        self.cnt = {k: 0 for k in sems}

    def op(self, eng, fn, inc=None, k=1):
        if inc is None:
            self.streams[eng].append(lambda e, fn=fn: fn(e))
            return None
        self.cnt[inc] += k
        sem = self.sems[inc]
        self.streams[eng].append(lambda e, fn=fn, sem=sem, k=k: fn(e).then_inc(sem, k))
        return self.cnt[inc]

    def wait(self, eng, name, val):
        if not val:
            return
        sem = self.sems[name]
        self.streams[eng].append(lambda e, sem=sem, val=val: e.wait_ge(sem, val))

    def replay(self, block):
        for name in _ENGS:
            stream = self.streams[name]
            getattr(block, name)(lambda eng, stream=stream: [f(eng) for f in stream])


def _build_program(n_passes=NPASS, n_experts=NE):
    nc = bass.Bass("TRN2", target_bir_lowering=False)

    d_x = nc.dram_tensor("x_fm", [NPASS, NKC, 128, TPASS], F32, kind="ExternalInput").ap()
    d_xrow = nc.dram_tensor("xrow", [1, 1024], F32, kind="ExternalInput").ap()
    d_wgu = nc.dram_tensor("wgu", [NE, NGRP, 128, 8192], F32, kind="ExternalInput").ap()
    d_wd = nc.dram_tensor("wd", [NE, NF, 128, 2048], F32, kind="ExternalInput").ap()
    d_wel = nc.dram_tensor("wel", [128, 192], F32, kind="ExternalInput").ap()
    d_wle = nc.dram_tensor("wle", [24, 1024], F32, kind="ExternalInput").ap()
    d_gw = nc.dram_tensor("gw", [128, 64], F32, kind="ExternalInput").ap()
    d_ble = nc.dram_tensor("ble", [128, 8], F32, kind="ExternalInput").ap()
    d_bel = nc.dram_tensor("bel", [24, 1], F32, kind="ExternalInput").ap()
    d_ht = nc.dram_tensor("ht", [24, 12], F32, kind="ExternalInput").ap()
    d_ident = nc.dram_tensor("ident", [128, 128], F32, kind="ExternalInput").ap()
    d_sel = nc.dram_tensor("sel_in", [8, 1024], F32, kind="ExternalInput").ap()
    d_ka = nc.dram_tensor("ka", [128, 8], F32, kind="ExternalInput").ap()
    d_fre = nc.dram_tensor("fre", [128, 8], F32, kind="ExternalInput").ap()
    d_fim = nc.dram_tensor("fim", [128, 8], F32, kind="ExternalInput").ap()
    d_cosh = nc.dram_tensor("cosh", [128, 8192], F32, kind="ExternalInput").ap()
    d_cosl = nc.dram_tensor("cosl", [128, 8192], F32, kind="ExternalInput").ap()
    d_sinh = nc.dram_tensor("sinh", [128, 8192], F32, kind="ExternalInput").ap()
    d_sinl = nc.dram_tensor("sinl", [128, 8192], F32, kind="ExternalInput").ap()
    d_out = nc.dram_tensor("out", [NPASS, NF, 128, TPASS], F32, kind="ExternalOutput").ap()
    d_hext = nc.dram_tensor("hext", [2048], BF16, kind="Internal").ap()

    with contextlib.ExitStack() as stk:
        ent = stk.enter_context
        xn = ent(nc.sbuf_tensor("xn", [128, NKC * TPASS], BF16))
        zall = ent(nc.sbuf_tensor("zall", [128, NM * TPASS], BF16))
        acc = ent(nc.sbuf_tensor("acc", [128, NF * TPASS], F32))
        wgu0 = ent(nc.sbuf_tensor("wgu0", [128, 8192], BF16))
        wgu1 = ent(nc.sbuf_tensor("wgu1", [128, 8192], BF16))
        wdbs = [ent(nc.sbuf_tensor(f"wdb{i}", [128, 2048], BF16)) for i in range(4)]
        xck0 = ent(nc.sbuf_tensor("xck0", [128, 1024], F32))
        xck1 = ent(nc.sbuf_tensor("xck1", [128, 1024], F32))
        xsq = ent(nc.sbuf_tensor("xsq", [128, 2048], BF16))
        ball = ent(nc.sbuf_tensor("ball", [128, 1024], BF16))
        rowsb = ent(nc.sbuf_tensor("rowsb", [128, 1024], BF16))
        matrow = ent(nc.sbuf_tensor("matrow", [1, 4096], F32))
        rowA = ent(nc.sbuf_tensor("rowA", [1, 1024], F32))
        rowB = ent(nc.sbuf_tensor("rowB", [1, 1024], F32))
        msc = ent(nc.sbuf_tensor("msc", [1, 64], F32))
        fsc = ent(nc.sbuf_tensor("fsc", [128, 64], F32))
        fsc2 = ent(nc.sbuf_tensor("fsc2", [128, 64], F32))
        dbc = ent(nc.sbuf_tensor("dbc", [128, 16], F32))
        csb = ent(nc.sbuf_tensor("csb", [128, 2048], BF16))
        tmpn = ent(nc.sbuf_tensor("tmpn", [128, 512], F32))
        siluT = ent(nc.sbuf_tensor("siluT", [128, 512], F32))
        ucT = ent(nc.sbuf_tensor("ucT", [128, 512], F32))
        outst = ent(nc.sbuf_tensor("outst", [128, 1024], F32))
        cwsb = ent(nc.sbuf_tensor("cwsb", [128, 192], BF16))
        welsb = ent(nc.sbuf_tensor("welsb", [128, 192], BF16))
        gwbf = ent(nc.sbuf_tensor("gwbf", [128, 64], BF16))
        identf = ent(nc.sbuf_tensor("identf", [128, 128], F32))
        wlesb = ent(nc.sbuf_tensor("wlesb", [24, 1024], BF16))
        bits = ent(nc.sbuf_tensor("bits", [24, 1024], BF16))
        synb = ent(nc.sbuf_tensor("synb", [12, 1024], BF16))
        mfix = ent(nc.sbuf_tensor("mfix", [24, 1024], BF16))
        corr = ent(nc.sbuf_tensor("corr", [24, 1024], BF16))
        cmbf = ent(nc.sbuf_tensor("cmbf", [8, 1024], BF16))
        cmtm = ent(nc.sbuf_tensor("cmtm", [128, 64], F32))
        bleb = ent(nc.sbuf_tensor("bleb", [128, 8], F32))
        belb = ent(nc.sbuf_tensor("belb", [24, 1], F32))
        htb = ent(nc.sbuf_tensor("htb", [24, 12], BF16))
        sel = ent(nc.sbuf_tensor("sel", [8, 1024], BF16))
        onesf = ent(nc.sbuf_tensor("onesf", [1, 128], F32))
        onesfc = ent(nc.sbuf_tensor("onesfc", [128, 1], F32))
        onesbf = ent(nc.sbuf_tensor("onesbf", [1, 128], BF16))
        oneskbf = ent(nc.sbuf_tensor("oneskbf", [128, 1], BF16))
        ones12bf = ent(nc.sbuf_tensor("ones12bf", [12, 1], BF16))
        onecol1 = ent(nc.sbuf_tensor("onecol1", [1, 1], F32))
        cns = ent(nc.sbuf_tensor("cns", [128, 16], F32))
        lxrow = ent(nc.sbuf_tensor("lxrow", [1, 8], F32))

        ps = [ent(nc.psum_tensor(f"p{i}", [128, 512], F32)) for i in range(8)]

        sems = {}
        for name in ("ldx", "ldy", "ldw", "sv", "ss", "stt", "outs", "ou2"):
            sems[name] = ent(nc.semaphore(name))
        P = Plan(sems)

        trig = acc[:]

        lx_np = np.log(np.array([2.0, 4.0, 8.0, 16.0, 32.0], np.float32))
        lxsum = float(lx_np.sum())
        den = np.float32(5.0) * np.float32((lx_np * lx_np).sum()) - np.float32(lx_np.sum()) ** 2
        neg_invden = float(-(np.float32(1.0) / den))

        def v(fn):
            return P.op("vector", fn, "sv")

        def s_(fn):
            return P.op("scalar", fn, "ss")

        def t_(fn):
            return P.op("tensor", fn, None)

        def drain():
            return P.op("tensor", lambda e: e.drain(), "stt")

        def vw(name, val):
            P.wait("vector", name, val)

        def sw(name, val):
            P.wait("scalar", name, val)

        def tw(name, val):
            P.wait("tensor", name, val)

        def vself():
            P.wait("vector", "sv", P.cnt["sv"])

        v(lambda e: e.memset(onesf[:], 1.0))
        v(lambda e: e.memset(onesfc[:], 1.0))
        v(lambda e: e.memset(onesbf[:], 1.0))
        v(lambda e: e.memset(oneskbf[:], 1.0))
        v(lambda e: e.memset(ones12bf[:], 1.0))
        v(lambda e: e.memset(onecol1[:], 1.0))
        v(lambda e: e.memset(cns[:, 0:1], float(np.float32(np.pi))))
        v(lambda e: e.memset(cns[:, 1:2], float(np.float32(np.pi / 2))))
        v(lambda e: e.memset(cns[:, 2:3], -1.0))
        v(lambda e: e.memset(cns[:, 3:4], 0.0))
        v(lambda e: e.memset(cns[:, 4:5], 1.0))
        v(lambda e: e.memset(mfix[:], 0.0))
        v(lambda e: e.memset(dbc[:, 1:2], 1.0))
        for i, lv in enumerate(lx_np):
            v(lambda e, i=i, lv=float(lv): e.memset(lxrow[0:1, i:i + 1], lv))
        sv_consts = P.cnt["sv"]

        def dx(fn):
            P.wait("sync", "ldx", P.cnt["ldx"])
            return P.op("sync", fn, "ldx", 16)

        def dy(fn):
            P.wait("sync", "ldy", P.cnt["ldy"])
            return P.op("sync", fn, "ldy", 16)

        def dg(fn):
            P.wait("gpsimd", "ldw", P.cnt["ldw"])
            return P.op("gpsimd", fn, "ldw", 16)

        dx(lambda e: e.dma_start(out=fsc[:, 0:8], in_=d_ka))
        dx(lambda e: e.dma_start(out=fsc[:, 16:24], in_=d_fre))
        dx(lambda e: e.dma_start(out=fsc[:, 24:32], in_=d_fim))
        dx(lambda e: e.dma_start(out=rowA[:], in_=d_xrow))
        dx(lambda e: e.dma_start(out=bleb[:], in_=d_ble))
        dx(lambda e: e.dma_start(out=belb[:], in_=d_bel))
        dx(lambda e: e.dma_start(out=identf[:], in_=d_ident))
        ldx_consts = P.cnt["ldx"]
        dg(lambda e: e.dma_start(out=welsb[:], in_=d_wel))
        dg(lambda e: e.dma_start(out=wlesb[:], in_=d_wle))
        dg(lambda e: e.dma_start(out=htb[:], in_=d_ht))
        dg(lambda e: e.dma_start(out=gwbf[:], in_=d_gw))
        dg(lambda e: e.dma_start(out=sel[:], in_=d_sel))
        ldw_consts = P.cnt["ldw"]

        vw("ldx", ldx_consts)
        vw("sv", sv_consts)
        v(lambda e: e.tensor_tensor(fsc[:, 8:16], fsc[:, 0:8], fsc[:, 0:8], ALU.mult))
        v(lambda e: e.reduce_sum(msc[0:1, 0:1], rowA[:], axis=AX.X))
        vself()
        v(lambda e: e.tensor_scalar(msc[0:1, 1:2], msc[0:1, 0:1], 1.0 / 1024.0, None, ALU.mult))
        vself()
        v(lambda e: e.tensor_scalar(rowA[:], rowA[:], msc[0:1, 1:2], None, ALU.subtract))
        vself()
        v(lambda e: e.tensor_tensor(rowB[:], rowA[:], rowA[:], ALU.mult))
        vself()
        v(lambda e: e.reduce_sum(msc[0:1, 2:3], rowB[:], axis=AX.X))
        vself()
        v(lambda e: e.tensor_scalar(msc[0:1, 3:4], msc[0:1, 2:3], 1.0 / 1024.0, None, ALU.mult))
        vself()
        v(lambda e: e.tensor_scalar(msc[0:1, 4:5], msc[0:1, 3:4], 1e-5, None, ALU.add))
        sw("sv", P.cnt["sv"])
        s_(lambda e: e.activation(msc[0:1, 5:6], msc[0:1, 4:5], AF.Sqrt))
        vw("ss", P.cnt["ss"])
        v(lambda e: e.reciprocal(msc[0:1, 6:7], msc[0:1, 5:6]))
        vself()
        v(lambda e: e.tensor_tensor(msc[0:1, 7:8], msc[0:1, 6:7], msc[0:1, 6:7], ALU.mult))
        vself()
        v(lambda e: e.tensor_tensor(msc[0:1, 8:9], msc[0:1, 4:5], msc[0:1, 7:8], ALU.mult))
        vself()
        v(lambda e: e.tensor_scalar(msc[0:1, 9:10], msc[0:1, 8:9], -0.5, 1.5, ALU.mult, ALU.add))
        vself()
        v(lambda e: e.tensor_tensor(msc[0:1, 10:11], msc[0:1, 6:7], msc[0:1, 9:10], ALU.mult))
        vself()
        v(lambda e: e.tensor_scalar(matrow[0:1, 2048:3072], rowA[:], msc[0:1, 10:11],
                                    None, ALU.mult))
        vself()
        for i, sc2 in enumerate((4, 16, 64, 256, 1024)):
            v(lambda e, sc2=sc2: e.reduce_sum(msc[0:1, 12:13],
                                              matrow[0:1, 2048:2048 + sc2], axis=AX.X))
            vself()
            v(lambda e, sc2=sc2: e.tensor_scalar(msc[0:1, 13:14], msc[0:1, 12:13],
                                                 1.0 / sc2, None, ALU.mult))
            vself()
            v(lambda e, sc2=sc2: e.tensor_scalar(rowB[0:1, 0:sc2],
                                                 matrow[0:1, 2048:2048 + sc2],
                                                 msc[0:1, 13:14], None, ALU.is_gt))
            vself()
            v(lambda e, sc2=sc2: e.reduce_sum(msc[0:1, 14:15], rowB[0:1, 0:sc2], axis=AX.X))
            vself()
            v(lambda e, i=i: e.tensor_scalar(msc[0:1, 32 + i:33 + i], msc[0:1, 14:15],
                                             1.0, None, ALU.max))
            vself()
        sw("sv", P.cnt["sv"])
        s_(lambda e: e.activation(msc[0:1, 40:45], msc[0:1, 32:37], AF.Ln))
        vw("ss", P.cnt["ss"])
        v(lambda e: e.tensor_tensor(msc[0:1, 48:53], msc[0:1, 40:45], lxrow[0:1, 0:5],
                                    ALU.mult))
        vself()
        v(lambda e: e.reduce_sum(msc[0:1, 16:17], msc[0:1, 48:53], axis=AX.X))
        v(lambda e: e.reduce_sum(msc[0:1, 17:18], msc[0:1, 40:45], axis=AX.X))
        vself()
        v(lambda e: e.tensor_scalar(msc[0:1, 18:19], msc[0:1, 16:17], 5.0, None, ALU.mult))
        v(lambda e: e.tensor_scalar(msc[0:1, 19:20], msc[0:1, 17:18], lxsum, None, ALU.mult))
        vself()
        v(lambda e: e.tensor_tensor(msc[0:1, 20:21], msc[0:1, 18:19], msc[0:1, 19:20],
                                    ALU.subtract))
        vself()
        v(lambda e: e.tensor_scalar(msc[0:1, 21:22], msc[0:1, 20:21], neg_invden,
                                    None, ALU.mult))
        vself()
        v(lambda e: e.tensor_scalar(msc[0:1, 30:31], msc[0:1, 21:22], 1.0, 2.0,
                                    ALU.max, ALU.min))
        sv_D = P.cnt["sv"]
        tw("sv", sv_D)
        t_(lambda e: e.matmul(ps[0][:, 0:1], onesf[:], msc[0:1, 30:31],
                              start=True, stop=True))
        stt_db = drain()
        vw("stt", stt_db)
        v(lambda e: e.tensor_copy(dbc[:, 0:1], ps[0][:, 0:1]))
        vself()

        def filt_pipeline(xi, out0):
            X = dbc[:, xi:xi + 1]
            c0 = 2 + 3 * xi
            v(lambda e: e.tensor_tensor(dbc[:, c0:c0 + 1], X, X, ALU.mult))
            vself()
            v(lambda e: e.tensor_scalar(dbc[:, c0:c0 + 1], dbc[:, c0:c0 + 1], 0.2,
                                        None, ALU.mult))
            vself()
            v(lambda e: e.tensor_tensor(dbc[:, c0:c0 + 1], X, dbc[:, c0:c0 + 1], ALU.add))
            v(lambda e: e.tensor_scalar(dbc[:, c0 + 1:c0 + 2], X, 0.5, 1.0,
                                        ALU.mult, ALU.add))
            v(lambda e: e.tensor_scalar(dbc[:, c0 + 2:c0 + 3], X, 0.1, None, ALU.mult))
            vself()
            v(lambda e: e.tensor_scalar(fsc[:, 32:40], fsc[:, 0:8], dbc[:, c0 + 1:c0 + 2],
                                        None, ALU.mult))
            vself()
            v(lambda e: e.tensor_scalar(fsc[:, 32:40], fsc[:, 32:40], dbc[:, c0:c0 + 1],
                                        None, ALU.add))
            sw("sv", P.cnt["sv"])
            s_(lambda e: e.activation(fsc[:, 40:48], fsc[:, 32:40], AF.Sin,
                                      bias=cns[:, 0:1], scale=cns[:, 2:3]))
            v(lambda e: e.tensor_scalar(fsc[:, 48:56], fsc[:, 8:16], dbc[:, c0 + 2:c0 + 3],
                                        None, ALU.mult))
            vself()
            v(lambda e: e.tensor_tensor(fsc[:, 48:56], fsc[:, 48:56], fsc[:, 0:8],
                                        ALU.subtract))
            vself()
            v(lambda e: e.tensor_scalar(fsc[:, 48:56], fsc[:, 48:56], dbc[:, c0:c0 + 1],
                                        None, ALU.add))
            sw("sv", P.cnt["sv"])
            s_(lambda e: e.activation(fsc[:, 56:64], fsc[:, 48:56], AF.Sin,
                                      bias=cns[:, 1:2], scale=cns[:, 2:3]))
            s_(lambda e: e.activation(fsc2[:, 48:56], fsc[:, 48:56], AF.Sin,
                                      bias=cns[:, 3:4], scale=cns[:, 4:5]))
            vw("ss", P.cnt["ss"])
            v(lambda e, out0=out0: e.tensor_tensor(fsc2[:, out0:out0 + 8],
                                                   fsc[:, 40:48], fsc[:, 56:64], ALU.mult))
            vself()
            v(lambda e, out0=out0: e.tensor_tensor(fsc2[:, out0 + 8:out0 + 16],
                                                   fsc[:, 40:48], fsc2[:, 48:56], ALU.mult))
            vself()

        filt_pipeline(0, 0)
        filt_pipeline(1, 16)
        v(lambda e: e.tensor_tensor(fsc2[:, 32:40], fsc2[:, 0:8], fsc2[:, 16:24],
                                    ALU.subtract))
        v(lambda e: e.tensor_tensor(fsc2[:, 40:48], fsc2[:, 8:16], fsc2[:, 24:32],
                                    ALU.subtract))
        vself()
        v(lambda e: e.tensor_tensor(fsc2[:, 32:40], fsc2[:, 32:40], fsc[:, 16:24], ALU.add))
        v(lambda e: e.tensor_tensor(fsc2[:, 40:48], fsc2[:, 40:48], fsc[:, 24:32], ALU.add))
        vself()
        v(lambda e: e.tensor_scalar(fsc2[:, 32:40], fsc2[:, 32:40], 1.0 / 1024.0,
                                    None, ALU.mult))
        v(lambda e: e.tensor_scalar(fsc2[:, 40:48], fsc2[:, 40:48], -1.0 / 1024.0,
                                    None, ALU.mult))
        sv_spec = P.cnt["sv"]

        half_free = {0: 0, 1: 0}
        first = [True, True]
        hparts = ((d_cosh, 32), (d_sinh, 40), (d_cosl, 32), (d_sinl, 40))
        tw("sv", sv_spec)
        stt_hd = 0
        for li, (dsrc, repc) in enumerate(hparts):
            for hf in range(2):
                P.wait("sync", "stt", half_free[hf])
                c = dx(lambda e, dsrc=dsrc, hf=hf: e.dma_start(
                    out=trig[:, hf * 4096:(hf + 1) * 4096],
                    in_=dsrc[:, hf * 4096:(hf + 1) * 4096]))
                tw("ldx", c)
                for oh in range(2):
                    for kc in range(4 * hf, 4 * hf + 4):
                        st = first[oh]
                        first[oh] = False
                        sp = (li == 3 and hf == 1 and kc == 7)
                        t_(lambda e, oh=oh, kc=kc, repc=repc, st=st, sp=sp:
                           e.matmul(ps[1 + oh][0:1, 0:512],
                                    fsc2[:, repc + kc:repc + kc + 1],
                                    trig[:, kc * 1024 + oh * 512: kc * 1024 + oh * 512 + 512],
                                    start=st, stop=sp))
                stt_hd = drain()
                half_free[hf] = stt_hd
        stt_h = stt_hd
        vw("stt", stt_h)
        v(lambda e: e.tensor_copy(rowB[0:1, 0:512], ps[1][0:1, 0:512]))
        v(lambda e: e.tensor_copy(rowB[0:1, 512:1024], ps[2][0:1, 0:512]))
        vself()
        v(lambda e: e.tensor_copy(rowsb[32:33, :], rowB[:]))
        sv_hbf = P.cnt["sv"]
        P.wait("sync", "sv", sv_hbf)
        dx(lambda e: e.dma_start(out=d_hext[0:1024], in_=rowsb[32:33, :]))
        dx(lambda e: e.dma_start(out=d_hext[1024:2048], in_=rowsb[32:33, :]))
        c_hext = P.cnt["ldx"]
        P.wait("sync", "ldx", c_hext)
        x0_load = dx(lambda e: e.dma_start(
            out=acc[:].rearrange("p (kc t) -> p kc t", kc=8),
            in_=d_x[0].transpose([1, 0, 2])))
        P.wait("sync", "ldx", P.cnt["ldx"])
        for p in range(128):
            ap_p = bass.AP(d_hext.tensor, 1024 - p, [[128, 8], [1, 128]])
            P.op("sync", lambda e, p=p, ap_p=ap_p: e.dma_start(
                out=ball[p:p + 1, :].rearrange("a (d j) -> a d j", d=8), in_=ap_p),
                "ldx", 16)
        ldx_ball = P.cnt["ldx"]

        sv_dpath = P.cnt["sv"]
        cw_emitted = [False]

        def emit_cw():
            cw_emitted[0] = True
            tw("ldx", ldx_ball)
            tw("ldw", ldw_consts)
            for kc in range(8):
                for fc in range(8):
                    d = (kc - fc) % 8
                    t_(lambda e, kc=kc, fc=fc, d=d:
                       e.matmul(ps[3][:, kc * 24:(kc + 1) * 24],
                                ball[:, d * 128:(d + 1) * 128],
                                welsb[:, fc * 24:(fc + 1) * 24],
                                start=(fc == 0), stop=(fc == 7)))
            stt_cw = drain()
            vw("stt", stt_cw)
            v(lambda e: e.tensor_copy(cwsb[:], ps[3][:, 0:192]))

        wgu_free = {0: 0, 1: 0}
        wd_free = {0: 0, 1: 0, 2: 0, 3: 0}
        wgu_loaded = {}
        wd_loaded = {}
        xck = (xck0, xck1)
        xck_freeMM = {0: 0, 1: 0}
        xck_freeV = {0: 0, 1: 0}
        outst_free = {0: 0, 1: 0}
        stt_asm_done = stt_h
        ss_cbc_prev = 0
        sv_cmbf = 0
        sv_acc_last = 0
        sv_csb_read_last = 0
        zdone = {}
        silu_ss = {}
        prev_z = [0]
        pb67_free = [0]
        pb_pair_free = {0: 0, 1: 0, 2: 0}
        xsq_free = [stt_h]
        stats_first = {}
        stats_last_drain = {}
        xpair_done = {}

        def emit_stats_pair(pp2, kcs, is_last):
            cx, cy = xpair_done[(pp2, kcs)]
            sw("ldx", cx)
            sw("ldy", cy)
            sw("stt", xsq_free[0])
            for kc in kcs:
                s_(lambda e, kc=kc: e.square(
                    xsq[:, (kc % 2) * 1024:(kc % 2) * 1024 + 1024],
                    acc[:, kc * 1024:(kc + 1) * 1024]))
            ssq = P.cnt["ss"]
            first = stats_first.setdefault(pp2, [True])
            tw("ldx", cx)
            tw("ldy", cy)
            for j, kc in enumerate(kcs):
                st = first[0] and j == 0
                sp = is_last and j == len(kcs) - 1
                for tb in range(NTB):
                    t_(lambda e, kc=kc, tb=tb, st=st, sp=sp:
                       e.matmul(ps[4 + tb][0:1, 0:512], onesfc[:],
                                acc[:, kc * 1024 + tb * 512: kc * 1024 + tb * 512 + 512],
                                start=st, stop=sp))
            tw("ss", ssq)
            for j, kc in enumerate(kcs):
                st = first[0] and j == 0
                sp = is_last and j == len(kcs) - 1
                for tb in range(NTB):
                    t_(lambda e, kc=kc, tb=tb, st=st, sp=sp:
                       e.matmul(ps[6 + tb][0:1, 0:512], oneskbf[:],
                                xsq[:, (kc % 2) * 1024 + tb * 512: (kc % 2) * 1024 + tb * 512 + 512],
                                start=st, stop=sp))
            first[0] = False
            ds = drain()
            xsq_free[0] = ds
            stats_last_drain[pp2] = ds
            return ds
        ss_cbc_prev = [0]
        ss_csb_of = {}
        csb_read_done = {}

        PAIRS_ASM = ((6, 7), (4, 5), (2, 3), (0, 1))
        for pp in range(n_passes):
            if pp == 0:
                for pr in PAIRS_ASM:
                    xpair_done[(0, pr)] = (x0_load, 0)
            if pp == 0:
                tw("sv", sv_dpath)
                sw("sv", sv_dpath)
                for i, pr in enumerate(((0, 1), (2, 3), (4, 5), (6, 7))):
                    emit_stats_pair(0, pr, i == 3)
            else:
                emit_stats_pair(pp, (0, 1), True)
            stt_stats = stats_last_drain[pp]
            cxl, cyl = xpair_done[(pp, (0, 1))]
            vw("stt", stt_stats)
            for tb in range(NTB):
                v(lambda e, tb=tb: e.tensor_scalar(matrow[0:1, tb * 512:tb * 512 + 512],
                                                   ps[4 + tb][0:1, 0:512],
                                                   1.0 / 1024.0, None, ALU.mult))
                v(lambda e, tb=tb: e.tensor_scalar(rowA[0:1, tb * 512:tb * 512 + 512],
                                                   ps[6 + tb][0:1, 0:512],
                                                   1.0 / 1024.0, None, ALU.mult))
            vself()
            v(lambda e: e.tensor_tensor(rowB[:], matrow[0:1, 0:1024],
                                        matrow[0:1, 0:1024], ALU.mult))
            vself()
            v(lambda e: e.tensor_tensor(rowB[:], rowA[:], rowB[:], ALU.subtract))
            vself()
            v(lambda e: e.tensor_scalar(rowB[:], rowB[:], 1e-5, None, ALU.add))
            sw("sv", P.cnt["sv"])
            s_(lambda e: e.activation(rowA[:], rowB[:], AF.Sqrt))
            vw("ss", P.cnt["ss"])
            v(lambda e: e.reciprocal(matrow[0:1, 1024:2048], rowA[:]))
            vself()
            v(lambda e: e.tensor_tensor(rowA[:], matrow[0:1, 1024:2048],
                                        matrow[0:1, 1024:2048], ALU.mult))
            vself()
            v(lambda e: e.tensor_tensor(rowA[:], rowB[:], rowA[:], ALU.mult))
            vself()
            v(lambda e: e.tensor_scalar(rowA[:], rowA[:], -0.5, 1.5, ALU.mult, ALU.add))
            vself()
            v(lambda e: e.tensor_tensor(matrow[0:1, 1024:2048], matrow[0:1, 1024:2048],
                                        rowA[:], ALU.mult))
            sv_stats = P.cnt["sv"]

            tw("sv", sv_stats)
            tw("stt", stt_asm_done)
            for tb in range(NTB):
                t_(lambda e, tb=tb: e.matmul(ps[tb][:, 0:512], onesf[:],
                                             matrow[0:1, tb * 512:tb * 512 + 512],
                                             start=True, stop=True))
                t_(lambda e, tb=tb: e.matmul(ps[2 + tb][:, 0:512], onesf[:],
                                             matrow[0:1, 1024 + tb * 512:1024 + tb * 512 + 512],
                                             start=True, stop=True))
            stt_bc = drain()
            sw("stt", stt_bc)
            sw("sv", xck_freeV[0])
            sw("sv", xck_freeV[1])
            s_(lambda e: e.copy(xck0[:, 0:512], ps[0][:, 0:512]))
            s_(lambda e: e.copy(xck0[:, 512:1024], ps[1][:, 0:512]))
            s_(lambda e: e.copy(xck1[:, 0:512], ps[2][:, 0:512]))
            s_(lambda e: e.copy(xck1[:, 512:1024], ps[3][:, 0:512]))
            ss_mb = P.cnt["ss"]

            vw("ss", ss_mb)
            vw("ldx", cxl)
            vw("ldy", cyl)
            vw("stt", stt_asm_done)
            xn_chunk = {}
            for kc in range(NKC):
                vself()
                v(lambda e, kc=kc: e.tensor_tensor(
                    xn[:, kc * 1024:(kc + 1) * 1024],
                    acc[:, kc * 1024:(kc + 1) * 1024], xck0[:], ALU.subtract))
                vself()
                xn_chunk[kc] = v(lambda e, kc=kc: e.tensor_tensor(
                    xn[:, kc * 1024:(kc + 1) * 1024],
                    xn[:, kc * 1024:(kc + 1) * 1024], xck1[:], ALU.mult))
            sv_xn = P.cnt["sv"]
            xck_freeV[0] = sv_xn
            xck_freeV[1] = sv_xn
            tw("ss", ss_mb)
            if pp == 0:
                tw("ldw", ldw_consts)
            for kc in range(NKC):
                tw("sv", xn_chunk[kc])
                for tile in range(8):
                    t_(lambda e, kc=kc, tile=tile: e.matmul(
                        ps[tile][:, 0:8],
                        xn[:, kc * 1024 + tile * 128: kc * 1024 + (tile + 1) * 128],
                        gwbf[:, kc * 8:(kc + 1) * 8],
                        start=(kc == 0), stop=(kc == 7)))
            stt_lg = drain()
            acc_freeMM = stt_lg

            vw("stt", stt_lg)
            for tile in range(8):
                v(lambda e, tile=tile: e.tensor_copy(cmtm[:, tile * 8:(tile + 1) * 8],
                                                     ps[tile][:, 0:8]))
            sv_cmtm = P.cnt["sv"]
            vself()
            lgv = cmtm[:, 0:64]
            lg3 = lgv.rearrange("p (t j) -> p t j", t=8)
            a1s = tmpn[:, 0:8]
            a2s = tmpn[:, 8:16]
            w1c = tmpn[:, 16:24]
            w2c = tmpn[:, 24:32]
            d12 = tmpn[:, 32:40]
            m1v = tmpn[:, 64:128]
            csA = tmpn[:, 128:192]
            csB = tmpn[:, 192:256]
            f1v = tmpn[:, 256:320]
            mskd = tmpn[:, 320:384]
            f2v = tmpn[:, 384:448]

            def t3(ap):
                return ap.rearrange("p (t j) -> p t j", t=8)

            def bcast8(col8):
                return col8[:, :, None].broadcast_to([128, 8, 8])

            def cumsum_tile(src_v, dstA, dstB):
                v(lambda e: e.tensor_copy(dstA, src_v))
                vself()
                v(lambda e: e.tensor_copy(t3(dstB)[:, :, 0:1], t3(dstA)[:, :, 0:1]))
                v(lambda e: e.tensor_tensor(t3(dstB)[:, :, 1:8], t3(dstA)[:, :, 1:8],
                                            t3(dstA)[:, :, 0:7], ALU.add))
                vself()
                v(lambda e: e.tensor_copy(t3(dstA)[:, :, 0:2], t3(dstB)[:, :, 0:2]))
                v(lambda e: e.tensor_tensor(t3(dstA)[:, :, 2:8], t3(dstB)[:, :, 2:8],
                                            t3(dstB)[:, :, 0:6], ALU.add))
                vself()
                v(lambda e: e.tensor_copy(t3(dstB)[:, :, 0:4], t3(dstA)[:, :, 0:4]))
                v(lambda e: e.tensor_tensor(t3(dstB)[:, :, 4:8], t3(dstA)[:, :, 4:8],
                                            t3(dstA)[:, :, 0:4], ALU.add))
                vself()

            for tile in range(8):
                v(lambda e, tile=tile: e.reduce_max(a1s[:, tile:tile + 1],
                                                    lgv[:, tile * 8:(tile + 1) * 8],
                                                    axis=AX.X))
            vself()
            v(lambda e: e.tensor_tensor(t3(m1v), lg3, bcast8(a1s), ALU.is_ge))
            vself()
            cumsum_tile(m1v, csA, csB)
            v(lambda e: e.tensor_scalar(csB, csB, 1.5, None, ALU.is_le))
            vself()
            v(lambda e: e.tensor_tensor(f1v, m1v, csB, ALU.mult))
            vself()
            v(lambda e: e.scalar_tensor_tensor(mskd, f1v, -1e9, lgv, ALU.mult, ALU.add))
            vself()
            for tile in range(8):
                v(lambda e, tile=tile: e.reduce_max(a2s[:, tile:tile + 1],
                                                    mskd[:, tile * 8:(tile + 1) * 8],
                                                    axis=AX.X))
            vself()
            v(lambda e: e.tensor_tensor(t3(m1v), t3(mskd), bcast8(a2s), ALU.is_ge))
            vself()
            cumsum_tile(m1v, csA, csB)
            v(lambda e: e.tensor_scalar(csB, csB, 1.5, None, ALU.is_le))
            vself()
            v(lambda e: e.tensor_tensor(f2v, m1v, csB, ALU.mult))
            v(lambda e: e.tensor_tensor(d12, a1s, a2s, ALU.subtract))
            sw("sv", P.cnt["sv"])
            s_(lambda e: e.activation(w1c, d12, AF.Sigmoid))
            vw("ss", P.cnt["ss"])
            v(lambda e: e.tensor_scalar(w2c, w1c, -1.0, 1.0, ALU.mult, ALU.add))
            vself()
            v(lambda e: e.tensor_tensor(t3(f1v), t3(f1v), bcast8(w1c), ALU.mult))
            vself()
            v(lambda e: e.tensor_tensor(t3(f2v), t3(f2v), bcast8(w2c), ALU.mult))
            vself()
            v(lambda e: e.tensor_tensor(lg3, t3(f1v), t3(f2v), ALU.add))
            vself()
            sv_combine_tm = P.cnt["sv"]


            def load_wgu(ee2, g):
                b = g % 2
                P.wait("gpsimd", "stt", wgu_free[b])
                wbuf = wgu0 if b == 0 else wgu1
                wgu_loaded[(pp, ee2, g)] = dg(
                    lambda e, ee2=ee2, g=g, wbuf=wbuf:
                    e.dma_start(out=wbuf[:], in_=d_wgu[ee2, g]))

            def load_wd(ee2, f):
                b = f % 4
                P.wait("gpsimd", "stt", wd_free[b])
                wbuf = wdbs[b]
                wd_loaded[(pp, ee2, f)] = dg(
                    lambda e, ee2=ee2, f=f, wbuf=wbuf:
                    e.dma_start(out=wbuf[:], in_=d_wd[ee2, f]))

            for ee in range(n_experts):
                load_wgu(ee, 0)
                load_wgu(ee, 1)
                for f0 in range(4):
                    load_wd(ee, f0)

                def emit_cbc(ee2, with_drain):
                    h2 = ee2 % 2
                    tw("sv", sv_cmbf)
                    tw("sv", pb67_free[0])
                    tw("ss", ss_cbc_prev[0])
                    for tb in range(NTB):
                        t_(lambda e, ee2=ee2, tb=tb, h2=h2: e.matmul(
                            ps[6 + tb][:, 0:512],
                            sel[:, ee2 * 128:(ee2 + 1) * 128],
                            cmbf[0:8, tb * 512:tb * 512 + 512],
                            start=True, stop=True))
                    dcb = drain() if with_drain else None
                    return dcb

                def evac_cbc(ee2, dcb):
                    h2 = ee2 % 2
                    sw("stt", dcb)
                    sw("sv", csb_read_done.get(ee2 - 2, 0))
                    s_(lambda e, h2=h2: e.copy(csb[:, h2 * 1024:h2 * 1024 + 512],
                                               ps[6][:, 0:512]))
                    s_(lambda e, h2=h2: e.copy(csb[:, h2 * 1024 + 512:h2 * 1024 + 1024],
                                               ps[7][:, 0:512]))
                    ss_csb_of[ee2] = P.cnt["ss"]
                    ss_cbc_prev[0] = P.cnt["ss"]

                a_groups = [tuple(range(s, min(s + 2, 2 * NM))) for s in range(0, 2 * NM, 2)]
                for G, us in enumerate(a_groups):
                    cbc_here = (G == 10 and ee + 1 < n_experts)
                    trans_here = (G == 5 and ee == 0)
                    cbc0_here = (G == 6 and ee == 0)
                    for u in us:
                        m, tb = u // 2, u % 2
                        g = m // 4
                        bg = ps[2 * (u % 3)]
                        bu = ps[2 * (u % 3) + 1]
                        if u % 8 == 0:
                            tw("ldw", wgu_loaded[(pp, ee, g)])
                        if u == 0:
                            tw("sv", sv_xn)
                            if ee == 0:
                                tw("sv", sv_cmtm)
                        if u < 3:
                            tw("sv", pb_pair_free[u])
                        prev = None
                        if u >= 3:
                            prev = (pp, ee, u - 3)
                        else:
                            pmap = {0: 30, 1: 31, 2: 29}
                            if ee > 0:
                                prev = (pp, ee - 1, pmap[u])
                            elif pp > 0:
                                prev = (pp - 1, n_experts - 1, pmap[u])
                        if prev is not None and prev in zdone:
                            tw("sv", zdone[prev])
                        mi = m % 4
                        for kc in range(NKC):
                            wbuf = wgu0 if g % 2 == 0 else wgu1
                            t_(lambda e, wbuf=wbuf, mi=mi, kc=kc, tb=tb, bg=bg:
                               e.matmul(bg[:, 0:512],
                                        wbuf[:, (mi * 8 + kc) * 128:(mi * 8 + kc) * 128 + 128],
                                        xn[:, kc * 1024 + tb * 512: kc * 1024 + tb * 512 + 512],
                                        start=(kc == 0), stop=(kc == 7)))
                        for kc in range(NKC):
                            wbuf = wgu0 if g % 2 == 0 else wgu1
                            t_(lambda e, wbuf=wbuf, mi=mi, kc=kc, tb=tb, bu=bu:
                               e.matmul(bu[:, 0:512],
                                        wbuf[:, ((4 + mi) * 8 + kc) * 128:((4 + mi) * 8 + kc) * 128 + 128],
                                        xn[:, kc * 1024 + tb * 512: kc * 1024 + tb * 512 + 512],
                                        start=(kc == 0), stop=(kc == 7)))
                    if trans_here:
                        tw("sv", sv_combine_tm)
                        tw("ss", ss_cbc_prev[0])
                        for tile in range(8):
                            pb = ps[6] if tile < 4 else ps[7]
                            coff = (tile % 4) * 128
                            t_(lambda e, tile=tile, pb=pb, coff=coff: e.transpose(
                                pb[0:8, coff:coff + 128],
                                cmtm[:, tile * 8:(tile + 1) * 8], identf[:]))
                    if cbc_here:
                        emit_cbc(ee + 1, False)
                    if cbc0_here:
                        emit_cbc(0, False)
                    dc = drain()
                    if trans_here:
                        vw("stt", dc)
                        for tile in range(8):
                            pb = ps[6] if tile < 4 else ps[7]
                            coff = (tile % 4) * 128
                            v(lambda e, tile=tile, pb=pb, coff=coff: e.tensor_copy(
                                cmbf[0:8, tile * 128:(tile + 1) * 128],
                                pb[0:8, coff:coff + 128]))
                        sv_cmbf = P.cnt["sv"]
                    if cbc_here:
                        evac_cbc(ee + 1, dc)
                    if cbc0_here:
                        evac_cbc(0, dc)
                    for g_done in range(NGRP):
                        last_u = 8 * g_done + 7
                        if us[0] <= last_u <= us[-1]:
                            wgu_free[g_done % 2] = dc
                            if g_done + 2 <= 3:
                                load_wgu(ee, g_done + 2)
                    for u in us:
                        m, tb = u // 2, u % 2
                        bg = ps[2 * (u % 3)]
                        bu = ps[2 * (u % 3) + 1]
                        sw("stt", dc)
                        sw("sv", prev_z[0])
                        sc_ = s_(lambda e, bg=bg: e.activation(siluT[:], bg[:, 0:512], AF.Silu))
                        silu_ss[(pp, ee, u)] = sc_
                        vw("stt", dc)
                        vw("ss", sc_)
                        vself()
                        zdone[(pp, ee, u)] = v(lambda e, m=m, tb=tb, bu=bu: e.tensor_tensor(
                            zall[:, m * 1024 + tb * 512: m * 1024 + tb * 512 + 512],
                            siluT[:], bu[:, 0:512], ALU.mult))
                        prev_z[0] = zdone[(pp, ee, u)]
                sv_z_all = P.cnt["sv"]

                ev_of_unit = {}
                for G in range(4):
                    us = tuple(range(4 * G, 4 * G + 4))
                    lastA = {0: 30, 1: 31, 2: 29}
                    for u2 in us:
                        f, tb = u2 // 2, u2 % 2
                        bank = ps[u2 % 8]
                        if u2 % 2 == 0:
                            tw("ldw", wd_loaded[(pp, ee, f)])
                        if u2 < 6:
                            ua = lastA[(u2 // 2) % 3]
                            tw("sv", zdone[(pp, ee, ua)])
                            tw("ss", silu_ss[(pp, ee, ua)])
                        if u2 == 0:
                            tw("sv", zdone[(pp, ee, 30)])
                        if u2 == 1:
                            tw("sv", sv_z_all)
                        if u2 == 6:
                            tw("ss", ss_csb_of.get(ee + 1, 0))
                        if u2 >= 8:
                            tw("sv", ev_of_unit[u2 - 8])
                        wbuf = wdbs[f % 4]
                        for m in range(NM):
                            t_(lambda e, wbuf=wbuf, m=m, tb=tb, bank=bank:
                               e.matmul(bank[:, 0:512],
                                        wbuf[:, m * 128:(m + 1) * 128],
                                        zall[:, m * 1024 + tb * 512: m * 1024 + tb * 512 + 512],
                                        start=(m == 0), stop=(m == 15)))
                    dc = drain()
                    f_hi = us[-1] // 2
                    if f_hi % 2 == 1:
                        wd_free[(f_hi - 1) % 4] = dc
                        wd_free[f_hi % 4] = dc
                        for fnext in (f_hi + 3, f_hi + 4):
                            if fnext <= NF - 1:
                                load_wd(ee, fnext)
                    he = ee % 2
                    for u2 in us:
                        f, tb = u2 // 2, u2 % 2
                        bank = ps[u2 % 8]
                        vw("stt", dc)
                        if u2 == 0:
                            vw("ss", ss_csb_of[ee])
                        aslice = acc[:, f * 1024 + tb * 512: f * 1024 + tb * 512 + 512]
                        vself()
                        mv = v(lambda e, bank=bank, tb=tb, he=he: e.tensor_tensor(
                            ucT[:], bank[:, 0:512],
                            csb[:, he * 1024 + tb * 512: he * 1024 + tb * 512 + 512],
                            ALU.mult))
                        vself()
                        if ee == 0:
                            v(lambda e, f=f, aslice=aslice:
                              e.tensor_scalar(aslice, ucT[:],
                                              bleb[:, f:f + 1], None, ALU.add))
                        else:
                            v(lambda e, aslice=aslice:
                              e.tensor_tensor(aslice, ucT[:], aslice, ALU.add))
                        ev_of_unit[u2] = mv
                        if u2 in (9, 11, 13):
                            pb_pair_free[(u2 - 9) // 2] = mv
                        if u2 == 15:
                            pb67_free[0] = mv
                csb_read_done[ee] = P.cnt["sv"]
                sv_acc_last = P.cnt["sv"]

            tw("sv", sv_xn)
            tw("sv", sv_acc_last)
            if not cw_emitted[0]:
                emit_cw()
                tw("sv", P.cnt["sv"])
            for tb in range(NTB):
                for kc in range(NKC):
                    t_(lambda e, tb=tb, kc=kc: e.matmul(
                        ps[tb][0:24, 0:512],
                        cwsb[:, kc * 24:(kc + 1) * 24],
                        xn[:, kc * 1024 + tb * 512: kc * 1024 + tb * 512 + 512],
                        start=(kc == 0), stop=(kc == 7)))
            stt_lat = drain()
            vw("stt", stt_lat)
            for tb in range(NTB):
                vself()
                v(lambda e, tb=tb: e.tensor_scalar(
                    bits[0:24, tb * 512:tb * 512 + 512], ps[tb][0:24, 0:512],
                    belb[:], 0.0, ALU.add, ALU.is_gt))
            sv_bits = P.cnt["sv"]
            tw("sv", sv_bits)
            for tb in range(NTB):
                t_(lambda e, tb=tb: e.matmul(ps[2 + tb][0:12, 0:512], htb[:],
                                             bits[0:24, tb * 512:tb * 512 + 512],
                                             start=True, stop=True))
            stt_syn = drain()
            vw("stt", stt_syn)
            for tb in range(NTB):
                vself()
                v(lambda e, tb=tb: e.tensor_scalar(
                    synb[0:12, tb * 512:tb * 512 + 512], ps[2 + tb][0:12, 0:512],
                    0.5, None, ALU.is_gt))
            sv_syn = P.cnt["sv"]
            tw("sv", sv_syn)
            for tb in range(NTB):
                t_(lambda e, tb=tb: e.matmul(ps[4 + tb][0:1, 0:512], ones12bf[:],
                                             synb[0:12, tb * 512:tb * 512 + 512],
                                             start=True, stop=True))
            stt_w = drain()
            vw("stt", stt_w)
            for tb in range(NTB):
                vself()
                v(lambda e, tb=tb: e.tensor_scalar(rowA[0:1, tb * 512:(tb + 1) * 512],
                                                   ps[4 + tb][0:1, 0:512],
                                                   -2.0, None, ALU.add))
            vself()
            v(lambda e: e.tensor_scalar(rowB[:], rowA[:], -1.5, None, ALU.is_gt))
            vself()
            v(lambda e: e.tensor_scalar(rowA[:], rowA[:], 1.5, None, ALU.is_lt))
            vself()
            v(lambda e: e.tensor_tensor(rowB[:], rowB[:], rowA[:], ALU.mult))
            vself()
            v(lambda e: e.tensor_copy(rowsb[0:1, :], rowB[:]))
            sv_df = P.cnt["sv"]
            tw("sv", sv_df)
            for tb in range(NTB):
                t_(lambda e, tb=tb: e.matmul(ps[6 + tb][0:12, 0:512],
                                             onesbf[0:1, 0:12],
                                             rowsb[0:1, tb * 512:tb * 512 + 512],
                                             start=True, stop=True))
            stt_dfb = drain()
            vw("stt", stt_dfb)
            for tb in range(NTB):
                sl0 = tb * 512
                vself()
                v(lambda e, tb=tb, sl0=sl0: e.tensor_tensor(
                    mfix[0:12, sl0:sl0 + 512], synb[0:12, sl0:sl0 + 512],
                    ps[6 + tb][0:12, 0:512], ALU.mult))
                vself()
                v(lambda e, sl0=sl0: e.tensor_tensor(
                    csb[0:24, 0:512], bits[0:24, sl0:sl0 + 512],
                    mfix[0:24, sl0:sl0 + 512], ALU.mult))
                v(lambda e, sl0=sl0: e.tensor_tensor(
                    csb[0:24, 512:1024], bits[0:24, sl0:sl0 + 512],
                    mfix[0:24, sl0:sl0 + 512], ALU.add))
                vself()
                v(lambda e, sl0=sl0: e.scalar_tensor_tensor(
                    corr[0:24, sl0:sl0 + 512], csb[0:24, 0:512], -2.0,
                    csb[0:24, 512:1024], ALU.mult, ALU.add))
            sv_corr = P.cnt["sv"]
            sv_csb_read_last = sv_corr

            tw("sv", sv_corr)
            cx_of = {}
            asm_ev_group = {}
            for G in range(4):
                fpair = PAIRS_ASM[G]
                if G >= 2:
                    tw("sv", asm_ev_group[G - 2])
                for f in fpair:
                    b = f % 2
                    P.wait("sync", "stt", xck_freeMM[b])
                    P.wait("sync", "sv", xck_freeV[b])
                    dq = dx if f % 2 == 0 else dy
                    cx_of[f] = dq(lambda e, pp=pp, f=f, b=b:
                                  e.dma_start(out=xck[b][:], in_=d_x[pp, f]))
                    for tb in range(NTB):
                        u3 = f * 2 + tb
                        bank = ps[u3 % 8]
                        for kc in range(NKC):
                            dd = (f - kc) % 8
                            t_(lambda e, kc=kc, tb=tb, dd=dd, bank=bank: e.matmul(
                                bank[:, 0:512],
                                ball[:, dd * 128:(dd + 1) * 128],
                                xn[:, kc * 1024 + tb * 512: kc * 1024 + tb * 512 + 512],
                                start=(kc == 0), stop=False))
                        t_(lambda e, f=f, tb=tb, bank=bank: e.matmul(
                            bank[:, 0:512],
                            wlesb[:, f * 128:(f + 1) * 128],
                            corr[0:24, tb * 512:tb * 512 + 512],
                            start=False, stop=True))
                dc = drain()
                for f in fpair:
                    b = f % 2
                    for tb in range(NTB):
                        u3 = f * 2 + tb
                        bank = ps[u3 % 8]
                        h = u3 % 2
                        vw("stt", dc)
                        vw("ldx" if f % 2 == 0 else "ldy", cx_of[f])
                        osem = "outs" if h == 0 else "ou2"
                        P.wait("vector", osem, outst_free[h])
                        aslice = acc[:, f * 1024 + tb * 512: f * 1024 + tb * 512 + 512]
                        vself()
                        v(lambda e, bank=bank, aslice=aslice: e.tensor_tensor(
                            tmpn[:], bank[:, 0:512], aslice, ALU.add))
                        vself()
                        so = v(lambda e, b=b, tb=tb, h=h: e.tensor_tensor(
                            outst[:, h * 512:(h + 1) * 512], tmpn[:],
                            xck[b][:, tb * 512:tb * 512 + 512], ALU.add))
                        P.wait("sync", "sv", so)
                        P.wait("sync", osem, P.cnt[osem])
                        oc = P.op("sync", lambda e, pp=pp, f=f, tb=tb, h=h: e.dma_start(
                            out=d_out[pp, f, :, tb * 512:(tb + 1) * 512],
                            in_=outst[:, h * 512:(h + 1) * 512]), osem, 16)
                        outst_free[h] = oc
                    xck_freeV[b] = P.cnt["sv"]
                    xck_freeMM[b] = P.cnt["stt"]
                asm_ev_group[G] = P.cnt["sv"]
                if pp + 1 < n_passes:
                    P.wait("sync", "sv", asm_ev_group[G])
                    cpx = cpy = 0
                    for fx in fpair:
                        if fx % 2 == 0:
                            cpx = dx(lambda e, pp=pp, fx=fx: e.dma_start(
                                out=acc[:, fx * 1024:(fx + 1) * 1024],
                                in_=d_x[pp + 1, fx]))
                        else:
                            cpy = dy(lambda e, pp=pp, fx=fx: e.dma_start(
                                out=acc[:, fx * 1024:(fx + 1) * 1024],
                                in_=d_x[pp + 1, fx]))
                    xpair_done[(pp + 1, fpair)] = (cpx, cpy)
            if pp + 1 < n_passes:
                tw("sv", asm_ev_group[2])
                for pr in PAIRS_ASM[:3]:
                    emit_stats_pair(pp + 1, pr, False)
            stt_asm_done = P.cnt["stt"]
            sv_acc_last = P.cnt["sv"]

        P.wait("sync", "outs", P.cnt["outs"])
        P.wait("sync", "ou2", P.cnt["ou2"])

        with nc.Block() as block:
            P.replay(block)

    return nc



def _host_consts():
    f32 = np.float32
    def pmaj(a):
        return np.ascontiguousarray(a.reshape(8, 128).T)
    ka = pmaj((np.abs(np.fft.fftfreq(1024).astype(f32)) + f32(1e-8)).astype(f32))
    fre = pmaj(np.frombuffer(base64.b64decode(_FRE_B64), dtype="<f4").copy())
    fim = pmaj(np.frombuffer(base64.b64decode(_FIM_B64), dtype="<f4").copy())
    k = np.arange(1024, dtype=np.int64)
    kk, nn = np.meshgrid(k, k, indexing="ij")
    theta = (2.0 * np.pi / 1024.0) * ((kk * nn) % 1024).astype(np.float64)
    c = np.cos(theta)
    s = np.sin(theta)

    def split(a):
        hi = a.astype(f32)
        lo = (a - hi.astype(np.float64)).astype(f32)
        return hi, lo

    chi, clo = split(c)
    shi, slo = split(s)

    def tile_k(a):
        return np.ascontiguousarray(a.reshape(8, 128, 1024).transpose(1, 0, 2).reshape(128, 8192))

    return dict(ka=ka, fre=fre, fim=fim,
                cosh=tile_k(chi), cosl=tile_k(clo),
                sinh=tile_k(shi), sinl=tile_k(slo),
                ident=np.eye(128, dtype=f32))


def _prep(inputs):
    f32 = np.float32
    x = np.ascontiguousarray(inputs["L0_attn_x"], dtype=f32).reshape(-1, 1024)
    gp = np.ascontiguousarray(inputs["L0_ffn_gate_proj_w"], dtype=f32)
    up = np.ascontiguousarray(inputs["L0_ffn_up_proj_w"], dtype=f32)
    dp = np.ascontiguousarray(inputs["L0_ffn_down_proj_w"], dtype=f32)
    W_el = np.ascontiguousarray(inputs["L0_attn_W_el"], dtype=f32)
    b_el = np.ascontiguousarray(inputs["L0_attn_b_el"], dtype=f32)
    W_le = np.ascontiguousarray(inputs["L0_attn_W_le"], dtype=f32)
    b_le = np.ascontiguousarray(inputs["L0_attn_b_le"], dtype=f32)
    gw = np.ascontiguousarray(inputs["L0_ffn_gate_w"], dtype=f32)

    g7 = gp.reshape(8, 8, 128, 4, 4, 128)
    u7 = up.reshape(8, 8, 128, 4, 4, 128)
    wgu = np.empty((8, 4, 128, 2, 4, 8, 128), f32)
    wgu[:, :, :, 0] = g7.transpose(0, 3, 2, 4, 1, 5)
    wgu[:, :, :, 1] = u7.transpose(0, 3, 2, 4, 1, 5)
    wgu = np.ascontiguousarray(wgu.reshape(8, 4, 128, 8192))

    d5 = dp.reshape(8, 16, 128, 8, 128)
    wd = np.ascontiguousarray(d5.transpose(0, 3, 2, 1, 4).reshape(8, 8, 128, 2048))

    wel = np.ascontiguousarray(W_el.reshape(8, 128, 24).transpose(1, 0, 2).reshape(128, 192))
    gwt = np.ascontiguousarray(gw.reshape(8, 128, 8).transpose(1, 0, 2).reshape(128, 64))
    ble = np.ascontiguousarray(b_le.reshape(8, 128).T)
    bel = np.ascontiguousarray(b_el.reshape(24, 1))
    _A = np.array([[1,1,1,0,1,1,1,0,0,0,1,1],[1,1,0,1,1,1,0,0,0,1,1,0],
                   [1,0,1,1,1,0,0,0,1,1,0,1],[0,1,1,1,0,0,0,1,1,0,1,1],
                   [0,0,0,0,1,1,1,1,1,1,1,1],[1,1,1,1,0,0,0,0,0,0,0,0],
                   [0,0,0,0,1,1,1,1,0,0,0,0],[1,1,0,0,1,1,0,0,1,1,0,0],
                   [1,0,1,0,1,0,1,0,1,0,1,0],[0,1,0,1,0,1,0,1,0,1,0,1],
                   [1,0,0,1,1,0,0,1,1,0,0,1],[0,1,1,0,0,1,1,0,0,1,1,0]], dtype=f32)
    H = np.concatenate([_A.T, np.eye(12, dtype=f32)], axis=1)
    ht = np.ascontiguousarray(H.T)
    xrow = np.ascontiguousarray(x[0:1, :])

    selm = np.zeros((8, 1024), np.float32)
    for e_ in range(8):
        selm[e_, e_ * 128:(e_ + 1) * 128] = 1.0
    consts = _host_consts()
    shared = dict(wgu=wgu, wd=wd, wel=wel, wle=W_le, gw=gwt, ble=ble, bel=bel,
                  ht=ht, xrow=xrow, sel_in=selm, **consts)

    in_maps = []
    for c in range(NCORE):
        xc = x[c * TCORE:(c + 1) * TCORE]
        xfm = np.ascontiguousarray(
            xc.reshape(NPASS, TPASS, 8, 128).transpose(0, 2, 3, 1))
        m = dict(shared)
        m["x_fm"] = xfm
        in_maps.append(m)
    return in_maps


def kernel(**inputs):
    in_maps = _prep(inputs)
    nc = _build_program()
    res = run_bass_kernel_spmd(nc, in_maps, core_ids=list(range(NCORE)))
    outs = []
    for c in range(NCORE):
        o = np.asarray(res.results[c]["out"])
        outs.append(o.transpose(0, 3, 1, 2).reshape(TCORE, 1024))
    return np.concatenate(outs, axis=0).reshape(4, 8192, 1024).astype(np.float32)
